# version 15
# baseline (speedup 1.0000x reference)
import numpy as np
import ml_dtypes
import contextlib
import concourse.bass as bass
import concourse.mybir as mybir
from concourse.bass_utils import run_bass_kernel_spmd
from concourse.alu_op_type import AluOpType as A

F32 = mybir.dt.float32
BF16 = mybir.dt.bfloat16
AF = mybir.ActivationFunctionType
AX = mybir.AxisListType

NCORES = 8
B, S, D = 2, 8192, 1024
E, FF = 8, 2048
T = 2048
HALO = 640
EXT = HALO + T
NT = EXT // 128
KV = 2560
NEG = -1.0e30
PCH = [512, 512, 512, 512, 512, 2]
CCH = [512, 512, 512, 512, 511]

DEBUG_TAPS = ()
CAP = 768
SLOTS = 8 * CAP
BIGIDX = 1.0e9

ENGS = ("sync", "gpsimd", "vector", "scalar", "tensor")


class Plan:
    def __init__(self, nc, ctx, semnames):
        self.nc = nc
        self.q = {e: [] for e in ENGS}
        self.cnt = {}
        self.sems = {}
        for s in semnames:
            self.sems[s] = ctx.enter_context(nc.semaphore(s))
            self.cnt[s] = 0

    def op(self, eng, fn, wait=(), inc=None, k=1):
        w = [(self.sems[s], v) for s, v in wait if v > 0]
        if inc is not None:
            self.cnt[inc] += k
            ih = self.sems[inc]

            def run(h, fn=fn, w=w, ih=ih, k=k):
                for s, v in w:
                    h.wait_ge(s, v)
                fn(h).then_inc(ih, k)
        else:
            def run(h, fn=fn, w=w):
                for s, v in w:
                    h.wait_ge(s, v)
                fn(h)
        self.q[eng].append(run)
        return self.cnt[inc] if inc is not None else None


def build_program():
    nc = bass.Bass("TRN2", target_bir_lowering=False)
    ctx = contextlib.ExitStack()

    def din(name, shape, dt=F32):
        return nc.dram_tensor(name, shape, dt, kind="ExternalInput").ap()

    x_ext = din("x_ext", [EXT, D])
    wq_in = din("wq", [8, 128, 1024])
    wk_in = din("wk", [8, 128, 1024])
    wv_in = din("wv", [8, 128, 1024])
    wo_in = din("wo", [128, 8192])
    bq_in = din("bq", [128, 8])
    bk_in = din("bk", [128, 8])
    bv_in = din("bv", [128, 8])
    bo_in = din("bo_rep", [128, 1024])
    cw_in = din("cw", [128, 96])
    cdiag_in = din("cdiag", [8, 128, 1536])
    cb_in = din("cb", [128, 24])
    cost_in = din("cost", [128, KV])
    sint_in = din("sint", [128, KV])
    kmask_in = din("kmask", [128, 20])
    rwt_in = din("router_wt", [128, 64])
    rb_in = din("router_b", [8, 1])
    temp_in = din("temp", [1, 1])
    wg_s = din("wg_s", [8, 16, 128, 1024])
    wp_s = din("wp_s", [8, 16, 128, 1024])
    wo_s = din("wo_s", [8, 2, 128, 8192])
    ident_in = din("identity", [128, 128], BF16)
    identf_in = din("identityf", [128, 128], F32)
    rot_in = din("rot_t", [128, 128], BF16)
    tri0_in = din("tri0", [128, 128], BF16)
    tri4_in = din("tri4", [128, 128], BF16)
    sel2_in = din("sel2", [128, 2], BF16)
    selT2_in = din("selT2", [2, 128], BF16)
    cselA_in = din("colselA", [128, 128], BF16)
    cselB_in = din("colselB", [128, 128], BF16)
    ones8f_in = din("ones8f", [8, 1])
    ones18f_in = din("ones1_8f", [1, 8])
    sel64_in = din("sel64", [64, 32])
    ltri_in = din("ltri", [128, 128], BF16)
    ones128_in = din("ones128", [128, 1], BF16)
    ones1_128f_in = din("ones1_128f", [1, 128])
    ecap_in = din("ecap", [128, 8])
    m32_in = din("m32", [32, 8])
    g32_in = din("g32", [32, 1])
    actb_in = din("actbias", [128, 4])

    out = nc.dram_tensor("out", [T, D], F32, kind="ExternalOutput").ap()
    dbg = {}
    for tap, shape, dt in (
        ("xn", [128, 8 * EXT], BF16), ("qt", [128, KV], BF16), ("kt", [128, KV], BF16),
        ("vt", [128, KV], BF16), ("attnout", [128, 16384], BF16),
        ("accwo", [128, 16384], F32), ("gates", [128, 2048], F32),
        ("wgt", [128, 128], F32), ("scores", [8, 2048], F32), ("la", [8, 8192], F32),
        ("accffn", [128, 16384], F32), ("h0", [128, 12288], BF16),
        ("sl1", [128, 16], F32), ("sl2", [128, 16], F32),
        ("gs0", [128, 2048], BF16),
        ("osl", [SLOTS, 1024], BF16), ("gat", [SLOTS, 1024], BF16),
        ("tgs", [128, 32], F32), ("xg0", [128, 6144], BF16),
        ("attnout2", [128, 16384], BF16), ("woT", [128, 8192], BF16),
        ("lnstats", [128, 48], F32),
    ):
        if tap in DEBUG_TAPS:
            dbg[tap] = nc.dram_tensor("dbg_" + tap, shape, dt, kind="ExternalOutput").ap()

    xn_dram = nc.dram_tensor("xn_dram", [T, D], BF16)
    gath = nc.dram_tensor("gath", [SLOTS, D], BF16)
    oslots = nc.dram_tensor("oslots", [SLOTS, D], BF16)
    cc_in = nc.dram_tensor("cc_in", [8, T], F32)
    cc_out = nc.dram_tensor("cc_out", [64, T], F32)
    sk_scr = nc.dram_tensor("sk_scr", [8, 8192], F32)

    MB = 27648
    TOP = 229300

    def sb(name, shape, dt, off):
        assert off % 32 == 0, name
        nb = shape[1] * (2 if dt == BF16 else 4)
        assert MB + off + nb <= TOP, (name, off, nb)
        return nc.alloc_sbuf_tensor_at(name, shape, dt, offset=MB + off).ap()

    acc = sb("acc", [128, 16384], F32, 0)
    cost = sb("cost", [128, KV], F32, 0)
    sint = sb("sint", [128, KV], F32, 10240)
    xld = [sb(f"xld{i}", [128, 1024], F32, 20480 + 4096 * i) for i in range(2)]
    dev = sb("dev", [128, 1024], F32, 28672)
    sqd = sb("sqd", [128, 1024], F32, 32768)
    sc_r = sb("sc_r", [8, T], F32, 28672)
    qh = {pn: [sb(f"qh_{pn}{j}", [128, 516], BF16, 36864 + (2 * i + j) * 2080)
               for j in range(2)] for i, pn in enumerate(("q", "k", "v"))}
    convq = sb("convq", [128, 512], F32, 49344)
    convk = sb("convk", [128, 512], F32, 51392)
    rsin_q = sb("rsin_q", [128, 512], BF16, 53440)
    rsin_k = sb("rsin_k", [128, 512], BF16, 54464)
    rcos_q = sb("rcos_q", [128, 512], F32, 55488)
    rcos_k = sb("rcos_k", [128, 512], F32, 57536)
    qrot = sb("qrot", [128, 512], F32, 59584)
    krot = sb("krot", [128, 512], F32, 61632)
    sqq = sb("sqq", [128, 512], BF16, 63680)
    gn_own = sb("gn_own", [128, T], F32, 36864)
    xnt_bf = sb("xnt_bf", [128, 1024], BF16, 45056)
    ex32 = sb("ex32", [32, T], F32, 45056)
    raw64 = sb("raw64", [64, T], F32, 53248)
    B2 = 65536
    wpr = {pn: sb(f"wpr_{pn}", [128, 1024], BF16, B2 + i * 2048)
           for i, pn in enumerate(("q", "k", "v"))}
    qt_b = sb("qt_b", [128, KV], BF16, B2 + 6144)
    kt_b = sb("kt_b", [128, KV], BF16, B2 + 11264)
    vt_b = sb("vt_b", [128, KV], BF16, B2 + 16384)
    vtok = sb("vtok", [128, 20 * 130], BF16, B2 + 21504)
    pt = [sb(f"pt{i}", [128, 640], BF16, B2 + 26752 + 1280 * i) for i in range(2)]
    rr2 = sb("rr2", [2, 512], F32, B2 + 29312)
    rr2b = sb("rr2b", [2, 512], BF16, B2 + 31360)
    la = sb("la", [8, 8192], F32, B2 + 0)
    exs = sb("exs", [8, 1024], F32, B2 + 32768)
    lsee = sb("lsee", [1, 1024], F32, 61440)
    dn_own = sb("dn_own", [1, T], F32, B2 + 8192)
    wob = sb("wob", [128, 16384], BF16, B2 + 0)
    gbuf = [sb(f"gbuf{i}", [128, 512], BF16, B2 + 32768 + 1024 * i) for i in range(2)]
    pbuf = [sb(f"pbuf{i}", [128, 512], BF16, B2 + 34816 + 1024 * i) for i in range(2)]
    xsc = [sb(f"xsc{i}", [128, 1024], BF16, B2 + 0 + 2048 * i) for i in range(2)]
    osb = [sb(f"osb{i}", [128, 1024], BF16, 135168 + 36864 + 2048 * i)
           for i in range(2)]
    B3 = 102400
    attnout = sb("attnout", [128, 16384], BF16, B3)
    hbuf = sb("hbuf", [128, 16 * CAP], BF16, B3)
    xn = sb("xn", [128, 8 * EXT], BF16, 135168)
    XNR = 135168
    xg = sb("xg", [128, 6 * 1024], BF16, XNR)
    xg_fm = [sb(f"xg_fm{i}", [128, 8 * CAP], BF16, XNR + 12288 + 12288 * i)
             for i in range(2)]
    wgb = [sb(f"wgb{i}", [128, 1024], BF16, B3 + 24576 + 2048 * i) for i in range(2)]
    wpb = [sb(f"wpb{i}", [128, 1024], BF16, B3 + 28672 + 2048 * i) for i in range(2)]
    W4 = 178208
    woT = sb("woT", [128, 8192], BF16, W4)
    cdg = [sb(f"cdg{i}", [128, 1536], BF16, W4 + 3072 * i) for i in range(2)]
    xf = [sb(f"xf{i}", [128, 1024], F32, W4 + 4096 * i) for i in range(2)]
    gsl = [sb(f"gsl{i}", [128, 1024], BF16, B2 + 0 + 2048 * i) for i in range(2)]
    t1f = sb("t1f", [128, 1024], F32, W4 + 8192)
    bo_s = sb("bo_s", [128, 1024], F32, W4 + 12288)
    xnt = [sb(f"xnt{i}", [128, 1024], F32, 36864 + 4096 * i) for i in range(2)]
    off = [194624]

    def psm(name, shape, dt):
        t = sb(name, shape, dt, off[0])
        nb = shape[1] * (2 if dt == BF16 else 4)
        off[0] += (nb + 31) // 32 * 32
        return t

    m_all = psm("m_all", [128, 24], F32)
    rs_all = psm("rs_all", [128, 24], F32)
    ms1 = psm("ms1", [128, 1], F32)
    ms2 = psm("ms2", [128, 1], F32)
    ms3 = psm("ms3", [128, 1], F32)
    ident = psm("ident", [128, 128], BF16)
    identf = psm("identf", [128, 128], F32)
    rot_t = psm("rot_t", [128, 128], BF16)
    tri0 = psm("tri0", [128, 128], BF16)
    tri4 = psm("tri4", [128, 128], BF16)
    sel2 = psm("sel2", [128, 2], BF16)
    selT2 = psm("selT2", [2, 128], BF16)
    cselA = psm("cselA", [128, 128], BF16)
    cselB = psm("cselB", [128, 128], BF16)
    ones8f = psm("ones8f", [8, 1], F32)
    ones18f = psm("ones18f", [1, 8], F32)
    sel64 = psm("sel64", [64, 32], F32)
    m32 = psm("m32", [32, 8], F32)
    g32 = psm("g32", [32, 1], F32)
    cb = psm("cb", [128, 24], F32)
    bqkv = {"q": psm("bq_s", [128, 8], F32), "k": psm("bk_s", [128, 8], F32),
            "v": psm("bv_s", [128, 8], F32)}
    kmask = psm("kmask", [128, 20], F32)
    rwt = psm("rwt", [128, 64], BF16)
    rwtf = psm("rwtf", [128, 64], F32)
    rb8 = psm("rb8", [8, 1], F32)
    tempv = psm("tempv", [1, 1], F32)
    rtemp = psm("rtemp", [1, 1], F32)
    rtemp8 = psm("rtemp8", [8, 1], F32)
    den = psm("den", [128, 128], F32)
    denr = psm("denr", [128, 128], BF16)
    wgt = psm("wgt", [128, 128], F32)
    wtile = psm("wtile", [128, 8], F32)
    wm1 = psm("wm1", [128, 1], F32)
    wm2 = psm("wm2", [128, 1], F32)
    wmsk1 = psm("wmsk1", [128, 8], F32)
    wmsk2 = psm("wmsk2", [128, 8], F32)
    wg2v = psm("wg2v", [128, 8], F32)
    wden = psm("wden", [128, 1], F32)
    flq = psm("flq", [1, 32], BF16)
    actb = psm("actb", [128, 4], F32)
    ltri = psm("ltri", [128, 128], BF16)
    ones128 = psm("ones128", [128, 1], BF16)
    ones1_128f = psm("ones1_128f", [1, 128], F32)
    ecap = psm("ecap", [128, 8], F32)
    fbf = psm("fbf", [128, 8], BF16)
    baserep = psm("baserep", [128, 8], F32)
    cntsb = psm("cntsb", [1, 8], F32)
    sidx = psm("sidx", [128, 8], F32)
    sinr = psm("sinr", [128, 8], F32)
    svld = psm("svld", [128, 8], F32)
    sbig = psm("sbig", [128, 8], F32)
    stmp = psm("stmp", [128, 8], F32)
    msk21 = psm("msk21", [128, 8], F32)
    s1f = psm("s1f", [128, 1], F32)
    s2f = psm("s2f", [128, 1], F32)
    s1i = psm("s1i", [128, 16], mybir.dt.uint32)
    s2i = psm("s2i", [128, 16], mybir.dt.uint32)
    tg1 = psm("tg1", [128, 16], F32)
    tg2 = psm("tg2", [128, 16], F32)
    fl1 = psm("fl1", [128, 2], F32)
    ts8 = psm("ts8", [8, 1], F32)
    ts8b = psm("ts8b", [8, 1], F32)
    lsef = psm("lsef", [8, 1], F32)
    assert MB + off[0] <= TOP, off[0]

    PS_SA = [ctx.enter_context(nc.psum_tensor(f"ps_sa{i}", [128, 1024], F32))
             for i in range(2)]
    PS_AV = [ctx.enter_context(nc.psum_tensor(f"ps_av{i}", [128, 512], F32))
             for i in range(2)]
    PS_PQ = ctx.enter_context(nc.psum_tensor("ps_pq", [128, 512], F32))
    PS_VT = ctx.enter_context(nc.psum_tensor("ps_vt", [128, 128], BF16))

    P = Plan(nc, ctx, ("sq", "sg", "scc", "pe", "act", "dve", "pool"))
    C = P.cnt

    def dma(qeng, dst, src, wait=()):
        sem = "sg" if qeng == "gpsimd" else "sq"
        return P.op(qeng, lambda h, d=dst, s=src: h.dma_start(out=d, in_=s),
                    wait=wait, inc=sem, k=16)

    def dmat(qeng, dst, src, wait=()):
        return P.op(qeng, lambda h, d=dst, s=src: h.dma_start_transpose(d, s),
                    wait=wait, inc="sq", k=16)

    def mm(dst, lhsT, rhs, start, stop, wait=(), inc=None):
        return P.op("tensor",
                    lambda h, d=dst, l=lhsT, r=rhs, a=start, b=stop:
                    h.matmul(d, l, r, start=a, stop=b), wait=wait, inc=inc)

    def tpose(dst, src, idm, wait=(), inc=None):
        return P.op("tensor", lambda h, d=dst, s=src, i=idm: h.transpose(d, s, i),
                    wait=wait, inc=inc)

    def act(out_, in_, func, wait=(), inc=None, **kw):
        return P.op("scalar",
                    lambda h, o=out_, i=in_, f=func, kw=dict(kw):
                    h.activation(o, i, f, **kw), wait=wait, inc=inc)

    def vop(name, *args, wait=(), inc=None, eng="vector"):
        return P.op(eng, lambda h, n=name, a=args: getattr(h, n)(*a),
                    wait=wait, inc=inc)

    def allcnt():
        return [(s, C[s]) for s in ("pe", "act", "dve", "pool")]

    for d_, s_ in ((cost, cost_in), (sint, sint_in), (ident, ident_in),
                   (identf, identf_in), (rot_t, rot_in), (tri0, tri0_in),
                   (tri4, tri4_in), (sel2, sel2_in), (selT2, selT2_in),
                   (cselA, cselA_in), (cselB, cselB_in), (ones8f, ones8f_in),
                   (ones18f, ones18f_in), (sel64, sel64_in), (m32, m32_in),
                   (g32, g32_in), (cb, cb_in), (bqkv["q"], bq_in),
                   (bqkv["k"], bk_in), (bqkv["v"], bv_in), (kmask, kmask_in),
                   (rwtf, rwt_in), (rb8, rb_in), (tempv, temp_in),
                   (actb, actb_in), (ltri, ltri_in), (ones128, ones128_in),
                   (ones1_128f, ones1_128f_in), (ecap, ecap_in)):
        dma("sync", d_, s_)
    const_ld = C["sq"]
    dma("gpsimd", wpr["q"], wq_in[0])
    dma("gpsimd", wpr["k"], wk_in[0])
    dma("gpsimd", wpr["v"], wv_in[0])
    dma("gpsimd", cdg[0], cdiag_in[0])
    pw_ld = {0: C["sg"]}
    vop("memset", qt_b[:, KV - 1:KV], 0.0)
    vop("memset", kt_b[:, KV - 1:KV], 0.0)
    pad_dve = vop("memset", vt_b[:, KV - 1:KV], 0.0, inc="dve")

    x_til = x_ext.rearrange("(a p) d -> a p d", p=128)
    dev_done = {}
    tr_done = {-2: 0, -1: 0}
    rs_prev = 0
    vt_slot = {0: 0, 1: 0}
    xnd_wr = [0]
    for a in range(NT):
        buf = xld[a % 2]
        ld = dma("sync", buf, x_til[a], wait=[("dve", dev_done.get(a - 2, 0))])
        r1_ = vop("reduce_sum", ms1, buf, AX.X, wait=[("sq", ld)], inc="dve")
        m1_ = vop("tensor_scalar", m_all[:, a:a + 1], ms1, 1.0 / D, None, A.mult,
                  wait=[("dve", r1_)], inc="dve")
        dd = vop("tensor_scalar", dev, buf, m_all[:, a:a + 1], None, A.subtract,
                 wait=[("act", rs_prev), ("dve", m1_)], inc="dve")
        dev_done[a] = dd
        act(sqd, dev, AF.Square, wait=[("dve", dd)], accum_out=ms2)
        act(ms3, ms2, AF.Ln, scale=1.0 / D, bias=actb[:, 0:1])
        act(rs_all[:, a:a + 1], ms3, AF.Exp, scale=-0.5)
        rs = act(fl1[:, 0:1], rs_all[:, a:a + 1], AF.Copy, inc="act")
        rs_prev = rs
        xb = vop("tensor_scalar", xnt[a % 2], dev, rs_all[:, a:a + 1], None, A.mult,
                 wait=[("act", rs), ("dve", tr_done[a - 2])], inc="dve")
        if a >= 5:
            xc = act(xnt_bf[:], xnt[a % 2], AF.Copy,
                     wait=[("sq", xnd_wr[0]), ("dve", xb)], inc="act")
            xnd_wr[0] = dma("sync", xn_dram.ap().rearrange(
                "(t p) d -> t p d", p=128)[a - 5], xnt_bf[:],
                wait=[("act", xc)])
        for dc in range(8):
            sl_ = (a * 8 + dc) % 2
            tp = tpose(PS_AV[sl_][:, 0:128],
                       xnt[a % 2][:, dc * 128:(dc + 1) * 128], identf,
                       wait=[("dve", xb), ("dve", vt_slot[sl_])], inc="pe")
            vt_slot[sl_] = vop(
                "tensor_copy", xn[:, dc * EXT + a * 128: dc * EXT + (a + 1) * 128],
                PS_AV[sl_][:, 0:128], wait=[("pe", tp)], inc="dve")
        tr_done[a] = C["dve"]
    ln_sq = C["sq"]
    ln_dve = C["dve"]

    if "xn" in DEBUG_TAPS:
        dma("sync", dbg["xn"], xn[:], wait=[("dve", ln_dve)])

    tm_ = vop("tensor_scalar", rtemp, tempv, 0.1, None, A.max,
              wait=[("sq", const_ld)], inc="dve")
    vop("reciprocal", rtemp, rtemp, wait=[("dve", tm_)])
    rt = vop("tensor_copy", fl1[0:1, 0:1], rtemp, inc="dve")
    rwc = vop("tensor_copy", rwt, rwtf, inc="dve")
    r8 = mm(PS_PQ[0:8, 0:1], ones18f, rtemp, True, True, wait=[("dve", rt)], inc="pe")
    rt8a = vop("tensor_copy", rtemp8, PS_PQ[0:8, 0:1], wait=[("pe", r8)], inc="dve")
    rt8 = vop("tensor_copy", fl1[0:8, 1:2], rtemp8, wait=[("dve", rt8a)], inc="dve")
    sc_done = 0
    for tc in range(4):
        for dc in range(8):
            w = []
            if dc == 0:
                w = [("dve", rt8 if tc == 0 else sc_done), ("dve", ln_dve),
                     ("dve", rwc)]
            mm(PS_PQ[0:8, :], rwt[:, dc * 8:(dc + 1) * 8],
               xn[:, dc * EXT + HALO + tc * 512: dc * EXT + HALO + (tc + 1) * 512],
               dc == 0, dc == 7, wait=w, inc="pe" if dc == 7 else None)
        sc_done = vop("tensor_scalar", sc_r[:, tc * 512:(tc + 1) * 512],
                      PS_PQ[0:8, :], rb8, rtemp8, A.add, A.mult,
                      wait=[("pe", C["pe"])], inc="dve")
    cc_wr = dma("sync", cc_in.ap(), sc_r[:], wait=[("dve", sc_done)])
    P.op("gpsimd", lambda h: h.collective_compute(
        "AllGather", A.bypass, replica_groups=[list(range(NCORES))],
        ins=[cc_in.ap().opt()], outs=[cc_out.ap().opt()]),
        wait=[("sq", cc_wr)], inc="scc")
    if "scores" in DEBUG_TAPS:
        dma("sync", dbg["scores"], sc_r[:], wait=[("sq", cc_wr)])

    projs = ("q", "k", "v")
    proj_ps = {"q": lambda n: PS_SA[0][:, 0:n], "k": lambda n: PS_SA[0][:, 512:512 + n],
               "v": lambda n: PS_SA[1][:, 0:n]}
    evac_done = {"q": 0, "k": 0, "v": 0}
    conv_done = {}
    convread = {"q": 0, "k": 0, "v": 0}
    vtok_state = {"evac": 0}

    def emit_vtok(kt_i, pool_cnt):
        tp = tpose(PS_VT[:], vt_b[:, kt_i * 128:(kt_i + 1) * 128], ident,
                   wait=[("act", pool_cnt), ("dve", vtok_state["evac"])], inc="pe")
        vop("tensor_copy", vtok[:, kt_i * 130: kt_i * 130 + 64], PS_VT[:, 0:64],
            wait=[("pe", tp)])
        vtok_state["evac"] = vop(
            "tensor_copy", vtok[:, kt_i * 130 + 65: kt_i * 130 + 129],
            PS_VT[:, 64:128], inc="dve")

    def emit_conv_rot(pair, cc, st):
        csz = CCH[cc]
        col0 = 512 * cc
        cds = cdg[pair % 2]
        for ip, pn in enumerate(projs):
            src = qh[pn][cc % 2]
            if pn == "v":
                dsl = vt_b[:, col0:col0 + csz]
            else:
                dsl = (convq if pn == "q" else convk)[:, 0:csz]
            cbb = cb[:, ip * 8 + pair: ip * 8 + pair + 1]
            ps = proj_ps[pn](csz)
            for tap in range(4):
                w = []
                if tap == 0:
                    w = [("act", evac_done[pn]), ("dve", convread[pn])]
                    if cc < 4:
                        w.append(("dve", st["tail"][pn]))
                mm(ps, cds[:, (ip * 4 + tap) * 128:(ip * 4 + tap + 1) * 128],
                   src[:, tap:tap + csz], tap == 0, tap == 3, wait=w,
                   inc="pe" if tap == 3 else None)
            evac_done[pn] = act(dsl, ps, AF.Identity, bias=cbb,
                                wait=[("pe", C["pe"])], inc="act")
        conv_done[(pair, cc)] = C["act"]
        pc = C["act"]
        for pn in ("q", "k"):
            cvb = convq if pn == "q" else convk
            rsb = rsin_q if pn == "q" else rsin_k
            rcb = rcos_q if pn == "q" else rcos_k
            rob = qrot if pn == "q" else krot
            qtb = qt_b if pn == "q" else kt_b
            rot_ps = PS_SA[1][:, 512:512 + csz] if pn == "q" else PS_AV[0][:, 0:csz]
            l2_ps = PS_AV[1] if pn == "q" else PS_PQ
            r1 = vop("tensor_tensor", rsb[:, 0:csz], cvb[:, 0:csz],
                     sint[:, col0:col0 + csz], A.mult,
                     wait=[("act", pc), ("pe", st.get("rotmm_" + pn, 0))],
                     inc="dve")
            r2 = vop("tensor_tensor", rcb[:, 0:csz], cvb[:, 0:csz],
                     cost[:, col0:col0 + csz], A.mult, inc="dve")
            convread[pn] = r2
            rm = mm(rot_ps, rot_t, rsb[:, 0:csz], True, True,
                    wait=[("dve", r2), ("dve", st.get("qrot_" + pn, 0))], inc="pe")
            st["rotmm_" + pn] = rm
            qr = vop("tensor_tensor", rob[:, 0:csz], rcb[:, 0:csz], rot_ps, A.add,
                     wait=[("pe", rm), ("act", st.get("sq_" + pn, 0))], inc="dve")
            st["qrot_" + pn] = qr
            sq_ = act(sqq[:, 0:csz], rob[:, 0:csz], AF.Square,
                      wait=[("dve", qr), ("pe", st.get("l2mm_" + pn, 0))], inc="act")
            st["sq_" + pn] = sq_
            sm = mm(l2_ps[0:2, 0:csz], sel2, sqq[:, 0:csz], True, True,
                    wait=[("act", sq_), ("dve", st.get("qt_" + pn, 0))], inc="pe")
            act(rr2[:, 0:csz], l2_ps[0:2, 0:csz], AF.Ln, bias=actb[0:2, 1:2],
                wait=[("pe", sm)])
            if pn == "q":
                l2e = act(rr2b[:, 0:csz], rr2[:, 0:csz], AF.Exp, scale=-0.5,
                          bias=actb[0:2, 2:3], inc="act")
            else:
                l2e = act(rr2b[:, 0:csz], rr2[:, 0:csz], AF.Exp, scale=-0.5,
                          inc="act")
            st["l2e_" + pn] = l2e
            rp = mm(l2_ps[:, 0:csz], selT2, rr2b[:, 0:csz], True, True,
                    wait=[("act", l2e)], inc="pe")
            st["l2mm_" + pn] = rp
            qt_ = vop("tensor_tensor", qtb[:, col0:col0 + csz], rob[:, 0:csz],
                      l2_ps[:, 0:csz], A.mult, wait=[("pe", rp)], inc="dve")
            st["qt_" + pn] = qt_
        t0 = (col0 + 127) // 128
        t1 = (col0 + csz) // 128
        for kt_i in range(t0, t1):
            emit_vtok(kt_i, pc)

    def qkv_phase(pair, barrier):
        st = {"tail": {pn: 0 for pn in projs}}
        for ci, csz in enumerate(PCH):
            base = 126 + 512 * ci
            for pn in projs:
                ps = proj_ps[pn](csz)
                for dc in range(8):
                    w = []
                    if dc == 0:
                        w = [("sg", pw_ld[pair]), ("act", evac_done[pn])]
                        if ci == 0:
                            w = w + barrier
                    mm(ps, wpr[pn][:, dc * 128:(dc + 1) * 128],
                       xn[:, dc * EXT + base: dc * EXT + base + csz],
                       dc == 0, dc == 7, wait=w, inc="pe" if dc == 7 else None)
                if ci < 5:
                    dst = qh[pn][ci % 2][:, 0:csz]
                else:
                    dst = qh[pn][0][:, 512:514]
                ww = [("pe", C["pe"])]
                if ci >= 2:
                    ww.append(("act", conv_done.get((pair, ci - 2), 0)))
                evac_done[pn] = act(dst, ps, AF.Identity,
                                    bias=bqkv[pn][:, pair:pair + 1],
                                    wait=ww, inc="act")
            if 1 <= ci <= 4:
                for pn in projs:
                    st["tail"][pn] = vop(
                        "tensor_copy", qh[pn][(ci - 1) % 2][:, 512:515],
                        qh[pn][ci % 2][:, 0:3],
                        wait=[("act", evac_done[pn])], inc="dve")
                emit_conv_rot(pair, ci - 1, st)
        emit_conv_rot(pair, 4, st)

    def attn_inner(pair, qkv_barrier):
        its = [(h, q) for h in range(2) for q in range(16)]
        n = len(its)
        sd, ed, td, avd, dd_, rpd, nmd = {}, {}, {}, {}, {}, {}, {}
        for k in range(-4, 0):
            ed[k] = td[k] = avd[k] = dd_[k] = rpd[k] = nmd[k] = 0

        def emit_S(i):
            head, qt = its[i]
            hb = 64 * head
            sa = PS_SA[i % 2]
            for m in range(5):
                kt_i = qt + m
                dst = sa[:, m * 128:(m + 1) * 128] if m < 4 else sa[:, 512:640]
                w = []
                if m == 0:
                    w = [("act", ed[i - 2])]
                    if i == 0:
                        w = w + qkv_barrier
                mm(dst, kt_b[hb:hb + 64, kt_i * 128:(kt_i + 1) * 128],
                   qt_b[hb:hb + 64, 511 + qt * 128: 511 + qt * 128 + 128],
                   True, True, wait=w, inc="pe" if m == 4 else None)
            sd[i] = C["pe"]

        def emit_exp(i):
            head, qt = its[i]
            sa = PS_SA[i % 2]
            if qt <= 3:
                for m in range(5):
                    kt_i = qt + m
                    src = sa[:, m * 128:(m + 1) * 128] if m < 4 else sa[:, 512:640]
                    ee = act(pt[i % 2][:, m * 128:(m + 1) * 128], src, AF.Exp,
                             bias=kmask[:, kt_i:kt_i + 1],
                             wait=[("pe", sd[i])] if m == 0 else [],
                             inc="act" if m in (0, 4) else None)
                    if m == 0:
                        e0 = ee
            else:
                e0 = act(pt[i % 2][:, 0:128], sa[:, 0:128], AF.Exp,
                         wait=[("pe", sd[i])], inc="act")
                act(pt[i % 2][:, 128:512], sa[:, 128:512], AF.Exp)
                ee = act(pt[i % 2][:, 512:640], sa[:, 512:640], AF.Exp, inc="act")
            ed[i] = ee
            vop("tensor_tensor", pt[i % 2][:, 0:128], pt[i % 2][:, 0:128], tri0,
                A.mult, wait=[("act", e0)])
            td[i] = vop("tensor_tensor", pt[i % 2][:, 512:640], pt[i % 2][:, 512:640],
                        tri4, A.mult, wait=[("act", ee)], inc="dve")

        def emit_AV(i):
            head, qt = its[i]
            av = PS_AV[i % 2]
            for j, m in enumerate((1, 2, 3, 0, 4)):
                kt_i = qt + m
                w = []
                if j == 0:
                    w = [("act", ed[i]), ("dve", nmd[i - 2])]
                if m == 0:
                    w = [("dve", td[i])]
                mm(av[0:65, 0:128],
                   vtok[:, kt_i * 130 + head * 65: kt_i * 130 + head * 65 + 65],
                   pt[i % 2][:, m * 128:(m + 1) * 128], j == 0, j == 4,
                   wait=w, inc="pe" if j == 4 else None)
            avd[i] = C["pe"]
            oc = pair * 2048 + qt * 128
            hb = 64 * head
            act(attnout[hb:hb + 64, oc:oc + 128], av[0:64, 0:128], AF.Copy,
                wait=[("pe", avd[i])])
            act(den[64:65, :], av[64:65, 0:128], AF.Ln)
            dd_[i] = act(denr[64:65, :], den[64:65, :], AF.Exp, scale=-1.0,
                         inc="act")

        def emit_norm(i):
            head, qt = its[i]
            av = PS_AV[i % 2]
            hb = 64 * head
            oc = pair * 2048 + qt * 128
            csel = cselA if head == 0 else cselB
            rp = mm(av[:, 128:256], csel[64:65, :], denr[64:65, :], True, True,
                    wait=[("act", dd_[i]), ("dve", nmd[i - 2])], inc="pe")
            rpd[i] = rp
            nmd[i] = vop("tensor_tensor", attnout[hb:hb + 64, oc:oc + 128],
                         attnout[hb:hb + 64, oc:oc + 128], av[hb:hb + 64, 128:256],
                         A.mult, wait=[("pe", rp)], inc="dve")

        for step in range(n):
            emit_S(step)
            emit_exp(step)
            if step >= 1:
                emit_AV(step - 1)
            if step >= 2:
                emit_norm(step - 2)
        emit_AV(n - 1)
        emit_norm(n - 2)
        emit_norm(n - 1)

    vtok_ones = vtok.rearrange("p (t c) -> p t c", c=130)
    woT_ld = 0
    for pair in range(8):
        barrier = allcnt() + [("dve", pad_dve), ("dve", ln_dve)]
        qkv_phase(pair, barrier)
        if pair + 1 < 8:
            dma("gpsimd", wpr["q"], wq_in[pair + 1], wait=[("pe", C["pe"])])
            dma("gpsimd", wpr["k"], wk_in[pair + 1])
            dma("gpsimd", wpr["v"], wv_in[pair + 1])
            dma("gpsimd", cdg[(pair + 1) % 2], cdiag_in[pair + 1])
            pw_ld[pair + 1] = C["sg"]
        vop("memset", vtok_ones[:, :, 64:65], 1.0, wait=[("act", C["act"])])
        vop("memset", vtok_ones[:, :, 129:130], 1.0, inc="dve")
        emit_vtok(19, C["act"])
        qkv_barrier = allcnt()
        if pair == 0:
            for tap, bufr in (("qt", qt_b), ("kt", kt_b), ("vt", vt_b)):
                if tap in DEBUG_TAPS:
                    dma("sync", dbg[tap], bufr[:], wait=qkv_barrier)
        attn_inner(pair, qkv_barrier)
        if pair == 7:
            dma("gpsimd", woT, wo_in, wait=[("pe", C["pe"])])
            woT_ld = C["sg"]
    attn_end = allcnt()
    if "attnout" in DEBUG_TAPS:
        dma("sync", dbg["attnout"], attnout[:], wait=attn_end)

    ld = dma("sync", raw64[:], cc_out.ap(), wait=[("scc", 1)] + attn_end)
    for cl in range(4):
        for hh in range(4):
            mm(PS_PQ[0:8, :], sel64[:, cl * 8:(cl + 1) * 8],
               raw64[:, hh * 512:(hh + 1) * 512], True, True,
               wait=[("sq", ld), ("dve", C["dve"])], inc="pe")
            vop("tensor_copy", la[:, cl * 2048 + hh * 512: cl * 2048 + (hh + 1) * 512],
                PS_PQ[0:8, :], wait=[("pe", C["pe"])], inc="dve")

    def sink_iter():
        for ch in range(8):
            sl = la[:, ch * 1024:(ch + 1) * 1024]
            e1 = act(exs[:], sl, AF.Exp,
                     wait=[("dve", C["dve"]), ("pe", C["pe"])], inc="act")
            mm(PS_PQ[0:1, 0:512], ones8f, exs[:, 0:512], True, True,
               wait=[("act", e1)])
            mm(PS_AV[0][0:1, 0:512], ones8f, exs[:, 512:1024], True, True,
               inc="pe")
            act(lsee[:, 0:512], PS_PQ[0:1, 0:512], AF.Ln, wait=[("pe", C["pe"])])
            l1 = act(lsee[:, 512:1024], PS_AV[0][0:1, 0:512], AF.Ln, inc="act")
            for hh in range(2):
                mm(PS_SA[0][0:8, hh * 512:(hh + 1) * 512], ones18f,
                   lsee[:, hh * 512:(hh + 1) * 512], True, True,
                   wait=[("act", l1), ("dve", C["dve"])] if hh == 0 else [],
                   inc="pe" if hh == 1 else None)
            vop("tensor_tensor", sl, sl, PS_SA[0][0:8, 0:1024], A.subtract,
                wait=[("pe", C["pe"])], inc="dve")
        zz = vop("memset", ts8, 0.0, inc="dve")
        for ch in range(8):
            sl = la[:, ch * 1024:(ch + 1) * 1024]
            e2 = act(exs[:], sl, AF.Exp,
                     wait=[("dve", C["dve"]), ("pe", C["pe"])], inc="act")
            rr_ = vop("reduce_sum", ts8b, exs[:], AX.X, wait=[("act", e2)],
                      inc="dve")
            zz = vop("tensor_tensor", ts8, ts8, ts8b, A.add,
                     wait=[("dve", rr_)], inc="dve")
        fz = vop("tensor_copy", fl1[0:8, 0:1], ts8, wait=[("dve", zz)], inc="dve")
        act(lsef, ts8, AF.Ln, wait=[("dve", fz)])
        lt = act(fl1[0:8, 1:2], lsef, AF.Copy, inc="act")
        for ch in range(8):
            vop("tensor_scalar", la[:, ch * 1024:(ch + 1) * 1024],
                la[:, ch * 1024:(ch + 1) * 1024], lsef, None, A.subtract,
                wait=[("act", lt)] if ch == 0 else [], inc="dve")

    for _ in range(3):
        sink_iter()
    if "la" in DEBUG_TAPS:
        dma("sync", dbg["la"], la[:], wait=[("dve", C["dve"])])
    skw = dma("sync", sk_scr.ap(), la[:], wait=[("dve", C["dve"])])
    for cl in range(4):
        skr = dma("sync", raw64[cl * 8:(cl + 1) * 8, :],
                  sk_scr.ap()[:, cl * T:(cl + 1) * T], wait=[("sq", skw)])
    ex = act(ex32[:], raw64[0:32, :], AF.Exp, wait=[("sq", skr)], inc="act")
    for g2 in range(2):
        for hh in range(2):
            mm(PS_SA[0][0:8, hh * 512:(hh + 1) * 512], m32,
               ex32[:, g2 * 1024 + hh * 512: g2 * 1024 + (hh + 1) * 512],
               True, True,
               wait=[("act", ex), ("dve", C["dve"])] if hh == 0 else [],
               inc="pe" if hh == 1 else None)
        vop("tensor_copy", gn_own[0:8, g2 * 1024:(g2 + 1) * 1024],
            PS_SA[0][0:8, 0:1024], wait=[("pe", C["pe"])], inc="dve")
    for hh in range(4):
        mm(PS_PQ[0:1, 0:512], g32, ex32[:, hh * 512:(hh + 1) * 512], True, True,
           wait=[("dve", C["dve"])], inc="pe")
        vop("tensor_scalar", dn_own[:, hh * 512:(hh + 1) * 512], PS_PQ[0:1, 0:512],
            1e-8, None, A.add, wait=[("pe", C["pe"])], inc="dve")
    rcp = vop("reciprocal", dn_own[:], dn_own[:], inc="dve")
    for g2 in range(2):
        for hh in range(2):
            mm(PS_SA[1][0:8, hh * 512:(hh + 1) * 512], ones18f,
               dn_own[:, g2 * 1024 + hh * 512: g2 * 1024 + (hh + 1) * 512],
               True, True,
               wait=[("dve", rcp if g2 == 0 else C["dve"])] if hh == 0 else [],
               inc="pe" if hh == 1 else None)
        vop("tensor_tensor", gn_own[0:8, g2 * 1024:(g2 + 1) * 1024],
            gn_own[0:8, g2 * 1024:(g2 + 1) * 1024], PS_SA[1][0:8, 0:1024],
            A.mult, wait=[("pe", C["pe"])], inc="dve")
    if "gates" in DEBUG_TAPS:
        dma("sync", dbg["gates"], gn_own[:], wait=[("dve", C["dve"])])
    vop("memset", baserep, 0.0, inc="dve")
    for a in range(16):
        tp = tpose(PS_PQ[:, 0:128], gn_own[:, a * 128:(a + 1) * 128], identf,
                   wait=[("dve", C["dve"])], inc="pe")
        q_ = vop("tensor_copy", wtile, PS_PQ[:, 0:8], wait=[("pe", tp)], inc="dve")
        q_ = vop("reduce_max", wm1, wtile, AX.X, wait=[("dve", q_)], inc="dve")
        q_ = vop("tensor_scalar", wmsk1, wtile, wm1, None, A.is_equal,
                 wait=[("dve", q_)], inc="dve")
        q_ = vop("scalar_tensor_tensor", wg2v, wmsk1, NEG, wtile, A.mult, A.add,
                 wait=[("dve", q_)], inc="dve")
        q_ = vop("reduce_max", wm2, wg2v, AX.X, wait=[("dve", q_)], inc="dve")
        q_ = vop("tensor_scalar", wmsk2, wtile, wm2, None, A.is_ge,
                 wait=[("dve", q_)], inc="dve")
        q_ = vop("tensor_tensor", wden, wm1, wm2, A.add, wait=[("dve", q_)],
                 inc="dve")
        q_ = vop("tensor_scalar", wden, wden, 1e-8, None, A.add,
                 wait=[("dve", q_)], inc="dve")
        q_ = vop("reciprocal", wden, wden, wait=[("dve", q_)], inc="dve")
        q_ = vop("tensor_scalar", wtile, wtile, wden, None, A.mult,
                 wait=[("dve", q_)], inc="dve")
        q_ = vop("tensor_tensor", wgt[:, a * 8:(a + 1) * 8], wtile, wmsk2, A.mult,
                 wait=[("dve", q_)], inc="dve")
        q_ = vop("tensor_copy", fbf, wmsk2, wait=[("dve", q_)], inc="dve")
        pm_ = mm(PS_PQ[:, 128:136], ltri, fbf, True, True,
                 wait=[("dve", q_)], inc="pe")
        cm_ = mm(PS_PQ[0:1, 136:144], ones128, fbf, True, True, inc="pe")
        q_ = vop("tensor_copy", cntsb, PS_PQ[0:1, 136:144],
                 wait=[("pe", cm_)], inc="dve")
        rm_ = mm(PS_PQ[:, 144:152], ones1_128f, cntsb, True, True,
                 wait=[("dve", q_)], inc="pe")
        q_ = vop("tensor_tensor", sidx, PS_PQ[:, 128:136], baserep, A.add,
                 wait=[("pe", rm_)], inc="dve")
        q_ = vop("tensor_tensor", baserep, baserep, PS_PQ[:, 144:152], A.add,
                 wait=[("dve", q_)], inc="dve")
        q_ = vop("tensor_scalar", sinr, sidx, float(CAP), None, A.is_lt,
                 wait=[("dve", q_)], inc="dve")
        q_ = vop("tensor_tensor", svld, sinr, wmsk2, A.mult,
                 wait=[("dve", q_)], inc="dve")
        q_ = vop("tensor_tensor", stmp, sidx, ecap, A.add,
                 wait=[("dve", q_)], inc="dve")
        q_ = vop("tensor_scalar", sbig, svld, -BIGIDX, BIGIDX, A.mult, A.add,
                 wait=[("dve", q_)], inc="dve")
        q_ = vop("tensor_tensor", stmp, stmp, svld, A.mult,
                 wait=[("dve", q_)], inc="dve")
        q_ = vop("tensor_tensor", stmp, stmp, sbig, A.add,
                 wait=[("dve", q_)], inc="dve")
        q_ = vop("tensor_tensor", msk21, wmsk2, wmsk1, A.subtract,
                 wait=[("dve", q_)], inc="dve")
        q_ = vop("tensor_tensor", sbig, stmp, wmsk1, A.mult,
                 wait=[("dve", q_)], inc="dve")
        q_ = vop("reduce_sum", s1f, sbig, AX.X, wait=[("dve", q_)], inc="dve")
        q_ = vop("tensor_copy", s1i[:, a:a + 1], s1f, wait=[("dve", q_)], inc="dve")
        q_ = vop("tensor_tensor", sbig, stmp, msk21, A.mult,
                 wait=[("dve", q_)], inc="dve")
        q_ = vop("reduce_sum", s2f, sbig, AX.X, wait=[("dve", q_)], inc="dve")
        q_ = vop("tensor_copy", s2i[:, a:a + 1], s2f, wait=[("dve", q_)], inc="dve")
        q_ = vop("tensor_tensor", sbig, wgt[:, a * 8:(a + 1) * 8], wmsk1, A.mult,
                 wait=[("dve", q_)], inc="dve")
        q_ = vop("reduce_sum", tg1[:, a:a + 1], sbig, AX.X,
                 wait=[("dve", q_)], inc="dve")
        q_ = vop("tensor_tensor", sbig, wgt[:, a * 8:(a + 1) * 8], msk21, A.mult,
                 wait=[("dve", q_)], inc="dve")
        q_ = vop("reduce_sum", tg2[:, a:a + 1], sbig, AX.X,
                 wait=[("dve", q_)], inc="dve")
    gates_done = C["dve"]
    BCREG = {}
    P.op("gpsimd", lambda h: BCREG.__setitem__("r", h.to_reg(SLOTS - 1)))
    scat = {-2: 0, -1: 0}
    xn_til = xn_dram.ap().rearrange("(t p) d -> t p d", p=128)
    gath_ap = gath.ap()
    for tt in range(16):
        ld = dma("sync", xsc[tt % 2], xn_til[tt],
                 wait=[("sg", scat[tt - 2]), ("sq", xnd_wr[0])])
        P.op("gpsimd", lambda h, tt=tt: h.indirect_dma_start(
            out=gath_ap, out_offset=bass.IndirectOffsetOnAxis(
                ap=s1i[:, tt:tt + 1], axis=0),
            in_=xsc[tt % 2][:], in_offset=None, bounds_check=BCREG["r"],
            oob_is_err=False),
            wait=[("sq", ld), ("dve", gates_done)], inc="sg", k=16)
        scat[tt] = P.op("gpsimd", lambda h, tt=tt: h.indirect_dma_start(
            out=gath_ap, out_offset=bass.IndirectOffsetOnAxis(
                ap=s2i[:, tt:tt + 1], axis=0),
            in_=xsc[tt % 2][:], in_offset=None, bounds_check=BCREG["r"],
            oob_is_err=False), inc="sg", k=16)
    scat_end = C["sg"]
    if "sl1" in DEBUG_TAPS:
        dma("gpsimd", dbg["sl1"], s1i[:], wait=[("dve", gates_done)])
        dma("gpsimd", dbg["sl2"], s2i[:], wait=[("dve", gates_done)])
        dma("sync", dbg["tgs"][:, 0:16], tg1[:], wait=[("dve", gates_done)])
        dma("sync", dbg["tgs"][:, 16:32], tg2[:], wait=[("dve", gates_done)])
    if "wgt" in DEBUG_TAPS:
        dma("sync", dbg["wgt"], wgt[:], wait=[("dve", gates_done)])

    pre_wo_tap = 0
    if "attnout2" in DEBUG_TAPS:
        dma("sync", dbg["attnout2"], attnout[:], wait=[("dve", gates_done)])
        pre_wo_tap = dma("sync", dbg["woT"], woT[:], wait=[("sg", woT_ld)])
    wo_evac = {0: 0, 1: 0}
    for tt in range(16):
        for oh in range(2):
            i2 = (tt * 2 + oh) % 2
            ps = PS_AV[i2][:]
            for cc in range(8):
                w = []
                if cc == 0:
                    w = [("dve", wo_evac[i2]), ("sg", woT_ld)]
                    if tt == 0 and oh == 0:
                        w = w + [("dve", gates_done), ("pe", C["pe"]),
                                 ("sq", pre_wo_tap)]
                mm(ps, attnout[:, cc * 2048 + tt * 128: cc * 2048 + tt * 128 + 128],
                   woT[:, cc * 1024 + oh * 512: cc * 1024 + oh * 512 + 512],
                   cc == 0, cc == 7, wait=w, inc="pe" if cc == 7 else None)
            wo_evac[i2] = vop(
                "tensor_copy",
                acc[:, tt * 1024 + oh * 512: tt * 1024 + oh * 512 + 512],
                ps, wait=[("pe", C["pe"])], inc="dve")
    accwo_tap = 0
    if "accwo" in DEBUG_TAPS:
        accwo_tap = dma("sync", dbg["accwo"], acc[:], wait=[("dve", C["dve"])])

    ffn_barrier = allcnt()
    oslots_til = oslots.ap().rearrange("(t p) d -> t p d", p=128)
    gath_til = gath.ap().rearrange("(t p) d -> t p d", p=128)
    dma("gpsimd", wgb[0], wg_s[0, 0], wait=[("pe", C["pe"])])
    dma("gpsimd", wpb[0], wp_s[0, 0])
    wg_ld = {0: C["sg"]}
    dma("gpsimd", wob[:, 0:8192], wo_s[0, 0], wait=[("pe", C["pe"])])
    dma("gpsimd", wob[:, 8192:16384], wo_s[0, 1])
    wo_ld2 = {0: C["sg"]}
    wo_pe = {}
    for st in range(6):
        dma("sync", xg[:, st * 1024:(st + 1) * 1024], gath_til[st],
            wait=[("sg", scat_end)] if st == 0 else [])
    xg_ld = {0: C["sq"]}
    if "xg0" in DEBUG_TAPS:
        dma("sync", dbg["xg0"], xg[:], wait=[("sg", C["sg"])])
    gp_evac = {}
    hmul_done = {}
    wo_evac2 = {0: 0, 1: 0}
    osb_wr = {-2: 0, -1: 0}
    TCH = [(0, 512), (512, 256)]

    tp_evac = {"act": 0}

    def emit_tp(e, idx):
        st, dc = idx // 8, idx % 8
        fme = xg_fm[e % 2]
        tpv = tpose(PS_VT[:], xg[:, st * 1024 + dc * 128:
                                 st * 1024 + dc * 128 + 128], ident,
                    wait=[("sq", xg_ld[e]), ("act", tp_evac["act"])], inc="pe")
        tp_evac["act"] = act(
            fme[:, dc * CAP + st * 128: dc * CAP + st * 128 + 128],
            PS_VT[:], AF.Copy, wait=[("pe", tpv)], inc="act")

    for idx in range(48):
        emit_tp(0, idx)
    tp_done = {0: C["act"]}
    for e in range(E):
        tp_i = [0]
        if e + 1 < E:
            for st in range(6):
                dma("sync", xg[:, st * 1024:(st + 1) * 1024],
                    gath_til[(e + 1) * 6 + st],
                    wait=[("pe", C["pe"])] if st == 0 else [])
            xg_ld[e + 1] = C["sq"]
        fme = xg_fm[e % 2]
        for fc in range(16):
            cur = e * 16 + fc
            nxt = cur + 1
            if nxt < 128 and nxt not in wg_ld:
                dma("gpsimd", wgb[nxt % 2], wg_s[nxt // 16, nxt % 16],
                    wait=[("pe", C["pe"])])
                dma("gpsimd", wpb[nxt % 2], wp_s[nxt // 16, nxt % 16])
                wg_ld[nxt] = C["sg"]
            if e >= 1 and fc == 2:
                dma("gpsimd", wob[:, 0:8192], wo_s[e, 0],
                    wait=[("pe", wo_pe[e - 1])])
            if e >= 1 and fc == 6:
                dma("gpsimd", wob[:, 8192:16384], wo_s[e, 1])
                wo_ld2[e] = C["sg"]
            for ti, (toff, tn) in enumerate(TCH):
                hslice = hbuf[:, fc * CAP + toff: fc * CAP + toff + tn]
                qi = (fc * 2 + ti) % 4
                g_ps = (PS_SA[0][:, 0:512], PS_SA[0][:, 512:1024],
                        PS_AV[0][:], PS_AV[1][:])[qi]
                p_ps = (PS_SA[1][:, 0:512], PS_SA[1][:, 512:1024],
                        PS_PQ[:], PS_SA[1][:, 0:512])[qi % 3]
                for which, wbuf, ps_, ebuf, fn, slot in (
                    ("g", wgb[cur % 2], g_ps, gbuf, AF.Relu, qi),
                    ("p", wpb[cur % 2], p_ps, pbuf, AF.Copy, qi % 3),
                ):
                    ps = ps_[:, 0:tn] if ps_.shape[1] > tn else ps_
                    for dc in range(8):
                        w = []
                        if dc == 0:
                            w = [("sg", wg_ld[cur]),
                                 ("act", gp_evac.get((which, slot), 0)),
                                 ("act", tp_done[e])]
                            if which == "g" and qi >= 2:
                                w.append(("dve", wo_evac2[qi - 2]))
                            if cur == 0 and ti == 0 and which == "g":
                                w = w + ffn_barrier
                        mm(ps, wbuf[:, dc * 128:(dc + 1) * 128],
                           fme[:, dc * CAP + toff: dc * CAP + toff + tn],
                           dc == 0, dc == 7, wait=w,
                           inc="pe" if dc == 7 else None)
                    gp_evac[(which, slot)] = act(
                        ebuf[ti][:, 0:tn], ps, fn,
                        wait=[("pe", C["pe"]), ("dve", hmul_done.get(ti, 0))],
                        inc="act")
                hmul_done[ti] = vop(
                    "tensor_tensor", hslice, gbuf[ti][:, 0:tn], pbuf[ti][:, 0:tn],
                    A.mult, wait=[("act", gp_evac[("g", qi)]),
                                  ("act", gp_evac[("p", qi % 3)])], inc="dve")

        hm = C["dve"]
        for st in range(6):
            for oh in range(2):
                i2 = (st * 2 + oh) % 2
                ps = PS_AV[i2][:]
                for fc in range(16):
                    w = []
                    if fc == 0:
                        w = [("sg", wo_ld2[e]), ("dve", wo_evac2[i2])]
                        if st == 0 and oh == 0:
                            w.append(("dve", hm))
                    mm(ps, hbuf[:, fc * CAP + st * 128: fc * CAP + st * 128 + 128],
                       wob[:, fc * 1024 + oh * 512: fc * 1024 + oh * 512 + 512],
                       fc == 0, fc == 15, wait=w, inc="pe" if fc == 15 else None)
                wo_evac2[i2] = vop(
                    "tensor_copy", osb[st % 2][:, oh * 512: oh * 512 + 512], ps,
                    wait=[("pe", C["pe"]), ("sq", osb_wr[st - 2])], inc="dve")
                if e + 1 < E:
                    for _ in range(4):
                        if tp_i[0] < 48:
                            emit_tp(e + 1, tp_i[0])
                            tp_i[0] += 1
            osb_wr[st] = dma("sync", oslots_til[e * 6 + st], osb[st % 2][:],
                             wait=[("dve", wo_evac2[(st * 2 + 1) % 2])])
        wo_pe[e] = C["pe"]
        if e + 1 < E:
            while tp_i[0] < 48:
                emit_tp(e + 1, tp_i[0])
                tp_i[0] += 1
            tp_done[e + 1] = C["act"]
        osb_wr[-2], osb_wr[-1] = osb_wr[4], osb_wr[5]

    ffn_done = C["dve"]
    ffn_sq = C["sq"]

    if "osl" in DEBUG_TAPS:
        dma("sync", dbg["osl"], oslots.ap(), wait=[("sq", ffn_sq), ("dve", ffn_done)])
        dma("sync", dbg["gat"], gath.ap(), wait=[("sq", ffn_sq)])
    out_til = out.rearrange("(a p) d -> a p d", p=128)
    bo_ld = dma("sync", bo_s, bo_in, wait=[("dve", ffn_done)])
    xf_user = {0: 0, 1: 0}
    gsl_done = {-1: 0}
    for tt in range(16):
        a = 5 + tt
        wl = [("dve", xf_user[tt % 2])]
        if tt < 2:
            wl.append(("dve", ffn_done))
        ld = dma("sync", xf[tt % 2], x_til[a], wait=wl)
        z1 = vop("memset", gsl[0][:], 0.0, wait=[("dve", gsl_done[tt - 1])],
                 inc="dve")
        g1 = P.op("gpsimd", lambda h, tt=tt: h.indirect_dma_start(
            out=gsl[0][:], out_offset=None, in_=oslots.ap(),
            in_offset=bass.IndirectOffsetOnAxis(ap=s1i[:, tt:tt + 1], axis=0),
            bounds_check=BCREG["r"], oob_is_err=False),
            wait=[("dve", z1), ("sq", ffn_sq)], inc="sg", k=16)
        z2 = vop("memset", gsl[1][:], 0.0, inc="dve")
        g2 = P.op("gpsimd", lambda h, tt=tt: h.indirect_dma_start(
            out=gsl[1][:], out_offset=None, in_=oslots.ap(),
            in_offset=bass.IndirectOffsetOnAxis(ap=s2i[:, tt:tt + 1], axis=0),
            bounds_check=BCREG["r"], oob_is_err=False),
            wait=[("dve", z2)], inc="sg", k=16)
        g2 = dma("gpsimd", flq, oslots.ap()[0:1, 0:32])
        g1 = g2
        if tt == 0 and "gs0" in DEBUG_TAPS:
            dma("sync", dbg["gs0"][:, 0:1024], gsl[0][:], wait=[("sg", g1)])
            gtap = dma("sync", dbg["gs0"][:, 1024:2048], gsl[1][:],
                       wait=[("sg", g2)])
            P.op("vector", lambda h: h.memset(wden, 0.0), wait=[("sq", gtap)])
        w = [("sq", ld)]
        if tt == 0:
            w.append(("dve", ffn_done))
        sl = acc[:, tt * 1024:(tt + 1) * 1024]
        vop("tensor_tensor", sl, sl, xf[tt % 2], A.add, wait=w)
        vop("tensor_tensor", sl, sl, bo_s, A.add,
            wait=[("sq", bo_ld)] if tt == 0 else ())
        vop("tensor_scalar", t1f, xf[tt % 2], m_all[:, a:a + 1],
            rs_all[:, a:a + 1], A.subtract, A.mult)
        vop("tensor_tensor", sl, sl, t1f, A.add)
        vop("tensor_scalar", t1f, gsl[0][:], tg1[:, tt:tt + 1], None, A.mult,
            wait=[("sg", g1)])
        vop("tensor_tensor", sl, sl, t1f, A.add)
        vop("tensor_scalar", t1f, gsl[1][:], tg2[:, tt:tt + 1], None, A.mult,
            wait=[("sg", g2)])
        gsl_done[tt] = xf_user[tt % 2] = vop(
            "tensor_tensor", sl, sl, t1f, A.add, inc="dve")
        dma("sync", out_til[tt], sl, wait=[("dve", xf_user[tt % 2])])
    final_sq = C["sq"]
    P.op("sync", lambda h: h.nop(), wait=[("sq", final_sq)])

    ctx.enter_context(nc.allow_low_precision(reason="bf16 matmul operands by design"))
    with nc.Block() as block:
        @block.sync
        def _(eng):
            for f in P.q["sync"]:
                f(eng)

        @block.gpsimd
        def _(eng):
            for f in P.q["gpsimd"]:
                f(eng)

        @block.vector
        def _(eng):
            for f in P.q["vector"]:
                f(eng)

        @block.scalar
        def _(eng):
            for f in P.q["scalar"]:
                f(eng)

        @block.tensor
        def _(eng):
            for f in P.q["tensor"]:
                f(eng)

    ctx.close()
    return nc



def _rotary_tables(t0):
    d = np.arange(64)
    inv = 1.0 / (10000.0 ** ((d % 32) * 2.0 / 64.0))
    pos = t0 - 511 + np.arange(KV)
    pos = np.where(pos < 0, 0, pos).astype(np.float64)
    ang = pos[None, :] * inv[:, None]
    cos = np.cos(ang).astype(np.float32)
    sin = np.sin(ang).astype(np.float32)
    return (np.ascontiguousarray(np.concatenate([cos, cos], axis=0)),
            np.ascontiguousarray(np.concatenate([sin, sin], axis=0)))


def _consts():
    c = {}
    c["identity"] = np.eye(128, dtype=np.float32).astype(ml_dtypes.bfloat16)
    c["identityf"] = np.eye(128, dtype=np.float32)
    R = np.zeros((128, 128), dtype=np.float32)
    for do in range(128):
        h64, dd = (do // 64) * 64, do % 64
        if dd < 32:
            R[do, h64 + dd + 32] = -1.0
        else:
            R[do, h64 + dd - 32] = 1.0
    c["rot_t"] = np.ascontiguousarray(R.T).astype(ml_dtypes.bfloat16)
    kp = np.arange(128)[:, None]
    qf = np.arange(128)[None, :]
    c["tri0"] = (kp >= qf).astype(ml_dtypes.bfloat16)
    c["tri4"] = (kp < qf).astype(ml_dtypes.bfloat16)
    sel2 = np.zeros((128, 2), dtype=np.float32)
    sel2[:64, 0] = 1.0
    sel2[64:, 1] = 1.0
    c["sel2"] = sel2.astype(ml_dtypes.bfloat16)
    c["selT2"] = np.ascontiguousarray(sel2.T).astype(ml_dtypes.bfloat16)
    csA = np.zeros((128, 128), dtype=np.float32)
    csA[64, :64] = 1.0
    c["colselA"] = csA.astype(ml_dtypes.bfloat16)
    csB = np.zeros((128, 128), dtype=np.float32)
    csB[64, 64:] = 1.0
    c["colselB"] = csB.astype(ml_dtypes.bfloat16)
    c["ones8f"] = np.ones((8, 1), dtype=np.float32)
    c["ones1_8f"] = np.ones((1, 8), dtype=np.float32)
    kk = np.arange(128)[:, None]
    mm_ = np.arange(128)[None, :]
    c["ltri"] = (kk < mm_).astype(ml_dtypes.bfloat16)
    c["ones128"] = np.ones((128, 1), dtype=ml_dtypes.bfloat16)
    c["ones1_128f"] = np.ones((1, 128), dtype=np.float32)
    c["ecap"] = np.ascontiguousarray(
        np.tile((np.arange(8) * CAP).astype(np.float32)[None, :], (128, 1)))
    ab = np.zeros((128, 4), dtype=np.float32)
    ab[:, 0] = 1e-5
    ab[:, 1] = 1e-12
    ab[:, 2] = np.log(0.125)
    c["actbias"] = ab
    return c


def _prep_weights(inp):
    w = {}
    for nm, key in (("wq", "L0_attn_Wq"), ("wk", "L0_attn_Wk"), ("wv", "L0_attn_Wv")):
        W = np.asarray(inp[key], dtype=np.float32)
        WT = np.ascontiguousarray(W.T)
        w[nm] = np.ascontiguousarray(
            WT.reshape(8, 128, 8, 128).transpose(2, 1, 0, 3)).reshape(8, 128, 1024)
    WoT = np.ascontiguousarray(np.asarray(inp["L0_attn_Wo"], dtype=np.float32).T)
    w["wo"] = np.ascontiguousarray(
        WoT.reshape(8, 128, 8, 128).transpose(1, 0, 2, 3)).reshape(128, 8192)
    for nm, key in (("bq", "L0_attn_bq"), ("bk", "L0_attn_bk"), ("bv", "L0_attn_bv")):
        w[nm] = np.ascontiguousarray(
            np.asarray(inp[key], dtype=np.float32).reshape(8, 128).T)
    w["bo_rep"] = np.ascontiguousarray(np.tile(
        np.asarray(inp["L0_attn_bo"], dtype=np.float32)[None, :], (128, 1)))
    cwl, cbl = [], []
    for kw_, kb_ in (("L0_attn_cqw", "L0_attn_cqb"), ("L0_attn_ckw", "L0_attn_ckb"),
                     ("L0_attn_cvw", "L0_attn_cvb")):
        cwl.append(np.asarray(inp[kw_], dtype=np.float32)
                   .reshape(8, 128, 4).transpose(1, 0, 2).reshape(128, 32))
        cbl.append(np.asarray(inp[kb_], dtype=np.float32).reshape(8, 128).T)
    w["cw"] = np.ascontiguousarray(np.concatenate(cwl, axis=1))
    w["cb"] = np.ascontiguousarray(np.concatenate(cbl, axis=1))
    cd = np.zeros((8, 128, 1536), dtype=np.float32)
    p_ = np.arange(128)
    for ip, kw_ in enumerate(("L0_attn_cqw", "L0_attn_ckw", "L0_attn_cvw")):
        wfull = np.asarray(inp[kw_], dtype=np.float32)
        for pair in range(8):
            for tap in range(4):
                cd[pair, p_, (ip * 4 + tap) * 128 + p_] = wfull[pair * 128 + p_, tap]
    w["cdiag"] = cd
    rw = np.asarray(inp["L0_ffn_router_w"], dtype=np.float32)
    w["router_wt"] = np.ascontiguousarray(
        rw.T.reshape(8, 128, 8).transpose(1, 0, 2)).reshape(128, 64)
    w["router_b"] = np.ascontiguousarray(
        np.asarray(inp["L0_ffn_router_b"], dtype=np.float32).reshape(8, 1))
    w["temp"] = np.ascontiguousarray(
        np.asarray(inp["L0_ffn_temperature"], dtype=np.float32).reshape(1, 1))
    wg = np.asarray(inp["L0_ffn_wg"], dtype=np.float32)
    wp = np.asarray(inp["L0_ffn_wp"], dtype=np.float32)
    wo = np.asarray(inp["L0_ffn_wo"], dtype=np.float32)
    w["wg_s"] = np.ascontiguousarray(
        wg.reshape(8, 8, 128, 16, 128).transpose(0, 3, 2, 1, 4)).reshape(8, 16, 128, 1024)
    w["wp_s"] = np.ascontiguousarray(
        wp.reshape(8, 8, 128, 16, 128).transpose(0, 3, 2, 1, 4)).reshape(8, 16, 128, 1024)
    w["wo_s"] = np.ascontiguousarray(
        wo.reshape(8, 2, 8, 128, 1024).transpose(0, 1, 3, 2, 4)).reshape(8, 2, 128, 8192)
    return w


_BUILT = {}


def kernel(**inputs):
    x = np.asarray(inputs["L0_attn_x"], dtype=np.float32)
    w = _prep_weights(inputs)
    cst = _consts()

    if "nc" not in _BUILT:
        _BUILT["nc"] = build_program()
    nc = _BUILT["nc"]

    shared = dict(cst)
    for k in ("wq", "wk", "wv", "wo", "bq", "bk", "bv", "bo_rep", "cw", "cb",
              "cdiag", "router_wt", "router_b", "temp", "wg_s", "wp_s", "wo_s"):
        shared[k] = w[k]

    rot_cache = {}
    in_maps = []
    for core in range(NCORES):
        b, s = core // 4, core % 4
        t0 = s * T
        m = dict(shared)
        xe = np.zeros((EXT, D), dtype=np.float32)
        lo = t0 - HALO
        src_lo = max(lo, 0)
        xe[src_lo - lo:, :] = x[b, src_lo: t0 + T, :]
        m["x_ext"] = xe
        if t0 not in rot_cache:
            rot_cache[t0] = _rotary_tables(t0)
        m["cost"], m["sint"] = rot_cache[t0]
        km = np.zeros((128, 20), dtype=np.float32)
        if s == 0:
            kj = np.arange(2560).reshape(20, 128).T
            km = np.where(kj < 511, NEG, 0.0).astype(np.float32)
        km[127, 19] = NEG
        m["kmask"] = np.ascontiguousarray(km)
        sel = np.zeros((64, 32), dtype=np.float32)
        for cl in range(4):
            for e in range(8):
                sel[(4 * b + cl) * 8 + e, cl * 8 + e] = 1.0
        m["sel64"] = sel
        m32_ = np.zeros((32, 8), dtype=np.float32)
        g32_ = np.zeros((32, 1), dtype=np.float32)
        for e in range(8):
            m32_[s * 8 + e, e] = 1.0
            g32_[s * 8 + e, 0] = 1.0
        m["m32"] = m32_
        m["g32"] = g32_
        in_maps.append(m)

    res = run_bass_kernel_spmd(nc, in_maps, core_ids=list(range(NCORES)))
    _BUILT["res"] = res
    outp = np.empty((B, S, D), dtype=np.float32)
    for core in range(NCORES):
        b, s = core // 4, core % 4
        outp[b, s * T:(s + 1) * T, :] = res.results[core]["out"]
    return outp


# version 16
# speedup vs baseline: 1.1401x; 1.1401x over previous
import numpy as np
import ml_dtypes
import contextlib
import concourse.bass as bass
import concourse.mybir as mybir
from concourse.bass_utils import run_bass_kernel_spmd
from concourse.alu_op_type import AluOpType as A

F32 = mybir.dt.float32
BF16 = mybir.dt.bfloat16
AF = mybir.ActivationFunctionType
AX = mybir.AxisListType

NCORES = 8
B, S, D = 2, 8192, 1024
E, FF = 8, 2048
T = 2048
HALO = 640
EXT = HALO + T
NT = EXT // 128
KV = 2560
NEG = -1.0e30
PCH = [512, 512, 512, 512, 512, 2]
CCH = [512, 512, 512, 512, 511]

DEBUG_TAPS = ()
CAP = 768
SLOTS = 8 * CAP
BIGIDX = 1.0e9

ENGS = ("sync", "gpsimd", "vector", "scalar", "tensor")


class Plan:
    def __init__(self, nc, ctx, semnames):
        self.nc = nc
        self.q = {e: [] for e in ENGS}
        self.cnt = {}
        self.sems = {}
        for s in semnames:
            self.sems[s] = ctx.enter_context(nc.semaphore(s))
            self.cnt[s] = 0

    def op(self, eng, fn, wait=(), inc=None, k=1):
        w = [(self.sems[s], v) for s, v in wait if v > 0]
        if inc is not None:
            self.cnt[inc] += k
            ih = self.sems[inc]

            def run(h, fn=fn, w=w, ih=ih, k=k):
                for s, v in w:
                    h.wait_ge(s, v)
                fn(h).then_inc(ih, k)
        else:
            def run(h, fn=fn, w=w):
                for s, v in w:
                    h.wait_ge(s, v)
                fn(h)
        self.q[eng].append(run)
        return self.cnt[inc] if inc is not None else None


def build_program():
    nc = bass.Bass("TRN2", target_bir_lowering=False)
    ctx = contextlib.ExitStack()

    def din(name, shape, dt=F32):
        return nc.dram_tensor(name, shape, dt, kind="ExternalInput").ap()

    x_ext = din("x_ext", [EXT, D])
    wq_in = din("wq", [8, 128, 1024])
    wk_in = din("wk", [8, 128, 1024])
    wv_in = din("wv", [8, 128, 1024])
    wo_in = din("wo", [128, 8192])
    bq_in = din("bq", [128, 8])
    bk_in = din("bk", [128, 8])
    bv_in = din("bv", [128, 8])
    bo_in = din("bo_rep", [128, 1024])
    cw_in = din("cw", [128, 96])
    cdiag_in = din("cdiag", [8, 128, 1536])
    cb_in = din("cb", [128, 24])
    cost_in = din("cost", [128, KV])
    sint_in = din("sint", [128, KV])
    kmask_in = din("kmask", [128, 20])
    rwt_in = din("router_wt", [128, 64])
    rb_in = din("router_b", [8, 1])
    temp_in = din("temp", [1, 1])
    wg_s = din("wg_s", [8, 16, 128, 1024])
    wp_s = din("wp_s", [8, 16, 128, 1024])
    wo_s = din("wo_s", [8, 2, 128, 8192])
    ident_in = din("identity", [128, 128], BF16)
    identf_in = din("identityf", [128, 128], F32)
    rot_in = din("rot_t", [128, 128], BF16)
    tri0_in = din("tri0", [128, 128], BF16)
    tri4_in = din("tri4", [128, 128], BF16)
    sel2_in = din("sel2", [128, 2], BF16)
    selT2_in = din("selT2", [2, 128], BF16)
    cselA_in = din("colselA", [128, 128], BF16)
    cselB_in = din("colselB", [128, 128], BF16)
    ones8f_in = din("ones8f", [8, 1])
    ones18f_in = din("ones1_8f", [1, 8])
    sel64_in = din("sel64", [64, 32])
    ltri_in = din("ltri", [128, 128], BF16)
    ones128_in = din("ones128", [128, 1], BF16)
    ones1_128f_in = din("ones1_128f", [1, 128])
    ecap_in = din("ecap", [128, 8])
    m32_in = din("m32", [32, 8])
    g32_in = din("g32", [32, 1])
    actb_in = din("actbias", [128, 4])

    out = nc.dram_tensor("out", [T, D], F32, kind="ExternalOutput").ap()
    dbg = {}
    for tap, shape, dt in (
        ("xn", [128, 8 * EXT], BF16), ("qt", [128, KV], BF16), ("kt", [128, KV], BF16),
        ("vt", [128, KV], BF16), ("attnout", [128, 16384], BF16),
        ("accwo", [128, 16384], F32), ("gates", [128, 2048], F32),
        ("wgt", [128, 128], F32), ("scores", [8, 2048], F32), ("la", [8, 8192], F32),
        ("accffn", [128, 16384], F32), ("h0", [128, 12288], BF16),
        ("sl1", [128, 16], F32), ("sl2", [128, 16], F32),
        ("gs0", [128, 2048], BF16),
        ("osl", [SLOTS, 1024], BF16), ("gat", [SLOTS, 1024], BF16),
        ("tgs", [128, 32], F32), ("xg0", [128, 6144], BF16),
        ("attnout2", [128, 16384], BF16), ("woT", [128, 8192], BF16),
        ("lnstats", [128, 48], F32),
    ):
        if tap in DEBUG_TAPS:
            dbg[tap] = nc.dram_tensor("dbg_" + tap, shape, dt, kind="ExternalOutput").ap()

    xn_dram = nc.dram_tensor("xn_dram", [T, D], BF16)
    gath = nc.dram_tensor("gath", [SLOTS, D], BF16)
    oslots = nc.dram_tensor("oslots", [SLOTS, D], BF16)
    cc_in = nc.dram_tensor("cc_in", [8, T], F32)
    cc_out = nc.dram_tensor("cc_out", [64, T], F32)
    sk_scr = nc.dram_tensor("sk_scr", [8, 8192], F32)

    MB = 27648
    TOP = 229300

    def sb(name, shape, dt, off):
        assert off % 32 == 0, name
        nb = shape[1] * (2 if dt == BF16 else 4)
        assert MB + off + nb <= TOP, (name, off, nb)
        return nc.alloc_sbuf_tensor_at(name, shape, dt, offset=MB + off).ap()

    acc = sb("acc", [128, 16384], F32, 0)
    cost = sb("cost", [128, KV], F32, 0)
    sint = sb("sint", [128, KV], F32, 10240)
    xld = [sb(f"xld{i}", [128, 1024], F32, 20480 + 4096 * i) for i in range(2)]
    dev = sb("dev", [128, 1024], F32, 28672)
    sqd = sb("sqd", [128, 1024], F32, 32768)
    sc_r = sb("sc_r", [8, T], F32, 28672)
    qh = {pn: [sb(f"qh_{pn}{j}", [128, 516], BF16, 36864 + (2 * i + j) * 2080)
               for j in range(2)] for i, pn in enumerate(("q", "k", "v"))}
    convq = sb("convq", [128, 512], F32, 49344)
    convk = sb("convk", [128, 512], F32, 51392)
    rsin_q = sb("rsin_q", [128, 512], BF16, 53440)
    rsin_k = sb("rsin_k", [128, 512], BF16, 54464)
    rcos_q = sb("rcos_q", [128, 512], F32, 55488)
    rcos_k = sb("rcos_k", [128, 512], F32, 57536)
    qrot = sb("qrot", [128, 512], F32, 59584)
    krot = sb("krot", [128, 512], F32, 61632)
    sqq = sb("sqq", [128, 512], BF16, 63680)
    gn_own = sb("gn_own", [128, T], F32, 36864)
    xnt_bf = sb("xnt_bf", [128, 1024], BF16, 45056)
    ex32 = sb("ex32", [32, T], F32, 45056)
    raw64 = sb("raw64", [64, T], F32, 53248)
    B2 = 65536
    wpr = {pn: sb(f"wpr_{pn}", [128, 1024], BF16, B2 + i * 2048)
           for i, pn in enumerate(("q", "k", "v"))}
    qt_b = sb("qt_b", [128, KV], BF16, B2 + 6144)
    kt_b = sb("kt_b", [128, KV], BF16, B2 + 11264)
    vt_b = sb("vt_b", [128, KV], BF16, B2 + 16384)
    vtok = sb("vtok", [128, 20 * 130], BF16, B2 + 21504)
    pt = [sb(f"pt{i}", [128, 640], BF16, B2 + 26752 + 1280 * i) for i in range(2)]
    rr2 = sb("rr2", [2, 512], F32, B2 + 29312)
    rr2b = sb("rr2b", [2, 512], BF16, B2 + 31360)
    la = sb("la", [8, 8192], F32, B2 + 0)
    exs = sb("exs", [8, 1024], F32, B2 + 32768)
    lsee = sb("lsee", [1, 1024], F32, 61440)
    dn_own = sb("dn_own", [1, T], F32, B2 + 8192)
    wob = sb("wob", [128, 16384], BF16, B2 + 0)
    gbuf = [sb(f"gbuf{i}", [128, 512], BF16, B2 + 32768 + 1024 * i) for i in range(2)]
    pbuf = [sb(f"pbuf{i}", [128, 512], BF16, B2 + 34816 + 1024 * i) for i in range(2)]
    xsc = [sb(f"xsc{i}", [128, 1024], BF16, B2 + 0 + 2048 * i) for i in range(2)]
    osb = [sb(f"osb{i}", [128, 1024], BF16, 135168 + 36864 + 2048 * i)
           for i in range(2)]
    B3 = 102400
    attnout = sb("attnout", [128, 16384], BF16, B3)
    hbuf = sb("hbuf", [128, 16 * CAP], BF16, B3)
    xn = sb("xn", [128, 8 * EXT], BF16, 135168)
    XNR = 135168
    xg = sb("xg", [128, 6 * 1024], BF16, XNR)
    xg_fm = [sb(f"xg_fm{i}", [128, 8 * CAP], BF16, XNR + 12288 + 12288 * i)
             for i in range(2)]
    wgb = [sb(f"wgb{i}", [128, 1024], BF16, B3 + 24576 + 2048 * i) for i in range(2)]
    wpb = [sb(f"wpb{i}", [128, 1024], BF16, B3 + 28672 + 2048 * i) for i in range(2)]
    W4 = 178208
    woT = sb("woT", [128, 8192], BF16, W4)
    cdg = [sb(f"cdg{i}", [128, 1536], BF16, W4 + 3072 * i) for i in range(2)]
    xf = [sb(f"xf{i}", [128, 1024], F32, W4 + 4096 * i) for i in range(2)]
    gsl = [sb(f"gsl{i}", [128, 1024], BF16, B2 + 0 + 2048 * i) for i in range(2)]
    t1f = sb("t1f", [128, 1024], F32, W4 + 8192)
    bo_s = sb("bo_s", [128, 1024], F32, W4 + 12288)
    xnt = [sb(f"xnt{i}", [128, 1024], F32, 36864 + 4096 * i) for i in range(2)]
    off = [194624]

    def psm(name, shape, dt):
        t = sb(name, shape, dt, off[0])
        nb = shape[1] * (2 if dt == BF16 else 4)
        off[0] += (nb + 31) // 32 * 32
        return t

    m_all = psm("m_all", [128, 24], F32)
    rs_all = psm("rs_all", [128, 24], F32)
    ms1 = psm("ms1", [128, 1], F32)
    ms2 = psm("ms2", [128, 1], F32)
    ms3 = psm("ms3", [128, 1], F32)
    ident = psm("ident", [128, 128], BF16)
    identf = psm("identf", [128, 128], F32)
    rot_t = psm("rot_t", [128, 128], BF16)
    tri0 = psm("tri0", [128, 128], BF16)
    tri4 = psm("tri4", [128, 128], BF16)
    sel2 = psm("sel2", [128, 2], BF16)
    selT2 = psm("selT2", [2, 128], BF16)
    cselA = psm("cselA", [128, 128], BF16)
    cselB = psm("cselB", [128, 128], BF16)
    ones8f = psm("ones8f", [8, 1], F32)
    ones18f = psm("ones18f", [1, 8], F32)
    sel64 = psm("sel64", [64, 32], F32)
    m32 = psm("m32", [32, 8], F32)
    g32 = psm("g32", [32, 1], F32)
    cb = psm("cb", [128, 24], F32)
    bqkv = {"q": psm("bq_s", [128, 8], F32), "k": psm("bk_s", [128, 8], F32),
            "v": psm("bv_s", [128, 8], F32)}
    kmask = psm("kmask", [128, 20], F32)
    rwt = psm("rwt", [128, 64], BF16)
    rwtf = psm("rwtf", [128, 64], F32)
    rb8 = psm("rb8", [8, 1], F32)
    tempv = psm("tempv", [1, 1], F32)
    rtemp = psm("rtemp", [1, 1], F32)
    rtemp8 = psm("rtemp8", [8, 1], F32)
    den = psm("den", [128, 128], F32)
    denr = psm("denr", [128, 128], BF16)
    wgt = psm("wgt", [128, 128], F32)
    wtile = psm("wtile", [128, 8], F32)
    wm1 = psm("wm1", [128, 1], F32)
    wm2 = psm("wm2", [128, 1], F32)
    wmsk1 = psm("wmsk1", [128, 8], F32)
    wmsk2 = psm("wmsk2", [128, 8], F32)
    wg2v = psm("wg2v", [128, 8], F32)
    wden = psm("wden", [128, 1], F32)
    flq = psm("flq", [1, 32], BF16)
    actb = psm("actb", [128, 4], F32)
    ltri = psm("ltri", [128, 128], BF16)
    ones128 = psm("ones128", [128, 1], BF16)
    ones1_128f = psm("ones1_128f", [1, 128], F32)
    ecap = psm("ecap", [128, 8], F32)
    fbf = psm("fbf", [128, 8], BF16)
    baserep = psm("baserep", [128, 8], F32)
    cntsb = psm("cntsb", [1, 8], F32)
    sidx = psm("sidx", [128, 8], F32)
    sinr = psm("sinr", [128, 8], F32)
    svld = psm("svld", [128, 8], F32)
    sbig = psm("sbig", [128, 8], F32)
    stmp = psm("stmp", [128, 8], F32)
    msk21 = psm("msk21", [128, 8], F32)
    s1f = psm("s1f", [128, 1], F32)
    s2f = psm("s2f", [128, 1], F32)
    s1i = psm("s1i", [128, 16], mybir.dt.uint32)
    s2i = psm("s2i", [128, 16], mybir.dt.uint32)
    tg1 = psm("tg1", [128, 16], F32)
    tg2 = psm("tg2", [128, 16], F32)
    fl1 = psm("fl1", [128, 2], F32)
    ts8 = psm("ts8", [8, 1], F32)
    ts8b = psm("ts8b", [8, 1], F32)
    lsef = psm("lsef", [8, 1], F32)
    assert MB + off[0] <= TOP, off[0]

    PS_SA = [ctx.enter_context(nc.psum_tensor(f"ps_sa{i}", [128, 1024], F32))
             for i in range(2)]
    PS_AV = [ctx.enter_context(nc.psum_tensor(f"ps_av{i}", [128, 512], F32))
             for i in range(2)]
    PS_PQ = ctx.enter_context(nc.psum_tensor("ps_pq", [128, 512], F32))
    PS_VT = ctx.enter_context(nc.psum_tensor("ps_vt", [128, 128], BF16))

    P = Plan(nc, ctx, ("sq", "sg", "scc", "pe", "act", "dve", "pool"))
    C = P.cnt

    def dma(qeng, dst, src, wait=()):
        sem = "sg" if qeng == "gpsimd" else "sq"
        return P.op(qeng, lambda h, d=dst, s=src: h.dma_start(out=d, in_=s),
                    wait=wait, inc=sem, k=16)

    def dmat(qeng, dst, src, wait=()):
        return P.op(qeng, lambda h, d=dst, s=src: h.dma_start_transpose(d, s),
                    wait=wait, inc="sq", k=16)

    def mm(dst, lhsT, rhs, start, stop, wait=(), inc=None):
        return P.op("tensor",
                    lambda h, d=dst, l=lhsT, r=rhs, a=start, b=stop:
                    h.matmul(d, l, r, start=a, stop=b), wait=wait, inc=inc)

    def tpose(dst, src, idm, wait=(), inc=None):
        return P.op("tensor", lambda h, d=dst, s=src, i=idm: h.transpose(d, s, i),
                    wait=wait, inc=inc)

    def act(out_, in_, func, wait=(), inc=None, **kw):
        return P.op("scalar",
                    lambda h, o=out_, i=in_, f=func, kw=dict(kw):
                    h.activation(o, i, f, **kw), wait=wait, inc=inc)

    def vop(name, *args, wait=(), inc=None, eng="vector"):
        return P.op(eng, lambda h, n=name, a=args: getattr(h, n)(*a),
                    wait=wait, inc=inc)

    def allcnt():
        return [(s, C[s]) for s in ("pe", "act", "dve", "pool")]

    for d_, s_ in ((cost, cost_in), (sint, sint_in), (ident, ident_in),
                   (identf, identf_in), (rot_t, rot_in), (tri0, tri0_in),
                   (tri4, tri4_in), (sel2, sel2_in), (selT2, selT2_in),
                   (cselA, cselA_in), (cselB, cselB_in), (ones8f, ones8f_in),
                   (ones18f, ones18f_in), (sel64, sel64_in), (m32, m32_in),
                   (g32, g32_in), (cb, cb_in), (bqkv["q"], bq_in),
                   (bqkv["k"], bk_in), (bqkv["v"], bv_in), (kmask, kmask_in),
                   (rwtf, rwt_in), (rb8, rb_in), (tempv, temp_in),
                   (actb, actb_in), (ltri, ltri_in), (ones128, ones128_in),
                   (ones1_128f, ones1_128f_in), (ecap, ecap_in)):
        dma("sync", d_, s_)
    const_ld = C["sq"]
    dma("gpsimd", wpr["q"], wq_in[0])
    dma("gpsimd", wpr["k"], wk_in[0])
    dma("gpsimd", wpr["v"], wv_in[0])
    dma("gpsimd", cdg[0], cdiag_in[0])
    pw_ld = {0: C["sg"]}
    vop("memset", qt_b[:, KV - 1:KV], 0.0)
    vop("memset", kt_b[:, KV - 1:KV], 0.0)
    pad_dve = vop("memset", vt_b[:, KV - 1:KV], 0.0, inc="dve")

    x_til = x_ext.rearrange("(a p) d -> a p d", p=128)
    dev_done = {}
    tr_done = {-2: 0, -1: 0}
    rs_prev = 0
    vt_slot = {0: 0, 1: 0}
    xnd_wr = [0]
    for a in range(NT):
        buf = xld[a % 2]
        ld = dma("sync", buf, x_til[a], wait=[("dve", dev_done.get(a - 2, 0))])
        r1_ = vop("reduce_sum", ms1, buf, AX.X, wait=[("sq", ld)], inc="dve")
        m1_ = vop("tensor_scalar", m_all[:, a:a + 1], ms1, 1.0 / D, None, A.mult,
                  wait=[("dve", r1_)], inc="dve")
        dd = vop("tensor_scalar", dev, buf, m_all[:, a:a + 1], None, A.subtract,
                 wait=[("act", rs_prev), ("dve", m1_)], inc="dve")
        dev_done[a] = dd
        act(sqd, dev, AF.Square, wait=[("dve", dd)], accum_out=ms2)
        act(ms3, ms2, AF.Ln, scale=1.0 / D, bias=actb[:, 0:1])
        act(rs_all[:, a:a + 1], ms3, AF.Exp, scale=-0.5)
        rs = act(fl1[:, 0:1], rs_all[:, a:a + 1], AF.Copy, inc="act")
        rs_prev = rs
        xb = vop("tensor_scalar", xnt[a % 2], dev, rs_all[:, a:a + 1], None, A.mult,
                 wait=[("act", rs), ("dve", tr_done[a - 2])], inc="dve")
        if a >= 5:
            xc = act(xnt_bf[:], xnt[a % 2], AF.Copy,
                     wait=[("sq", xnd_wr[0]), ("dve", xb)], inc="act")
            xnd_wr[0] = dma("sync", xn_dram.ap().rearrange(
                "(t p) d -> t p d", p=128)[a - 5], xnt_bf[:],
                wait=[("act", xc)])
        for dc in range(8):
            sl_ = (a * 8 + dc) % 2
            tp = tpose(PS_AV[sl_][:, 0:128],
                       xnt[a % 2][:, dc * 128:(dc + 1) * 128], identf,
                       wait=[("dve", xb), ("dve", vt_slot[sl_])], inc="pe")
            vt_slot[sl_] = vop(
                "tensor_copy", xn[:, dc * EXT + a * 128: dc * EXT + (a + 1) * 128],
                PS_AV[sl_][:, 0:128], wait=[("pe", tp)], inc="dve")
        tr_done[a] = C["dve"]
    ln_sq = C["sq"]
    ln_dve = C["dve"]

    if "xn" in DEBUG_TAPS:
        dma("sync", dbg["xn"], xn[:], wait=[("dve", ln_dve)])

    tm_ = vop("tensor_scalar", rtemp, tempv, 0.1, None, A.max,
              wait=[("sq", const_ld)], inc="dve")
    vop("reciprocal", rtemp, rtemp, wait=[("dve", tm_)])
    rt = vop("tensor_copy", fl1[0:1, 0:1], rtemp, inc="dve")
    rwc = vop("tensor_copy", rwt, rwtf, inc="dve")
    r8 = mm(PS_PQ[0:8, 0:1], ones18f, rtemp, True, True, wait=[("dve", rt)], inc="pe")
    rt8a = vop("tensor_copy", rtemp8, PS_PQ[0:8, 0:1], wait=[("pe", r8)], inc="dve")
    rt8 = vop("tensor_copy", fl1[0:8, 1:2], rtemp8, wait=[("dve", rt8a)], inc="dve")
    sc_done = 0
    for tc in range(4):
        for dc in range(8):
            w = []
            if dc == 0:
                w = [("dve", rt8 if tc == 0 else sc_done), ("dve", ln_dve),
                     ("dve", rwc)]
            mm(PS_PQ[0:8, :], rwt[:, dc * 8:(dc + 1) * 8],
               xn[:, dc * EXT + HALO + tc * 512: dc * EXT + HALO + (tc + 1) * 512],
               dc == 0, dc == 7, wait=w, inc="pe" if dc == 7 else None)
        sc_done = vop("tensor_scalar", sc_r[:, tc * 512:(tc + 1) * 512],
                      PS_PQ[0:8, :], rb8, rtemp8, A.add, A.mult,
                      wait=[("pe", C["pe"])], inc="dve")
    cc_wr = dma("sync", cc_in.ap(), sc_r[:], wait=[("dve", sc_done)])
    P.op("gpsimd", lambda h: h.collective_compute(
        "AllGather", A.bypass, replica_groups=[list(range(NCORES))],
        ins=[cc_in.ap().opt()], outs=[cc_out.ap().opt()]),
        wait=[("sq", cc_wr)], inc="scc")
    if "scores" in DEBUG_TAPS:
        dma("sync", dbg["scores"], sc_r[:], wait=[("sq", cc_wr)])

    projs = ("q", "k", "v")
    proj_ps = {"q": lambda n: PS_SA[0][:, 0:n], "k": lambda n: PS_SA[0][:, 512:512 + n],
               "v": lambda n: PS_SA[1][:, 0:n]}
    evac_done = {"q": 0, "k": 0, "v": 0}
    conv_done = {}
    convread = {"q": 0, "k": 0, "v": 0}
    vtok_state = {"evac": 0}

    def emit_vtok(kt_i, pool_cnt):
        tp = tpose(PS_VT[:], vt_b[:, kt_i * 128:(kt_i + 1) * 128], ident,
                   wait=[("act", pool_cnt), ("dve", vtok_state["evac"])], inc="pe")
        vop("tensor_copy", vtok[:, kt_i * 130: kt_i * 130 + 64], PS_VT[:, 0:64],
            wait=[("pe", tp)])
        vtok_state["evac"] = vop(
            "tensor_copy", vtok[:, kt_i * 130 + 65: kt_i * 130 + 129],
            PS_VT[:, 64:128], inc="dve")

    def emit_conv_rot(pair, cc, st):
        csz = CCH[cc]
        col0 = 512 * cc
        cds = cdg[pair % 2]
        for ip, pn in enumerate(projs):
            src = qh[pn][cc % 2]
            if pn == "v":
                dsl = vt_b[:, col0:col0 + csz]
            else:
                dsl = (convq if pn == "q" else convk)[:, 0:csz]
            cbb = cb[:, ip * 8 + pair: ip * 8 + pair + 1]
            ps = proj_ps[pn](csz)
            for tap in range(4):
                w = []
                if tap == 0:
                    w = [("act", evac_done[pn]), ("dve", convread[pn])]
                    if cc < 4:
                        w.append(("dve", st["tail"][pn]))
                mm(ps, cds[:, (ip * 4 + tap) * 128:(ip * 4 + tap + 1) * 128],
                   src[:, tap:tap + csz], tap == 0, tap == 3, wait=w,
                   inc="pe" if tap == 3 else None)
            evac_done[pn] = act(dsl, ps, AF.Identity, bias=cbb,
                                wait=[("pe", C["pe"])], inc="act")
        conv_done[(pair, cc)] = C["act"]
        pc = C["act"]
        for pn in ("q", "k"):
            cvb = convq if pn == "q" else convk
            rsb = rsin_q if pn == "q" else rsin_k
            rcb = rcos_q if pn == "q" else rcos_k
            rob = qrot if pn == "q" else krot
            qtb = qt_b if pn == "q" else kt_b
            rot_ps = PS_SA[1][:, 512:512 + csz] if pn == "q" else PS_AV[0][:, 0:csz]
            l2_ps = PS_AV[1] if pn == "q" else PS_PQ
            r1 = vop("tensor_tensor", rsb[:, 0:csz], cvb[:, 0:csz],
                     sint[:, col0:col0 + csz], A.mult,
                     wait=[("act", pc), ("pe", st.get("rotmm_" + pn, 0))],
                     inc="dve")
            r2 = vop("tensor_tensor", rcb[:, 0:csz], cvb[:, 0:csz],
                     cost[:, col0:col0 + csz], A.mult, inc="dve")
            convread[pn] = r2
            rm = mm(rot_ps, rot_t, rsb[:, 0:csz], True, True,
                    wait=[("dve", r2), ("dve", st.get("qrot_" + pn, 0))], inc="pe")
            st["rotmm_" + pn] = rm
            qr = vop("tensor_tensor", rob[:, 0:csz], rcb[:, 0:csz], rot_ps, A.add,
                     wait=[("pe", rm), ("act", st.get("sq_" + pn, 0))], inc="dve")
            st["qrot_" + pn] = qr
            sq_ = act(sqq[:, 0:csz], rob[:, 0:csz], AF.Square,
                      wait=[("dve", qr), ("pe", st.get("l2mm_" + pn, 0))], inc="act")
            st["sq_" + pn] = sq_
            sm = mm(l2_ps[0:2, 0:csz], sel2, sqq[:, 0:csz], True, True,
                    wait=[("act", sq_), ("dve", st.get("qt_" + pn, 0))], inc="pe")
            act(rr2[:, 0:csz], l2_ps[0:2, 0:csz], AF.Ln, bias=actb[0:2, 1:2],
                wait=[("pe", sm)])
            if pn == "q":
                l2e = act(rr2b[:, 0:csz], rr2[:, 0:csz], AF.Exp, scale=-0.5,
                          bias=actb[0:2, 2:3], inc="act")
            else:
                l2e = act(rr2b[:, 0:csz], rr2[:, 0:csz], AF.Exp, scale=-0.5,
                          inc="act")
            st["l2e_" + pn] = l2e
            rp = mm(l2_ps[:, 0:csz], selT2, rr2b[:, 0:csz], True, True,
                    wait=[("act", l2e)], inc="pe")
            st["l2mm_" + pn] = rp
            qt_ = vop("tensor_tensor", qtb[:, col0:col0 + csz], rob[:, 0:csz],
                      l2_ps[:, 0:csz], A.mult, wait=[("pe", rp)], inc="dve")
            st["qt_" + pn] = qt_
        t0 = (col0 + 127) // 128
        t1 = (col0 + csz) // 128
        for kt_i in range(t0, t1):
            emit_vtok(kt_i, pc)

    def qkv_phase(pair, barrier):
        st = {"tail": {pn: 0 for pn in projs}}
        for ci, csz in enumerate(PCH):
            base = 126 + 512 * ci
            for pn in projs:
                ps = proj_ps[pn](csz)
                for dc in range(8):
                    w = []
                    if dc == 0:
                        w = [("sg", pw_ld[pair]), ("act", evac_done[pn])]
                        if ci == 0:
                            w = w + barrier
                    mm(ps, wpr[pn][:, dc * 128:(dc + 1) * 128],
                       xn[:, dc * EXT + base: dc * EXT + base + csz],
                       dc == 0, dc == 7, wait=w, inc="pe" if dc == 7 else None)
                if ci < 5:
                    dst = qh[pn][ci % 2][:, 0:csz]
                else:
                    dst = qh[pn][0][:, 512:514]
                ww = [("pe", C["pe"])]
                if ci >= 2:
                    ww.append(("act", conv_done.get((pair, ci - 2), 0)))
                evac_done[pn] = act(dst, ps, AF.Identity,
                                    bias=bqkv[pn][:, pair:pair + 1],
                                    wait=ww, inc="act")
            if 1 <= ci <= 4:
                for pn in projs:
                    st["tail"][pn] = vop(
                        "tensor_copy", qh[pn][(ci - 1) % 2][:, 512:515],
                        qh[pn][ci % 2][:, 0:3],
                        wait=[("act", evac_done[pn])], inc="dve")
                emit_conv_rot(pair, ci - 1, st)
        emit_conv_rot(pair, 4, st)

    def attn_inner(pair, qkv_barrier):
        its = [(h, q) for h in range(2) for q in range(16)]
        n = len(its)
        sd, ed, td, avd, dd_, rpd, nmd = {}, {}, {}, {}, {}, {}, {}
        for k in range(-4, 0):
            ed[k] = td[k] = avd[k] = dd_[k] = rpd[k] = nmd[k] = 0

        def emit_S(i):
            head, qt = its[i]
            hb = 64 * head
            sa = PS_SA[i % 2]
            for m in range(5):
                kt_i = qt + m
                dst = sa[:, m * 128:(m + 1) * 128] if m < 4 else sa[:, 512:640]
                w = []
                if m == 0:
                    w = [("act", ed[i - 2])]
                    if i == 0:
                        w = w + qkv_barrier
                mm(dst, kt_b[hb:hb + 64, kt_i * 128:(kt_i + 1) * 128],
                   qt_b[hb:hb + 64, 511 + qt * 128: 511 + qt * 128 + 128],
                   True, True, wait=w, inc="pe" if m == 4 else None)
            sd[i] = C["pe"]

        def emit_exp(i):
            head, qt = its[i]
            sa = PS_SA[i % 2]
            if qt <= 3:
                for m in range(5):
                    kt_i = qt + m
                    src = sa[:, m * 128:(m + 1) * 128] if m < 4 else sa[:, 512:640]
                    ee = act(pt[i % 2][:, m * 128:(m + 1) * 128], src, AF.Exp,
                             bias=kmask[:, kt_i:kt_i + 1],
                             wait=[("pe", sd[i])] if m == 0 else [],
                             inc="act" if m in (0, 4) else None)
                    if m == 0:
                        e0 = ee
            else:
                e0 = act(pt[i % 2][:, 0:128], sa[:, 0:128], AF.Exp,
                         wait=[("pe", sd[i])], inc="act")
                act(pt[i % 2][:, 128:512], sa[:, 128:512], AF.Exp)
                ee = act(pt[i % 2][:, 512:640], sa[:, 512:640], AF.Exp, inc="act")
            ed[i] = ee
            vop("tensor_tensor", pt[i % 2][:, 0:128], pt[i % 2][:, 0:128], tri0,
                A.mult, wait=[("act", e0)])
            td[i] = vop("tensor_tensor", pt[i % 2][:, 512:640], pt[i % 2][:, 512:640],
                        tri4, A.mult, wait=[("act", ee)], inc="dve")

        def emit_AV(i):
            head, qt = its[i]
            av = PS_AV[i % 2]
            for j, m in enumerate((1, 2, 3, 0, 4)):
                kt_i = qt + m
                w = []
                if j == 0:
                    w = [("act", ed[i]), ("dve", nmd[i - 2])]
                if m == 0:
                    w = [("dve", td[i])]
                mm(av[0:65, 0:128],
                   vtok[:, kt_i * 130 + head * 65: kt_i * 130 + head * 65 + 65],
                   pt[i % 2][:, m * 128:(m + 1) * 128], j == 0, j == 4,
                   wait=w, inc="pe" if j == 4 else None)
            avd[i] = C["pe"]
            oc = pair * 2048 + qt * 128
            hb = 64 * head
            act(attnout[hb:hb + 64, oc:oc + 128], av[0:64, 0:128], AF.Copy,
                wait=[("pe", avd[i])])
            act(den[64:65, :], av[64:65, 0:128], AF.Ln)
            dd_[i] = act(denr[64:65, :], den[64:65, :], AF.Exp, scale=-1.0,
                         inc="act")

        def emit_norm(i):
            head, qt = its[i]
            av = PS_AV[i % 2]
            hb = 64 * head
            oc = pair * 2048 + qt * 128
            csel = cselA if head == 0 else cselB
            rp = mm(av[:, 128:256], csel[64:65, :], denr[64:65, :], True, True,
                    wait=[("act", dd_[i]), ("dve", nmd[i - 2])], inc="pe")
            rpd[i] = rp
            nmd[i] = vop("tensor_tensor", attnout[hb:hb + 64, oc:oc + 128],
                         attnout[hb:hb + 64, oc:oc + 128], av[hb:hb + 64, 128:256],
                         A.mult, wait=[("pe", rp)], inc="dve")

        for step in range(n):
            emit_S(step)
            emit_exp(step)
            if step >= 1:
                emit_AV(step - 1)
            if step >= 2:
                emit_norm(step - 2)
        emit_AV(n - 1)
        emit_norm(n - 2)
        emit_norm(n - 1)

    vtok_ones = vtok.rearrange("p (t c) -> p t c", c=130)
    woT_ld = 0
    for pair in range(8):
        barrier = allcnt() + [("dve", pad_dve), ("dve", ln_dve)]
        qkv_phase(pair, barrier)
        if pair + 1 < 8:
            dma("gpsimd", wpr["q"], wq_in[pair + 1], wait=[("pe", C["pe"])])
            dma("gpsimd", wpr["k"], wk_in[pair + 1])
            dma("gpsimd", wpr["v"], wv_in[pair + 1])
            dma("gpsimd", cdg[(pair + 1) % 2], cdiag_in[pair + 1])
            pw_ld[pair + 1] = C["sg"]
        vop("memset", vtok_ones[:, :, 64:65], 1.0, wait=[("act", C["act"])])
        vop("memset", vtok_ones[:, :, 129:130], 1.0, inc="dve")
        emit_vtok(19, C["act"])
        qkv_barrier = allcnt()
        if pair == 0:
            for tap, bufr in (("qt", qt_b), ("kt", kt_b), ("vt", vt_b)):
                if tap in DEBUG_TAPS:
                    dma("sync", dbg[tap], bufr[:], wait=qkv_barrier)
        attn_inner(pair, qkv_barrier)
        if pair == 7:
            dma("gpsimd", woT, wo_in, wait=[("pe", C["pe"])])
            woT_ld = C["sg"]
    attn_end = allcnt()
    if "attnout" in DEBUG_TAPS:
        dma("sync", dbg["attnout"], attnout[:], wait=attn_end)

    wo_state = {"i": 0, "evac": {0: 0, 1: 0}}

    def next_wo(n, limit):
        for _ in range(n):
            if wo_state["i"] >= min(32, limit):
                return
            tt, oh = divmod(wo_state["i"], 2)
            wo_state["i"] += 1
            i2 = (tt * 2 + oh) % 2
            ps = PS_AV[i2][:]
            for cc in range(8):
                w = []
                if cc == 0:
                    w = [("dve", wo_state["evac"][i2]), ("sg", woT_ld)]
                    if tt == 0 and oh == 0:
                        w = w + attn_end
                mm(ps, attnout[:, cc * 2048 + tt * 128: cc * 2048 + tt * 128 + 128],
                   woT[:, cc * 1024 + oh * 512: cc * 1024 + oh * 512 + 512],
                   cc == 0, cc == 7, wait=w, inc="pe" if cc == 7 else None)
            wo_state["evac"][i2] = vop(
                "tensor_copy",
                acc[:, tt * 1024 + oh * 512: tt * 1024 + oh * 512 + 512],
                ps, wait=[("pe", C["pe"])], inc="dve")

    ld = dma("sync", raw64[:], cc_out.ap(), wait=[("scc", 1)] + attn_end)
    for cl in range(4):
        for hh in range(4):
            mm(PS_PQ[0:8, :], sel64[:, cl * 8:(cl + 1) * 8],
               raw64[:, hh * 512:(hh + 1) * 512], True, True,
               wait=[("sq", ld), ("dve", C["dve"])], inc="pe")
            vop("tensor_copy", la[:, cl * 2048 + hh * 512: cl * 2048 + (hh + 1) * 512],
                PS_PQ[0:8, :], wait=[("pe", C["pe"])], inc="dve")
    next_wo(2, 18)

    def sink_iter(feed=False):
        for ch in range(8):
            if feed:
                next_wo(1, 18)
            sl = la[:, ch * 1024:(ch + 1) * 1024]
            e1 = act(exs[:], sl, AF.Exp,
                     wait=[("dve", C["dve"]), ("pe", C["pe"])], inc="act")
            mm(PS_PQ[0:1, 0:512], ones8f, exs[:, 0:512], True, True,
               wait=[("act", e1)])
            mm(PS_SA[1][0:1, 512:1024], ones8f, exs[:, 512:1024], True, True,
               inc="pe")
            act(lsee[:, 0:512], PS_PQ[0:1, 0:512], AF.Ln, wait=[("pe", C["pe"])])
            l1 = act(lsee[:, 512:1024], PS_SA[1][0:1, 512:1024], AF.Ln, inc="act")
            for hh in range(2):
                mm(PS_SA[0][0:8, hh * 512:(hh + 1) * 512], ones18f,
                   lsee[:, hh * 512:(hh + 1) * 512], True, True,
                   wait=[("act", l1), ("dve", C["dve"])] if hh == 0 else [],
                   inc="pe" if hh == 1 else None)
            vop("tensor_tensor", sl, sl, PS_SA[0][0:8, 0:1024], A.subtract,
                wait=[("pe", C["pe"])], inc="dve")
        zz = vop("memset", ts8, 0.0, inc="dve")
        for ch in range(8):
            sl = la[:, ch * 1024:(ch + 1) * 1024]
            e2 = act(exs[:], sl, AF.Exp,
                     wait=[("dve", C["dve"]), ("pe", C["pe"])], inc="act")
            rr_ = vop("reduce_sum", ts8b, exs[:], AX.X, wait=[("act", e2)],
                      inc="dve")
            zz = vop("tensor_tensor", ts8, ts8, ts8b, A.add,
                     wait=[("dve", rr_)], inc="dve")
        fz = vop("tensor_copy", fl1[0:8, 0:1], ts8, wait=[("dve", zz)], inc="dve")
        act(lsef, ts8, AF.Ln, wait=[("dve", fz)])
        lt = act(fl1[0:8, 1:2], lsef, AF.Copy, inc="act")
        for ch in range(8):
            vop("tensor_scalar", la[:, ch * 1024:(ch + 1) * 1024],
                la[:, ch * 1024:(ch + 1) * 1024], lsef, None, A.subtract,
                wait=[("act", lt)] if ch == 0 else [], inc="dve")

    for it_ in range(3):
        sink_iter(feed=it_ < 2)
    if "la" in DEBUG_TAPS:
        dma("sync", dbg["la"], la[:], wait=[("dve", C["dve"])])
    skw = dma("sync", sk_scr.ap(), la[:], wait=[("dve", C["dve"])])
    for cl in range(4):
        skr = dma("sync", raw64[cl * 8:(cl + 1) * 8, :],
                  sk_scr.ap()[:, cl * T:(cl + 1) * T], wait=[("sq", skw)])
    ex = act(ex32[:], raw64[0:32, :], AF.Exp, wait=[("sq", skr)], inc="act")
    for g2 in range(2):
        for hh in range(2):
            mm(PS_SA[0][0:8, hh * 512:(hh + 1) * 512], m32,
               ex32[:, g2 * 1024 + hh * 512: g2 * 1024 + (hh + 1) * 512],
               True, True,
               wait=[("act", ex), ("dve", C["dve"])] if hh == 0 else [],
               inc="pe" if hh == 1 else None)
        vop("tensor_copy", gn_own[0:8, g2 * 1024:(g2 + 1) * 1024],
            PS_SA[0][0:8, 0:1024], wait=[("pe", C["pe"])], inc="dve")
    for hh in range(4):
        mm(PS_PQ[0:1, 0:512], g32, ex32[:, hh * 512:(hh + 1) * 512], True, True,
           wait=[("dve", C["dve"])], inc="pe")
        vop("tensor_scalar", dn_own[:, hh * 512:(hh + 1) * 512], PS_PQ[0:1, 0:512],
            1e-8, None, A.add, wait=[("pe", C["pe"])], inc="dve")
    rcp = vop("reciprocal", dn_own[:], dn_own[:], inc="dve")
    for g2 in range(2):
        for hh in range(2):
            mm(PS_SA[1][0:8, hh * 512:(hh + 1) * 512], ones18f,
               dn_own[:, g2 * 1024 + hh * 512: g2 * 1024 + (hh + 1) * 512],
               True, True,
               wait=[("dve", rcp if g2 == 0 else C["dve"])] if hh == 0 else [],
               inc="pe" if hh == 1 else None)
        vop("tensor_tensor", gn_own[0:8, g2 * 1024:(g2 + 1) * 1024],
            gn_own[0:8, g2 * 1024:(g2 + 1) * 1024], PS_SA[1][0:8, 0:1024],
            A.mult, wait=[("pe", C["pe"])], inc="dve")
    if "gates" in DEBUG_TAPS:
        dma("sync", dbg["gates"], gn_own[:], wait=[("dve", C["dve"])])
    vop("memset", baserep, 0.0, inc="dve")
    for a in range(16):
        tp = tpose(PS_PQ[:, 0:128], gn_own[:, a * 128:(a + 1) * 128], identf,
                   wait=[("dve", C["dve"])], inc="pe")
        q_ = vop("tensor_copy", wtile, PS_PQ[:, 0:8], wait=[("pe", tp)], inc="dve")
        q_ = vop("reduce_max", wm1, wtile, AX.X, wait=[("dve", q_)], inc="dve")
        q_ = vop("tensor_scalar", wmsk1, wtile, wm1, None, A.is_equal,
                 wait=[("dve", q_)], inc="dve")
        q_ = vop("scalar_tensor_tensor", wg2v, wmsk1, NEG, wtile, A.mult, A.add,
                 wait=[("dve", q_)], inc="dve")
        q_ = vop("reduce_max", wm2, wg2v, AX.X, wait=[("dve", q_)], inc="dve")
        q_ = vop("tensor_scalar", wmsk2, wtile, wm2, None, A.is_ge,
                 wait=[("dve", q_)], inc="dve")
        q_ = vop("tensor_tensor", wden, wm1, wm2, A.add, wait=[("dve", q_)],
                 inc="dve")
        q_ = vop("tensor_scalar", wden, wden, 1e-8, None, A.add,
                 wait=[("dve", q_)], inc="dve")
        q_ = vop("reciprocal", wden, wden, wait=[("dve", q_)], inc="dve")
        q_ = vop("tensor_scalar", wtile, wtile, wden, None, A.mult,
                 wait=[("dve", q_)], inc="dve")
        q_ = vop("tensor_tensor", wgt[:, a * 8:(a + 1) * 8], wtile, wmsk2, A.mult,
                 wait=[("dve", q_)], inc="dve")
        q_ = vop("tensor_copy", fbf, wmsk2, wait=[("dve", q_)], inc="dve")
        pm_ = mm(PS_PQ[:, 128:136], ltri, fbf, True, True,
                 wait=[("dve", q_)], inc="pe")
        cm_ = mm(PS_PQ[0:1, 136:144], ones128, fbf, True, True, inc="pe")
        q_ = vop("tensor_copy", cntsb, PS_PQ[0:1, 136:144],
                 wait=[("pe", cm_)], inc="dve")
        rm_ = mm(PS_PQ[:, 144:152], ones1_128f, cntsb, True, True,
                 wait=[("dve", q_)], inc="pe")
        q_ = vop("tensor_tensor", sidx, PS_PQ[:, 128:136], baserep, A.add,
                 wait=[("pe", rm_)], inc="dve")
        q_ = vop("tensor_tensor", baserep, baserep, PS_PQ[:, 144:152], A.add,
                 wait=[("dve", q_)], inc="dve")
        q_ = vop("tensor_scalar", sinr, sidx, float(CAP), None, A.is_lt,
                 wait=[("dve", q_)], inc="dve")
        q_ = vop("tensor_tensor", svld, sinr, wmsk2, A.mult,
                 wait=[("dve", q_)], inc="dve")
        q_ = vop("tensor_tensor", stmp, sidx, ecap, A.add,
                 wait=[("dve", q_)], inc="dve")
        q_ = vop("tensor_scalar", sbig, svld, -BIGIDX, BIGIDX, A.mult, A.add,
                 wait=[("dve", q_)], inc="dve")
        q_ = vop("tensor_tensor", stmp, stmp, svld, A.mult,
                 wait=[("dve", q_)], inc="dve")
        q_ = vop("tensor_tensor", stmp, stmp, sbig, A.add,
                 wait=[("dve", q_)], inc="dve")
        q_ = vop("tensor_tensor", msk21, wmsk2, wmsk1, A.subtract,
                 wait=[("dve", q_)], inc="dve")
        q_ = vop("tensor_tensor", sbig, stmp, wmsk1, A.mult,
                 wait=[("dve", q_)], inc="dve")
        q_ = vop("reduce_sum", s1f, sbig, AX.X, wait=[("dve", q_)], inc="dve")
        q_ = vop("tensor_copy", s1i[:, a:a + 1], s1f, wait=[("dve", q_)], inc="dve")
        q_ = vop("tensor_tensor", sbig, stmp, msk21, A.mult,
                 wait=[("dve", q_)], inc="dve")
        q_ = vop("reduce_sum", s2f, sbig, AX.X, wait=[("dve", q_)], inc="dve")
        q_ = vop("tensor_copy", s2i[:, a:a + 1], s2f, wait=[("dve", q_)], inc="dve")
        q_ = vop("tensor_tensor", sbig, wgt[:, a * 8:(a + 1) * 8], wmsk1, A.mult,
                 wait=[("dve", q_)], inc="dve")
        q_ = vop("reduce_sum", tg1[:, a:a + 1], sbig, AX.X,
                 wait=[("dve", q_)], inc="dve")
        q_ = vop("tensor_tensor", sbig, wgt[:, a * 8:(a + 1) * 8], msk21, A.mult,
                 wait=[("dve", q_)], inc="dve")
        q_ = vop("reduce_sum", tg2[:, a:a + 1], sbig, AX.X,
                 wait=[("dve", q_)], inc="dve")
    gates_done = C["dve"]
    next_wo(32, 32)
    BCREG = {}
    P.op("gpsimd", lambda h: BCREG.__setitem__("r", h.to_reg(SLOTS - 1)))
    scat = {-2: 0, -1: 0}
    xn_til = xn_dram.ap().rearrange("(t p) d -> t p d", p=128)
    gath_ap = gath.ap()
    for tt in range(16):
        ld = dma("sync", xsc[tt % 2], xn_til[tt],
                 wait=[("sg", scat[tt - 2]), ("sq", xnd_wr[0])])
        P.op("gpsimd", lambda h, tt=tt: h.indirect_dma_start(
            out=gath_ap, out_offset=bass.IndirectOffsetOnAxis(
                ap=s1i[:, tt:tt + 1], axis=0),
            in_=xsc[tt % 2][:], in_offset=None, bounds_check=BCREG["r"],
            oob_is_err=False),
            wait=[("sq", ld), ("dve", gates_done)], inc="sg", k=16)
        scat[tt] = P.op("gpsimd", lambda h, tt=tt: h.indirect_dma_start(
            out=gath_ap, out_offset=bass.IndirectOffsetOnAxis(
                ap=s2i[:, tt:tt + 1], axis=0),
            in_=xsc[tt % 2][:], in_offset=None, bounds_check=BCREG["r"],
            oob_is_err=False), inc="sg", k=16)
    scat_end = C["sg"]
    if "sl1" in DEBUG_TAPS:
        dma("gpsimd", dbg["sl1"], s1i[:], wait=[("dve", gates_done)])
        dma("gpsimd", dbg["sl2"], s2i[:], wait=[("dve", gates_done)])
        dma("sync", dbg["tgs"][:, 0:16], tg1[:], wait=[("dve", gates_done)])
        dma("sync", dbg["tgs"][:, 16:32], tg2[:], wait=[("dve", gates_done)])
    if "wgt" in DEBUG_TAPS:
        dma("sync", dbg["wgt"], wgt[:], wait=[("dve", gates_done)])

    accwo_tap = 0
    if "accwo" in DEBUG_TAPS:
        accwo_tap = dma("sync", dbg["accwo"], acc[:], wait=[("dve", C["dve"])])

    ffn_barrier = allcnt()
    oslots_til = oslots.ap().rearrange("(t p) d -> t p d", p=128)
    gath_til = gath.ap().rearrange("(t p) d -> t p d", p=128)
    dma("gpsimd", wgb[0], wg_s[0, 0], wait=[("pe", C["pe"])])
    dma("gpsimd", wpb[0], wp_s[0, 0])
    wg_ld = {0: C["sg"]}
    dma("gpsimd", wob[:, 0:8192], wo_s[0, 0], wait=[("pe", C["pe"])])
    dma("gpsimd", wob[:, 8192:16384], wo_s[0, 1])
    wo_ld2 = {0: C["sg"]}
    wo_pe = {}
    for st in range(6):
        dma("sync", xg[:, st * 1024:(st + 1) * 1024], gath_til[st],
            wait=[("sg", scat_end)] if st == 0 else [])
    xg_ld = {0: C["sq"]}
    if "xg0" in DEBUG_TAPS:
        dma("sync", dbg["xg0"], xg[:], wait=[("sg", C["sg"])])
    gp_evac = {}
    hmul_done = {}
    wo_evac2 = {0: 0, 1: 0}
    osb_wr = {-2: 0, -1: 0}
    TCH = [(0, 512), (512, 256)]

    tp_evac = {"act": 0}

    def emit_tp(e, idx):
        st, dc = idx // 8, idx % 8
        fme = xg_fm[e % 2]
        tpv = tpose(PS_VT[:], xg[:, st * 1024 + dc * 128:
                                 st * 1024 + dc * 128 + 128], ident,
                    wait=[("sq", xg_ld[e]), ("act", tp_evac["act"])], inc="pe")
        tp_evac["act"] = act(
            fme[:, dc * CAP + st * 128: dc * CAP + st * 128 + 128],
            PS_VT[:], AF.Copy, wait=[("pe", tpv)], inc="act")

    for idx in range(48):
        emit_tp(0, idx)
    tp_done = {0: C["act"]}
    for e in range(E):
        tp_i = [0]
        if e + 1 < E:
            for st in range(6):
                dma("sync", xg[:, st * 1024:(st + 1) * 1024],
                    gath_til[(e + 1) * 6 + st],
                    wait=[("pe", C["pe"])] if st == 0 else [])
            xg_ld[e + 1] = C["sq"]
        fme = xg_fm[e % 2]
        for fc in range(16):
            cur = e * 16 + fc
            nxt = cur + 1
            if nxt < 128 and nxt not in wg_ld:
                dma("gpsimd", wgb[nxt % 2], wg_s[nxt // 16, nxt % 16],
                    wait=[("pe", C["pe"])])
                dma("gpsimd", wpb[nxt % 2], wp_s[nxt // 16, nxt % 16])
                wg_ld[nxt] = C["sg"]
            if e >= 1 and fc == 2:
                dma("gpsimd", wob[:, 0:8192], wo_s[e, 0],
                    wait=[("pe", wo_pe[e - 1])])
            if e >= 1 and fc == 6:
                dma("gpsimd", wob[:, 8192:16384], wo_s[e, 1])
                wo_ld2[e] = C["sg"]
            for ti, (toff, tn) in enumerate(TCH):
                hslice = hbuf[:, fc * CAP + toff: fc * CAP + toff + tn]
                qi = (fc * 2 + ti) % 4
                g_ps = (PS_SA[0][:, 0:512], PS_SA[0][:, 512:1024],
                        PS_AV[0][:], PS_AV[1][:])[qi]
                p_ps = (PS_SA[1][:, 0:512], PS_SA[1][:, 512:1024],
                        PS_PQ[:], PS_SA[1][:, 0:512])[qi % 3]
                for which, wbuf, ps_, ebuf, fn, slot in (
                    ("g", wgb[cur % 2], g_ps, gbuf, AF.Relu, qi),
                    ("p", wpb[cur % 2], p_ps, pbuf, AF.Copy, qi % 3),
                ):
                    ps = ps_[:, 0:tn] if ps_.shape[1] > tn else ps_
                    for dc in range(8):
                        w = []
                        if dc == 0:
                            w = [("sg", wg_ld[cur]),
                                 ("act", gp_evac.get((which, slot), 0)),
                                 ("act", tp_done[e])]
                            if which == "g" and qi >= 2:
                                w.append(("dve", wo_evac2[qi - 2]))
                            if cur == 0 and ti == 0 and which == "g":
                                w = w + ffn_barrier
                        mm(ps, wbuf[:, dc * 128:(dc + 1) * 128],
                           fme[:, dc * CAP + toff: dc * CAP + toff + tn],
                           dc == 0, dc == 7, wait=w,
                           inc="pe" if dc == 7 else None)
                    gp_evac[(which, slot)] = act(
                        ebuf[ti][:, 0:tn], ps, fn,
                        wait=[("pe", C["pe"]), ("dve", hmul_done.get(ti, 0))],
                        inc="act")
                hmul_done[ti] = vop(
                    "tensor_tensor", hslice, gbuf[ti][:, 0:tn], pbuf[ti][:, 0:tn],
                    A.mult, wait=[("act", gp_evac[("g", qi)]),
                                  ("act", gp_evac[("p", qi % 3)])], inc="dve")

        hm = C["dve"]
        for st in range(6):
            for oh in range(2):
                i2 = (st * 2 + oh) % 2
                ps = PS_AV[i2][:]
                for fc in range(16):
                    w = []
                    if fc == 0:
                        w = [("sg", wo_ld2[e]), ("dve", wo_evac2[i2])]
                        if st == 0 and oh == 0:
                            w.append(("dve", hm))
                    mm(ps, hbuf[:, fc * CAP + st * 128: fc * CAP + st * 128 + 128],
                       wob[:, fc * 1024 + oh * 512: fc * 1024 + oh * 512 + 512],
                       fc == 0, fc == 15, wait=w, inc="pe" if fc == 15 else None)
                wo_evac2[i2] = vop(
                    "tensor_copy", osb[st % 2][:, oh * 512: oh * 512 + 512], ps,
                    wait=[("pe", C["pe"]), ("sq", osb_wr[st - 2])], inc="dve")
                if e + 1 < E:
                    for _ in range(4):
                        if tp_i[0] < 48:
                            emit_tp(e + 1, tp_i[0])
                            tp_i[0] += 1
            osb_wr[st] = dma("sync", oslots_til[e * 6 + st], osb[st % 2][:],
                             wait=[("dve", wo_evac2[(st * 2 + 1) % 2])])
        wo_pe[e] = C["pe"]
        if e + 1 < E:
            while tp_i[0] < 48:
                emit_tp(e + 1, tp_i[0])
                tp_i[0] += 1
            tp_done[e + 1] = C["act"]
        osb_wr[-2], osb_wr[-1] = osb_wr[4], osb_wr[5]

    ffn_done = C["dve"]
    ffn_sq = C["sq"]

    if "osl" in DEBUG_TAPS:
        dma("sync", dbg["osl"], oslots.ap(), wait=[("sq", ffn_sq), ("dve", ffn_done)])
        dma("sync", dbg["gat"], gath.ap(), wait=[("sq", ffn_sq)])
    out_til = out.rearrange("(a p) d -> a p d", p=128)
    bo_ld = dma("sync", bo_s, bo_in, wait=[("dve", ffn_done)])
    xf_user = {0: 0, 1: 0}
    gsl_done = {-1: 0}
    for tt in range(16):
        a = 5 + tt
        wl = [("dve", xf_user[tt % 2])]
        if tt < 2:
            wl.append(("dve", ffn_done))
        ld = dma("sync", xf[tt % 2], x_til[a], wait=wl)
        z1 = vop("memset", gsl[0][:], 0.0, wait=[("dve", gsl_done[tt - 1])],
                 inc="dve")
        g1 = P.op("gpsimd", lambda h, tt=tt: h.indirect_dma_start(
            out=gsl[0][:], out_offset=None, in_=oslots.ap(),
            in_offset=bass.IndirectOffsetOnAxis(ap=s1i[:, tt:tt + 1], axis=0),
            bounds_check=BCREG["r"], oob_is_err=False),
            wait=[("dve", z1), ("sq", ffn_sq)], inc="sg", k=16)
        z2 = vop("memset", gsl[1][:], 0.0, inc="dve")
        g2 = P.op("gpsimd", lambda h, tt=tt: h.indirect_dma_start(
            out=gsl[1][:], out_offset=None, in_=oslots.ap(),
            in_offset=bass.IndirectOffsetOnAxis(ap=s2i[:, tt:tt + 1], axis=0),
            bounds_check=BCREG["r"], oob_is_err=False),
            wait=[("dve", z2)], inc="sg", k=16)
        g2 = dma("gpsimd", flq, oslots.ap()[0:1, 0:32])
        g1 = g2
        if tt == 0 and "gs0" in DEBUG_TAPS:
            dma("sync", dbg["gs0"][:, 0:1024], gsl[0][:], wait=[("sg", g1)])
            gtap = dma("sync", dbg["gs0"][:, 1024:2048], gsl[1][:],
                       wait=[("sg", g2)])
            P.op("vector", lambda h: h.memset(wden, 0.0), wait=[("sq", gtap)])
        w = [("sq", ld)]
        if tt == 0:
            w.append(("dve", ffn_done))
        sl = acc[:, tt * 1024:(tt + 1) * 1024]
        vop("tensor_tensor", sl, sl, xf[tt % 2], A.add, wait=w)
        vop("tensor_tensor", sl, sl, bo_s, A.add,
            wait=[("sq", bo_ld)] if tt == 0 else ())
        vop("tensor_scalar", t1f, xf[tt % 2], m_all[:, a:a + 1],
            rs_all[:, a:a + 1], A.subtract, A.mult)
        vop("tensor_tensor", sl, sl, t1f, A.add)
        vop("tensor_scalar", t1f, gsl[0][:], tg1[:, tt:tt + 1], None, A.mult,
            wait=[("sg", g1)])
        vop("tensor_tensor", sl, sl, t1f, A.add)
        vop("tensor_scalar", t1f, gsl[1][:], tg2[:, tt:tt + 1], None, A.mult,
            wait=[("sg", g2)])
        gsl_done[tt] = xf_user[tt % 2] = vop(
            "tensor_tensor", sl, sl, t1f, A.add, inc="dve")
        dma("sync", out_til[tt], sl, wait=[("dve", xf_user[tt % 2])])
    final_sq = C["sq"]
    P.op("sync", lambda h: h.nop(), wait=[("sq", final_sq)])

    ctx.enter_context(nc.allow_low_precision(reason="bf16 matmul operands by design"))
    with nc.Block() as block:
        @block.sync
        def _(eng):
            for f in P.q["sync"]:
                f(eng)

        @block.gpsimd
        def _(eng):
            for f in P.q["gpsimd"]:
                f(eng)

        @block.vector
        def _(eng):
            for f in P.q["vector"]:
                f(eng)

        @block.scalar
        def _(eng):
            for f in P.q["scalar"]:
                f(eng)

        @block.tensor
        def _(eng):
            for f in P.q["tensor"]:
                f(eng)

    ctx.close()
    return nc



def _rotary_tables(t0):
    d = np.arange(64)
    inv = 1.0 / (10000.0 ** ((d % 32) * 2.0 / 64.0))
    pos = t0 - 511 + np.arange(KV)
    pos = np.where(pos < 0, 0, pos).astype(np.float64)
    ang = pos[None, :] * inv[:, None]
    cos = np.cos(ang).astype(np.float32)
    sin = np.sin(ang).astype(np.float32)
    return (np.ascontiguousarray(np.concatenate([cos, cos], axis=0)),
            np.ascontiguousarray(np.concatenate([sin, sin], axis=0)))


def _consts():
    c = {}
    c["identity"] = np.eye(128, dtype=np.float32).astype(ml_dtypes.bfloat16)
    c["identityf"] = np.eye(128, dtype=np.float32)
    R = np.zeros((128, 128), dtype=np.float32)
    for do in range(128):
        h64, dd = (do // 64) * 64, do % 64
        if dd < 32:
            R[do, h64 + dd + 32] = -1.0
        else:
            R[do, h64 + dd - 32] = 1.0
    c["rot_t"] = np.ascontiguousarray(R.T).astype(ml_dtypes.bfloat16)
    kp = np.arange(128)[:, None]
    qf = np.arange(128)[None, :]
    c["tri0"] = (kp >= qf).astype(ml_dtypes.bfloat16)
    c["tri4"] = (kp < qf).astype(ml_dtypes.bfloat16)
    sel2 = np.zeros((128, 2), dtype=np.float32)
    sel2[:64, 0] = 1.0
    sel2[64:, 1] = 1.0
    c["sel2"] = sel2.astype(ml_dtypes.bfloat16)
    c["selT2"] = np.ascontiguousarray(sel2.T).astype(ml_dtypes.bfloat16)
    csA = np.zeros((128, 128), dtype=np.float32)
    csA[64, :64] = 1.0
    c["colselA"] = csA.astype(ml_dtypes.bfloat16)
    csB = np.zeros((128, 128), dtype=np.float32)
    csB[64, 64:] = 1.0
    c["colselB"] = csB.astype(ml_dtypes.bfloat16)
    c["ones8f"] = np.ones((8, 1), dtype=np.float32)
    c["ones1_8f"] = np.ones((1, 8), dtype=np.float32)
    kk = np.arange(128)[:, None]
    mm_ = np.arange(128)[None, :]
    c["ltri"] = (kk < mm_).astype(ml_dtypes.bfloat16)
    c["ones128"] = np.ones((128, 1), dtype=ml_dtypes.bfloat16)
    c["ones1_128f"] = np.ones((1, 128), dtype=np.float32)
    c["ecap"] = np.ascontiguousarray(
        np.tile((np.arange(8) * CAP).astype(np.float32)[None, :], (128, 1)))
    ab = np.zeros((128, 4), dtype=np.float32)
    ab[:, 0] = 1e-5
    ab[:, 1] = 1e-12
    ab[:, 2] = np.log(0.125)
    c["actbias"] = ab
    return c


def _prep_weights(inp):
    w = {}
    for nm, key in (("wq", "L0_attn_Wq"), ("wk", "L0_attn_Wk"), ("wv", "L0_attn_Wv")):
        W = np.asarray(inp[key], dtype=np.float32)
        WT = np.ascontiguousarray(W.T)
        w[nm] = np.ascontiguousarray(
            WT.reshape(8, 128, 8, 128).transpose(2, 1, 0, 3)).reshape(8, 128, 1024)
    WoT = np.ascontiguousarray(np.asarray(inp["L0_attn_Wo"], dtype=np.float32).T)
    w["wo"] = np.ascontiguousarray(
        WoT.reshape(8, 128, 8, 128).transpose(1, 0, 2, 3)).reshape(128, 8192)
    for nm, key in (("bq", "L0_attn_bq"), ("bk", "L0_attn_bk"), ("bv", "L0_attn_bv")):
        w[nm] = np.ascontiguousarray(
            np.asarray(inp[key], dtype=np.float32).reshape(8, 128).T)
    w["bo_rep"] = np.ascontiguousarray(np.tile(
        np.asarray(inp["L0_attn_bo"], dtype=np.float32)[None, :], (128, 1)))
    cwl, cbl = [], []
    for kw_, kb_ in (("L0_attn_cqw", "L0_attn_cqb"), ("L0_attn_ckw", "L0_attn_ckb"),
                     ("L0_attn_cvw", "L0_attn_cvb")):
        cwl.append(np.asarray(inp[kw_], dtype=np.float32)
                   .reshape(8, 128, 4).transpose(1, 0, 2).reshape(128, 32))
        cbl.append(np.asarray(inp[kb_], dtype=np.float32).reshape(8, 128).T)
    w["cw"] = np.ascontiguousarray(np.concatenate(cwl, axis=1))
    w["cb"] = np.ascontiguousarray(np.concatenate(cbl, axis=1))
    cd = np.zeros((8, 128, 1536), dtype=np.float32)
    p_ = np.arange(128)
    for ip, kw_ in enumerate(("L0_attn_cqw", "L0_attn_ckw", "L0_attn_cvw")):
        wfull = np.asarray(inp[kw_], dtype=np.float32)
        for pair in range(8):
            for tap in range(4):
                cd[pair, p_, (ip * 4 + tap) * 128 + p_] = wfull[pair * 128 + p_, tap]
    w["cdiag"] = cd
    rw = np.asarray(inp["L0_ffn_router_w"], dtype=np.float32)
    w["router_wt"] = np.ascontiguousarray(
        rw.T.reshape(8, 128, 8).transpose(1, 0, 2)).reshape(128, 64)
    w["router_b"] = np.ascontiguousarray(
        np.asarray(inp["L0_ffn_router_b"], dtype=np.float32).reshape(8, 1))
    w["temp"] = np.ascontiguousarray(
        np.asarray(inp["L0_ffn_temperature"], dtype=np.float32).reshape(1, 1))
    wg = np.asarray(inp["L0_ffn_wg"], dtype=np.float32)
    wp = np.asarray(inp["L0_ffn_wp"], dtype=np.float32)
    wo = np.asarray(inp["L0_ffn_wo"], dtype=np.float32)
    w["wg_s"] = np.ascontiguousarray(
        wg.reshape(8, 8, 128, 16, 128).transpose(0, 3, 2, 1, 4)).reshape(8, 16, 128, 1024)
    w["wp_s"] = np.ascontiguousarray(
        wp.reshape(8, 8, 128, 16, 128).transpose(0, 3, 2, 1, 4)).reshape(8, 16, 128, 1024)
    w["wo_s"] = np.ascontiguousarray(
        wo.reshape(8, 2, 8, 128, 1024).transpose(0, 1, 3, 2, 4)).reshape(8, 2, 128, 8192)
    return w


_BUILT = {}


def kernel(**inputs):
    x = np.asarray(inputs["L0_attn_x"], dtype=np.float32)
    w = _prep_weights(inputs)
    cst = _consts()

    if "nc" not in _BUILT:
        _BUILT["nc"] = build_program()
    nc = _BUILT["nc"]

    shared = dict(cst)
    for k in ("wq", "wk", "wv", "wo", "bq", "bk", "bv", "bo_rep", "cw", "cb",
              "cdiag", "router_wt", "router_b", "temp", "wg_s", "wp_s", "wo_s"):
        shared[k] = w[k]

    rot_cache = {}
    in_maps = []
    for core in range(NCORES):
        b, s = core // 4, core % 4
        t0 = s * T
        m = dict(shared)
        xe = np.zeros((EXT, D), dtype=np.float32)
        lo = t0 - HALO
        src_lo = max(lo, 0)
        xe[src_lo - lo:, :] = x[b, src_lo: t0 + T, :]
        m["x_ext"] = xe
        if t0 not in rot_cache:
            rot_cache[t0] = _rotary_tables(t0)
        m["cost"], m["sint"] = rot_cache[t0]
        km = np.zeros((128, 20), dtype=np.float32)
        if s == 0:
            kj = np.arange(2560).reshape(20, 128).T
            km = np.where(kj < 511, NEG, 0.0).astype(np.float32)
        km[127, 19] = NEG
        m["kmask"] = np.ascontiguousarray(km)
        sel = np.zeros((64, 32), dtype=np.float32)
        for cl in range(4):
            for e in range(8):
                sel[(4 * b + cl) * 8 + e, cl * 8 + e] = 1.0
        m["sel64"] = sel
        m32_ = np.zeros((32, 8), dtype=np.float32)
        g32_ = np.zeros((32, 1), dtype=np.float32)
        for e in range(8):
            m32_[s * 8 + e, e] = 1.0
            g32_[s * 8 + e, 0] = 1.0
        m["m32"] = m32_
        m["g32"] = g32_
        in_maps.append(m)

    res = run_bass_kernel_spmd(nc, in_maps, core_ids=list(range(NCORES)))
    _BUILT["res"] = res
    outp = np.empty((B, S, D), dtype=np.float32)
    for core in range(NCORES):
        b, s = core // 4, core % 4
        outp[b, s * T:(s + 1) * T, :] = res.results[core]["out"]
    return outp


# version 17 (final)
# speedup vs baseline: 1.1597x; 1.0172x over previous
import numpy as np
import ml_dtypes
import contextlib
import concourse.bass as bass
import concourse.mybir as mybir
from concourse.bass_utils import run_bass_kernel_spmd
from concourse.alu_op_type import AluOpType as A

F32 = mybir.dt.float32
BF16 = mybir.dt.bfloat16
AF = mybir.ActivationFunctionType
AX = mybir.AxisListType

NCORES = 8
B, S, D = 2, 8192, 1024
E, FF = 8, 2048
T = 2048
HALO = 640
EXT = HALO + T
NT = EXT // 128
KV = 2560
NEG = -1.0e30
PCH = [512, 512, 512, 512, 512, 2]
CCH = [512, 512, 512, 512, 511]

DEBUG_TAPS = ()
CAP = 768
SLOTS = 8 * CAP
BIGIDX = 1.0e9

ENGS = ("sync", "gpsimd", "vector", "scalar", "tensor")


class Plan:
    def __init__(self, nc, ctx, semnames):
        self.nc = nc
        self.q = {e: [] for e in ENGS}
        self.cnt = {}
        self.sems = {}
        for s in semnames:
            self.sems[s] = ctx.enter_context(nc.semaphore(s))
            self.cnt[s] = 0

    def op(self, eng, fn, wait=(), inc=None, k=1):
        w = [(self.sems[s], v) for s, v in wait if v > 0]
        if inc is not None:
            self.cnt[inc] += k
            ih = self.sems[inc]

            def run(h, fn=fn, w=w, ih=ih, k=k):
                for s, v in w:
                    h.wait_ge(s, v)
                fn(h).then_inc(ih, k)
        else:
            def run(h, fn=fn, w=w):
                for s, v in w:
                    h.wait_ge(s, v)
                fn(h)
        self.q[eng].append(run)
        return self.cnt[inc] if inc is not None else None


def build_program():
    nc = bass.Bass("TRN2", target_bir_lowering=False)
    ctx = contextlib.ExitStack()

    def din(name, shape, dt=F32):
        return nc.dram_tensor(name, shape, dt, kind="ExternalInput").ap()

    x_ext = din("x_ext", [EXT, D])
    wq_in = din("wq", [8, 128, 1024])
    wk_in = din("wk", [8, 128, 1024])
    wv_in = din("wv", [8, 128, 1024])
    wo_in = din("wo", [128, 8192])
    bq_in = din("bq", [128, 8])
    bk_in = din("bk", [128, 8])
    bv_in = din("bv", [128, 8])
    bo_in = din("bo_rep", [128, 1024])
    cw_in = din("cw", [128, 96])
    cdiag_in = din("cdiag", [8, 128, 1536])
    cb_in = din("cb", [128, 24])
    cost_in = din("cost", [128, KV])
    sint_in = din("sint", [128, KV])
    kmask_in = din("kmask", [128, 20])
    rwt_in = din("router_wt", [128, 64])
    rb_in = din("router_b", [8, 1])
    temp_in = din("temp", [1, 1])
    wg_s = din("wg_s", [8, 16, 128, 1024])
    wp_s = din("wp_s", [8, 16, 128, 1024])
    wo_s = din("wo_s", [8, 2, 128, 8192])
    ident_in = din("identity", [128, 128], BF16)
    identf_in = din("identityf", [128, 128], F32)
    rot_in = din("rot_t", [128, 128], BF16)
    tri0_in = din("tri0", [128, 128], BF16)
    tri4_in = din("tri4", [128, 128], BF16)
    sel2_in = din("sel2", [128, 2], BF16)
    selT2_in = din("selT2", [2, 128], BF16)
    cselA_in = din("colselA", [128, 128], BF16)
    cselB_in = din("colselB", [128, 128], BF16)
    ones8f_in = din("ones8f", [8, 1])
    ones18f_in = din("ones1_8f", [1, 8])
    sel64_in = din("sel64", [64, 32])
    ltri_in = din("ltri", [128, 128], BF16)
    ones128_in = din("ones128", [128, 1], BF16)
    ones1_128f_in = din("ones1_128f", [1, 128])
    ecap_in = din("ecap", [128, 8])
    m32_in = din("m32", [32, 8])
    g32_in = din("g32", [32, 1])
    actb_in = din("actbias", [128, 4])

    out = nc.dram_tensor("out", [T, D], F32, kind="ExternalOutput").ap()
    dbg = {}
    for tap, shape, dt in (
        ("xn", [128, 8 * EXT], BF16), ("qt", [128, KV], BF16), ("kt", [128, KV], BF16),
        ("vt", [128, KV], BF16), ("attnout", [128, 16384], BF16),
        ("accwo", [128, 16384], F32), ("gates", [128, 2048], F32),
        ("wgt", [128, 128], F32), ("scores", [8, 2048], F32), ("la", [8, 8192], F32),
        ("accffn", [128, 16384], F32), ("h0", [128, 12288], BF16),
        ("sl1", [128, 16], F32), ("sl2", [128, 16], F32),
        ("gs0", [128, 2048], BF16),
        ("osl", [SLOTS, 1024], BF16), ("gat", [SLOTS, 1024], BF16),
        ("tgs", [128, 32], F32), ("xg0", [128, 6144], BF16),
        ("attnout2", [128, 16384], BF16), ("woT", [128, 8192], BF16),
        ("lnstats", [128, 48], F32),
    ):
        if tap in DEBUG_TAPS:
            dbg[tap] = nc.dram_tensor("dbg_" + tap, shape, dt, kind="ExternalOutput").ap()

    xn_dram = nc.dram_tensor("xn_dram", [T, D], BF16)
    gath = nc.dram_tensor("gath", [SLOTS, D], BF16)
    oslots = nc.dram_tensor("oslots", [SLOTS, D], BF16)
    cc_in = nc.dram_tensor("cc_in", [8, T], F32)
    cc_out = nc.dram_tensor("cc_out", [64, T], F32)
    sk_scr = nc.dram_tensor("sk_scr", [8, 8192], F32)

    MB = 27648
    TOP = 229300

    def sb(name, shape, dt, off):
        assert off % 32 == 0, name
        nb = shape[1] * (2 if dt == BF16 else 4)
        assert MB + off + nb <= TOP, (name, off, nb)
        return nc.alloc_sbuf_tensor_at(name, shape, dt, offset=MB + off).ap()

    acc = sb("acc", [128, 16384], F32, 0)
    cost = sb("cost", [128, KV], F32, 0)
    sint = sb("sint", [128, KV], F32, 10240)
    xld = [sb(f"xld{i}", [128, 1024], F32, 20480 + 4096 * i) for i in range(2)]
    dev = sb("dev", [128, 1024], F32, 28672)
    sqd = sb("sqd", [128, 1024], F32, 32768)
    sc_r = sb("sc_r", [8, T], F32, 28672)
    qh = {pn: [sb(f"qh_{pn}{j}", [128, 516], BF16, 36864 + (2 * i + j) * 2080)
               for j in range(2)] for i, pn in enumerate(("q", "k", "v"))}
    convq = sb("convq", [128, 512], F32, 49344)
    convk = sb("convk", [128, 512], F32, 51392)
    rsin_q = sb("rsin_q", [128, 512], BF16, 53440)
    rsin_k = sb("rsin_k", [128, 512], BF16, 54464)
    rcos_q = sb("rcos_q", [128, 512], F32, 55488)
    rcos_k = sb("rcos_k", [128, 512], F32, 57536)
    qrot = sb("qrot", [128, 512], F32, 59584)
    krot = sb("krot", [128, 512], F32, 61632)
    sqq = sb("sqq", [128, 512], BF16, 63680)
    gn_own = sb("gn_own", [128, T], F32, 36864)
    xnt_bf = sb("xnt_bf", [128, 1024], BF16, 45056)
    ex32 = sb("ex32", [32, T], F32, 45056)
    raw64 = sb("raw64", [64, T], F32, 53248)
    B2 = 65536
    wpr = {pn: sb(f"wpr_{pn}", [128, 1024], BF16, B2 + i * 2048)
           for i, pn in enumerate(("q", "k", "v"))}
    qt_b = sb("qt_b", [128, KV], BF16, B2 + 6144)
    kt_b = sb("kt_b", [128, KV], BF16, B2 + 11264)
    vt_b = sb("vt_b", [128, KV], BF16, B2 + 16384)
    vtok = sb("vtok", [128, 20 * 130], BF16, B2 + 21504)
    pt = [sb(f"pt{i}", [128, 640], BF16, B2 + 26752 + 1280 * i) for i in range(2)]
    rr2 = sb("rr2", [2, 512], F32, B2 + 29312)
    rr2b = sb("rr2b", [2, 512], BF16, B2 + 31360)
    la = sb("la", [8, 8192], F32, B2 + 0)
    exs = sb("exs", [8, 1024], F32, B2 + 32768)
    lsee = sb("lsee", [1, 1024], F32, 61440)
    dn_own = sb("dn_own", [1, T], F32, B2 + 8192)
    wob = sb("wob", [128, 16384], BF16, B2 + 0)
    gbuf = [sb(f"gbuf{i}", [128, 512], BF16, B2 + 32768 + 1024 * i) for i in range(2)]
    pbuf = [sb(f"pbuf{i}", [128, 512], BF16, B2 + 34816 + 1024 * i) for i in range(2)]
    xsc = [sb(f"xsc{i}", [128, 1024], BF16, B2 + 0 + 2048 * i) for i in range(2)]
    osb = [sb(f"osb{i}", [128, 1024], BF16, 135168 + 36864 + 2048 * i)
           for i in range(2)]
    B3 = 102400
    attnout = sb("attnout", [128, 16384], BF16, B3)
    hbuf = sb("hbuf", [128, 16 * CAP], BF16, B3)
    xn = sb("xn", [128, 8 * EXT], BF16, 135168)
    XNR = 135168
    xg = sb("xg", [128, 6 * 1024], BF16, XNR)
    xg_fm = [sb(f"xg_fm{i}", [128, 8 * CAP], BF16, XNR + 12288 + 12288 * i)
             for i in range(2)]
    wgb = [sb(f"wgb{i}", [128, 1024], BF16, B3 + 24576 + 2048 * i) for i in range(2)]
    wpb = [sb(f"wpb{i}", [128, 1024], BF16, B3 + 28672 + 2048 * i) for i in range(2)]
    W4 = 178208
    woT = sb("woT", [128, 8192], BF16, W4)
    cdg = [sb(f"cdg{i}", [128, 1536], BF16, W4 + 3072 * i) for i in range(2)]
    xf = [sb(f"xf{i}", [128, 1024], F32, W4 + 4096 * i) for i in range(2)]
    gsl = [sb(f"gsl{i}", [128, 1024], BF16, B2 + 0 + 2048 * i) for i in range(2)]
    t1f = sb("t1f", [128, 1024], F32, W4 + 8192)
    bo_s = sb("bo_s", [128, 1024], F32, W4 + 12288)
    xnt = [sb(f"xnt{i}", [128, 1024], F32, 36864 + 4096 * i) for i in range(2)]
    off = [194624]

    def psm(name, shape, dt):
        t = sb(name, shape, dt, off[0])
        nb = shape[1] * (2 if dt == BF16 else 4)
        off[0] += (nb + 31) // 32 * 32
        return t

    m_all = psm("m_all", [128, 24], F32)
    rs_all = psm("rs_all", [128, 24], F32)
    ms1 = psm("ms1", [128, 1], F32)
    ms2 = psm("ms2", [128, 1], F32)
    ms3 = psm("ms3", [128, 1], F32)
    ident = psm("ident", [128, 128], BF16)
    identf = psm("identf", [128, 128], F32)
    rot_t = psm("rot_t", [128, 128], BF16)
    tri0 = psm("tri0", [128, 128], BF16)
    tri4 = psm("tri4", [128, 128], BF16)
    sel2 = psm("sel2", [128, 2], BF16)
    selT2 = psm("selT2", [2, 128], BF16)
    cselA = psm("cselA", [128, 128], BF16)
    cselB = psm("cselB", [128, 128], BF16)
    ones8f = psm("ones8f", [8, 1], F32)
    ones18f = psm("ones18f", [1, 8], F32)
    sel64 = psm("sel64", [64, 32], F32)
    m32 = psm("m32", [32, 8], F32)
    g32 = psm("g32", [32, 1], F32)
    cb = psm("cb", [128, 24], F32)
    bqkv = {"q": psm("bq_s", [128, 8], F32), "k": psm("bk_s", [128, 8], F32),
            "v": psm("bv_s", [128, 8], F32)}
    kmask = psm("kmask", [128, 20], F32)
    rwt = psm("rwt", [128, 64], BF16)
    rwtf = psm("rwtf", [128, 64], F32)
    rb8 = psm("rb8", [8, 1], F32)
    tempv = psm("tempv", [1, 1], F32)
    rtemp = psm("rtemp", [1, 1], F32)
    rtemp8 = psm("rtemp8", [8, 1], F32)
    den = psm("den", [128, 128], F32)
    denr = psm("denr", [128, 128], BF16)
    wgt = psm("wgt", [128, 128], F32)
    wtile = psm("wtile", [128, 8], F32)
    wm1 = psm("wm1", [128, 1], F32)
    wm2 = psm("wm2", [128, 1], F32)
    wmsk1 = psm("wmsk1", [128, 8], F32)
    wmsk2 = psm("wmsk2", [128, 8], F32)
    wg2v = psm("wg2v", [128, 8], F32)
    wden = psm("wden", [128, 1], F32)
    flq = psm("flq", [1, 32], BF16)
    actb = psm("actb", [128, 4], F32)
    ltri = psm("ltri", [128, 128], BF16)
    ones128 = psm("ones128", [128, 1], BF16)
    ones1_128f = psm("ones1_128f", [1, 128], F32)
    ecap = psm("ecap", [128, 8], F32)
    fbf = psm("fbf", [128, 8], BF16)
    baserep = psm("baserep", [128, 8], F32)
    cntsb = psm("cntsb", [1, 8], F32)
    sidx = psm("sidx", [128, 8], F32)
    sinr = psm("sinr", [128, 8], F32)
    svld = psm("svld", [128, 8], F32)
    sbig = psm("sbig", [128, 8], F32)
    stmp = psm("stmp", [128, 8], F32)
    msk21 = psm("msk21", [128, 8], F32)
    s1f = psm("s1f", [128, 1], F32)
    s2f = psm("s2f", [128, 1], F32)
    s1i = psm("s1i", [128, 16], mybir.dt.uint32)
    s2i = psm("s2i", [128, 16], mybir.dt.uint32)
    tg1 = psm("tg1", [128, 16], F32)
    tg2 = psm("tg2", [128, 16], F32)
    fl1 = psm("fl1", [128, 2], F32)
    ts8 = psm("ts8", [8, 1], F32)
    ts8b = psm("ts8b", [8, 1], F32)
    lsef = psm("lsef", [8, 1], F32)
    assert MB + off[0] <= TOP, off[0]

    PS_SA = [ctx.enter_context(nc.psum_tensor(f"ps_sa{i}", [128, 1024], F32))
             for i in range(2)]
    PS_AV = [ctx.enter_context(nc.psum_tensor(f"ps_av{i}", [128, 512], F32))
             for i in range(2)]
    PS_PQ = ctx.enter_context(nc.psum_tensor("ps_pq", [128, 512], F32))
    PS_VT = ctx.enter_context(nc.psum_tensor("ps_vt", [128, 128], BF16))

    P = Plan(nc, ctx, ("sq", "sg", "scc", "pe", "act", "dve", "pool"))
    C = P.cnt

    def dma(qeng, dst, src, wait=()):
        sem = "sg" if qeng == "gpsimd" else "sq"
        return P.op(qeng, lambda h, d=dst, s=src: h.dma_start(out=d, in_=s),
                    wait=wait, inc=sem, k=16)

    def dmat(qeng, dst, src, wait=()):
        return P.op(qeng, lambda h, d=dst, s=src: h.dma_start_transpose(d, s),
                    wait=wait, inc="sq", k=16)

    def mm(dst, lhsT, rhs, start, stop, wait=(), inc=None):
        return P.op("tensor",
                    lambda h, d=dst, l=lhsT, r=rhs, a=start, b=stop:
                    h.matmul(d, l, r, start=a, stop=b), wait=wait, inc=inc)

    def tpose(dst, src, idm, wait=(), inc=None):
        return P.op("tensor", lambda h, d=dst, s=src, i=idm: h.transpose(d, s, i),
                    wait=wait, inc=inc)

    def act(out_, in_, func, wait=(), inc=None, **kw):
        return P.op("scalar",
                    lambda h, o=out_, i=in_, f=func, kw=dict(kw):
                    h.activation(o, i, f, **kw), wait=wait, inc=inc)

    def vop(name, *args, wait=(), inc=None, eng="vector"):
        return P.op(eng, lambda h, n=name, a=args: getattr(h, n)(*a),
                    wait=wait, inc=inc)

    def allcnt():
        return [(s, C[s]) for s in ("pe", "act", "dve", "pool")]

    for d_, s_ in ((cost, cost_in), (sint, sint_in), (ident, ident_in),
                   (identf, identf_in), (rot_t, rot_in), (tri0, tri0_in),
                   (tri4, tri4_in), (sel2, sel2_in), (selT2, selT2_in),
                   (cselA, cselA_in), (cselB, cselB_in), (ones8f, ones8f_in),
                   (ones18f, ones18f_in), (sel64, sel64_in), (m32, m32_in),
                   (g32, g32_in), (cb, cb_in), (bqkv["q"], bq_in),
                   (bqkv["k"], bk_in), (bqkv["v"], bv_in), (kmask, kmask_in),
                   (rwtf, rwt_in), (rb8, rb_in), (tempv, temp_in),
                   (actb, actb_in), (ltri, ltri_in), (ones128, ones128_in),
                   (ones1_128f, ones1_128f_in), (ecap, ecap_in)):
        dma("sync", d_, s_)
    const_ld = C["sq"]
    dma("gpsimd", wpr["q"], wq_in[0])
    dma("gpsimd", wpr["k"], wk_in[0])
    dma("gpsimd", wpr["v"], wv_in[0])
    dma("gpsimd", cdg[0], cdiag_in[0])
    pw_ld = {0: C["sg"]}
    vop("memset", qt_b[:, KV - 1:KV], 0.0)
    vop("memset", kt_b[:, KV - 1:KV], 0.0)
    pad_dve = vop("memset", vt_b[:, KV - 1:KV], 0.0, inc="dve")

    x_til = x_ext.rearrange("(a p) d -> a p d", p=128)
    dev_done = {}
    tr_done = {-2: 0, -1: 0}
    rs_prev = 0
    vt_slot = {0: 0, 1: 0}
    xnd_wr = [0]
    for a in range(NT):
        buf = xld[a % 2]
        ld = dma("sync", buf, x_til[a], wait=[("dve", dev_done.get(a - 2, 0))])
        r1_ = vop("reduce_sum", ms1, buf, AX.X, wait=[("sq", ld)], inc="dve")
        m1_ = vop("tensor_scalar", m_all[:, a:a + 1], ms1, 1.0 / D, None, A.mult,
                  wait=[("dve", r1_)], inc="dve")
        dd = vop("tensor_scalar", dev, buf, m_all[:, a:a + 1], None, A.subtract,
                 wait=[("act", rs_prev), ("dve", m1_)], inc="dve")
        dev_done[a] = dd
        act(sqd, dev, AF.Square, wait=[("dve", dd)], accum_out=ms2)
        act(ms3, ms2, AF.Ln, scale=1.0 / D, bias=actb[:, 0:1])
        act(rs_all[:, a:a + 1], ms3, AF.Exp, scale=-0.5)
        rs = act(fl1[:, 0:1], rs_all[:, a:a + 1], AF.Copy, inc="act")
        rs_prev = rs
        xb = vop("tensor_scalar", xnt[a % 2], dev, rs_all[:, a:a + 1], None, A.mult,
                 wait=[("act", rs), ("dve", tr_done[a - 2])], inc="dve")
        if a >= 5:
            xc = act(xnt_bf[:], xnt[a % 2], AF.Copy,
                     wait=[("sq", xnd_wr[0]), ("dve", xb)], inc="act")
            xnd_wr[0] = dma("sync", xn_dram.ap().rearrange(
                "(t p) d -> t p d", p=128)[a - 5], xnt_bf[:],
                wait=[("act", xc)])
        for dc in range(8):
            sl_ = (a * 8 + dc) % 2
            tp = tpose(PS_AV[sl_][:, 0:128],
                       xnt[a % 2][:, dc * 128:(dc + 1) * 128], identf,
                       wait=[("dve", xb), ("dve", vt_slot[sl_])], inc="pe")
            vt_slot[sl_] = vop(
                "tensor_copy", xn[:, dc * EXT + a * 128: dc * EXT + (a + 1) * 128],
                PS_AV[sl_][:, 0:128], wait=[("pe", tp)], inc="dve")
        tr_done[a] = C["dve"]
    ln_sq = C["sq"]
    ln_dve = C["dve"]

    if "xn" in DEBUG_TAPS:
        dma("sync", dbg["xn"], xn[:], wait=[("dve", ln_dve)])

    tm_ = vop("tensor_scalar", rtemp, tempv, 0.1, None, A.max,
              wait=[("sq", const_ld)], inc="dve")
    vop("reciprocal", rtemp, rtemp, wait=[("dve", tm_)])
    rt = vop("tensor_copy", fl1[0:1, 0:1], rtemp, inc="dve")
    rwc = vop("tensor_copy", rwt, rwtf, inc="dve")
    r8 = mm(PS_PQ[0:8, 0:1], ones18f, rtemp, True, True, wait=[("dve", rt)], inc="pe")
    rt8a = vop("tensor_copy", rtemp8, PS_PQ[0:8, 0:1], wait=[("pe", r8)], inc="dve")
    rt8 = vop("tensor_copy", fl1[0:8, 1:2], rtemp8, wait=[("dve", rt8a)], inc="dve")
    sc_done = 0
    for tc in range(4):
        for dc in range(8):
            w = []
            if dc == 0:
                w = [("dve", rt8 if tc == 0 else sc_done), ("dve", ln_dve),
                     ("dve", rwc)]
            mm(PS_PQ[0:8, :], rwt[:, dc * 8:(dc + 1) * 8],
               xn[:, dc * EXT + HALO + tc * 512: dc * EXT + HALO + (tc + 1) * 512],
               dc == 0, dc == 7, wait=w, inc="pe" if dc == 7 else None)
        sc_done = vop("tensor_scalar", sc_r[:, tc * 512:(tc + 1) * 512],
                      PS_PQ[0:8, :], rb8, rtemp8, A.add, A.mult,
                      wait=[("pe", C["pe"])], inc="dve")
    cc_wr = dma("sync", cc_in.ap(), sc_r[:], wait=[("dve", sc_done)])
    P.op("gpsimd", lambda h: h.collective_compute(
        "AllGather", A.bypass, replica_groups=[list(range(NCORES))],
        ins=[cc_in.ap().opt()], outs=[cc_out.ap().opt()]),
        wait=[("sq", cc_wr)], inc="scc")
    if "scores" in DEBUG_TAPS:
        dma("sync", dbg["scores"], sc_r[:], wait=[("sq", cc_wr)])

    projs = ("q", "k", "v")
    proj_ps = {"q": lambda n: PS_SA[0][:, 0:n], "k": lambda n: PS_SA[0][:, 512:512 + n],
               "v": lambda n: PS_SA[1][:, 0:n]}
    evac_done = {"q": 0, "k": 0, "v": 0}
    conv_done = {}
    convread = {"q": 0, "k": 0, "v": 0}
    vtok_state = {"evac": 0}

    def emit_vtok(kt_i, pool_cnt):
        tp = tpose(PS_VT[:], vt_b[:, kt_i * 128:(kt_i + 1) * 128], ident,
                   wait=[("act", pool_cnt), ("dve", vtok_state["evac"])], inc="pe")
        vop("tensor_copy", vtok[:, kt_i * 130: kt_i * 130 + 64], PS_VT[:, 0:64],
            wait=[("pe", tp)])
        vtok_state["evac"] = vop(
            "tensor_copy", vtok[:, kt_i * 130 + 65: kt_i * 130 + 129],
            PS_VT[:, 64:128], inc="dve")

    def emit_conv_rot(pair, cc, st):
        csz = CCH[cc]
        col0 = 512 * cc
        cds = cdg[pair % 2]
        for ip, pn in enumerate(projs):
            src = qh[pn][cc % 2]
            if pn == "v":
                dsl = vt_b[:, col0:col0 + csz]
            else:
                dsl = (convq if pn == "q" else convk)[:, 0:csz]
            cbb = cb[:, ip * 8 + pair: ip * 8 + pair + 1]
            ps = proj_ps[pn](csz)
            for tap in range(4):
                w = []
                if tap == 0:
                    w = [("act", evac_done[pn]), ("dve", convread[pn])]
                    if cc < 4:
                        w.append(("dve", st["tail"][pn]))
                mm(ps, cds[:, (ip * 4 + tap) * 128:(ip * 4 + tap + 1) * 128],
                   src[:, tap:tap + csz], tap == 0, tap == 3, wait=w,
                   inc="pe" if tap == 3 else None)
            evac_done[pn] = act(dsl, ps, AF.Identity, bias=cbb,
                                wait=[("pe", C["pe"])], inc="act")
        conv_done[(pair, cc)] = C["act"]
        pc = C["act"]
        for pn in ("q", "k"):
            cvb = convq if pn == "q" else convk
            rsb = rsin_q if pn == "q" else rsin_k
            rcb = rcos_q if pn == "q" else rcos_k
            rob = qrot if pn == "q" else krot
            qtb = qt_b if pn == "q" else kt_b
            rot_ps = PS_SA[1][:, 512:512 + csz] if pn == "q" else PS_AV[0][:, 0:csz]
            l2_ps = PS_AV[1] if pn == "q" else PS_PQ
            r1 = vop("tensor_tensor", rsb[:, 0:csz], cvb[:, 0:csz],
                     sint[:, col0:col0 + csz], A.mult,
                     wait=[("act", pc), ("pe", st.get("rotmm_" + pn, 0))],
                     inc="dve")
            r2 = vop("tensor_tensor", rcb[:, 0:csz], cvb[:, 0:csz],
                     cost[:, col0:col0 + csz], A.mult, inc="dve")
            convread[pn] = r2
            rm = mm(rot_ps, rot_t, rsb[:, 0:csz], True, True,
                    wait=[("dve", r2), ("dve", st.get("qrot_" + pn, 0))], inc="pe")
            st["rotmm_" + pn] = rm
            qr = vop("tensor_tensor", rob[:, 0:csz], rcb[:, 0:csz], rot_ps, A.add,
                     wait=[("pe", rm), ("act", st.get("sq_" + pn, 0))], inc="dve")
            st["qrot_" + pn] = qr
            sq_ = act(sqq[:, 0:csz], rob[:, 0:csz], AF.Square,
                      wait=[("dve", qr), ("pe", st.get("l2mm_" + pn, 0))], inc="act")
            st["sq_" + pn] = sq_
            sm = mm(l2_ps[0:2, 0:csz], sel2, sqq[:, 0:csz], True, True,
                    wait=[("act", sq_), ("dve", st.get("qt_" + pn, 0))], inc="pe")
            act(rr2[:, 0:csz], l2_ps[0:2, 0:csz], AF.Ln, bias=actb[0:2, 1:2],
                wait=[("pe", sm)])
            if pn == "q":
                l2e = act(rr2b[:, 0:csz], rr2[:, 0:csz], AF.Exp, scale=-0.5,
                          bias=actb[0:2, 2:3], inc="act")
            else:
                l2e = act(rr2b[:, 0:csz], rr2[:, 0:csz], AF.Exp, scale=-0.5,
                          inc="act")
            st["l2e_" + pn] = l2e
            rp = mm(l2_ps[:, 0:csz], selT2, rr2b[:, 0:csz], True, True,
                    wait=[("act", l2e)], inc="pe")
            st["l2mm_" + pn] = rp
            qt_ = vop("tensor_tensor", qtb[:, col0:col0 + csz], rob[:, 0:csz],
                      l2_ps[:, 0:csz], A.mult, wait=[("pe", rp)], inc="dve")
            st["qt_" + pn] = qt_
        t0 = (col0 + 127) // 128
        t1 = (col0 + csz) // 128
        for kt_i in range(t0, t1):
            emit_vtok(kt_i, pc)

    def qkv_phase(pair, barrier):
        st = {"tail": {pn: 0 for pn in projs}}
        for ci, csz in enumerate(PCH):
            base = 126 + 512 * ci
            for pn in projs:
                ps = proj_ps[pn](csz)
                for dc in range(8):
                    w = []
                    if dc == 0:
                        w = [("sg", pw_ld[pair]), ("act", evac_done[pn])]
                        if ci == 0:
                            w = w + barrier
                    mm(ps, wpr[pn][:, dc * 128:(dc + 1) * 128],
                       xn[:, dc * EXT + base: dc * EXT + base + csz],
                       dc == 0, dc == 7, wait=w, inc="pe" if dc == 7 else None)
                if ci < 5:
                    dst = qh[pn][ci % 2][:, 0:csz]
                else:
                    dst = qh[pn][0][:, 512:514]
                ww = [("pe", C["pe"])]
                if ci >= 2:
                    ww.append(("act", conv_done.get((pair, ci - 2), 0)))
                evac_done[pn] = act(dst, ps, AF.Identity,
                                    bias=bqkv[pn][:, pair:pair + 1],
                                    wait=ww, inc="act")
            if 1 <= ci <= 4:
                for pn in projs:
                    st["tail"][pn] = vop(
                        "tensor_copy", qh[pn][(ci - 1) % 2][:, 512:515],
                        qh[pn][ci % 2][:, 0:3],
                        wait=[("act", evac_done[pn])], inc="dve")
                emit_conv_rot(pair, ci - 1, st)
        emit_conv_rot(pair, 4, st)

    def attn_inner(pair, qkv_barrier):
        its = [(h, q) for h in range(2) for q in range(16)]
        n = len(its)
        sd, ed, td, avd, dd_, rpd, nmd = {}, {}, {}, {}, {}, {}, {}
        for k in range(-4, 0):
            ed[k] = td[k] = avd[k] = dd_[k] = rpd[k] = nmd[k] = 0

        def emit_S(i):
            head, qt = its[i]
            hb = 64 * head
            sa = PS_SA[i % 2]
            for m in range(5):
                kt_i = qt + m
                dst = sa[:, m * 128:(m + 1) * 128] if m < 4 else sa[:, 512:640]
                w = []
                if m == 0:
                    w = [("act", ed[i - 2])]
                    if i == 0:
                        w = w + qkv_barrier
                mm(dst, kt_b[hb:hb + 64, kt_i * 128:(kt_i + 1) * 128],
                   qt_b[hb:hb + 64, 511 + qt * 128: 511 + qt * 128 + 128],
                   True, True, wait=w, inc="pe" if m == 4 else None)
            sd[i] = C["pe"]

        def emit_exp(i):
            head, qt = its[i]
            sa = PS_SA[i % 2]
            if qt <= 3:
                for m in range(5):
                    kt_i = qt + m
                    src = sa[:, m * 128:(m + 1) * 128] if m < 4 else sa[:, 512:640]
                    ee = act(pt[i % 2][:, m * 128:(m + 1) * 128], src, AF.Exp,
                             bias=kmask[:, kt_i:kt_i + 1],
                             wait=[("pe", sd[i])] if m == 0 else [],
                             inc="act" if m in (0, 4) else None)
                    if m == 0:
                        e0 = ee
            else:
                e0 = act(pt[i % 2][:, 0:128], sa[:, 0:128], AF.Exp,
                         wait=[("pe", sd[i])], inc="act")
                act(pt[i % 2][:, 128:512], sa[:, 128:512], AF.Exp)
                ee = act(pt[i % 2][:, 512:640], sa[:, 512:640], AF.Exp, inc="act")
            ed[i] = ee
            vop("tensor_tensor", pt[i % 2][:, 0:128], pt[i % 2][:, 0:128], tri0,
                A.mult, wait=[("act", e0)])
            td[i] = vop("tensor_tensor", pt[i % 2][:, 512:640], pt[i % 2][:, 512:640],
                        tri4, A.mult, wait=[("act", ee)], inc="dve")

        def emit_AV(i):
            head, qt = its[i]
            av = PS_AV[i % 2]
            for j, m in enumerate((1, 2, 3, 0, 4)):
                kt_i = qt + m
                w = []
                if j == 0:
                    w = [("act", ed[i]), ("dve", nmd[i - 2])]
                if m == 0:
                    w = [("dve", td[i])]
                mm(av[0:65, 0:128],
                   vtok[:, kt_i * 130 + head * 65: kt_i * 130 + head * 65 + 65],
                   pt[i % 2][:, m * 128:(m + 1) * 128], j == 0, j == 4,
                   wait=w, inc="pe" if j == 4 else None)
            avd[i] = C["pe"]
            oc = pair * 2048 + qt * 128
            hb = 64 * head
            act(attnout[hb:hb + 64, oc:oc + 128], av[0:64, 0:128], AF.Copy,
                wait=[("pe", avd[i])])
            act(den[64:65, :], av[64:65, 0:128], AF.Ln)
            dd_[i] = act(denr[64:65, :], den[64:65, :], AF.Exp, scale=-1.0,
                         inc="act")

        def emit_norm(i):
            head, qt = its[i]
            av = PS_AV[i % 2]
            hb = 64 * head
            oc = pair * 2048 + qt * 128
            csel = cselA if head == 0 else cselB
            rp = mm(av[:, 128:256], csel[64:65, :], denr[64:65, :], True, True,
                    wait=[("act", dd_[i]), ("dve", nmd[i - 2])], inc="pe")
            rpd[i] = rp
            nmd[i] = vop("tensor_tensor", attnout[hb:hb + 64, oc:oc + 128],
                         attnout[hb:hb + 64, oc:oc + 128], av[hb:hb + 64, 128:256],
                         A.mult, wait=[("pe", rp)], inc="dve")

        for step in range(n):
            emit_S(step)
            emit_exp(step)
            if step >= 1:
                emit_AV(step - 1)
            if step >= 2:
                emit_norm(step - 2)
        emit_AV(n - 1)
        emit_norm(n - 2)
        emit_norm(n - 1)

    vtok_ones = vtok.rearrange("p (t c) -> p t c", c=130)
    woT_ld = 0
    for pair in range(8):
        barrier = allcnt() + [("dve", pad_dve), ("dve", ln_dve)]
        qkv_phase(pair, barrier)
        if pair + 1 < 8:
            dma("gpsimd", wpr["q"], wq_in[pair + 1], wait=[("pe", C["pe"])])
            dma("gpsimd", wpr["k"], wk_in[pair + 1])
            dma("gpsimd", wpr["v"], wv_in[pair + 1])
            dma("gpsimd", cdg[(pair + 1) % 2], cdiag_in[pair + 1])
            pw_ld[pair + 1] = C["sg"]
        vop("memset", vtok_ones[:, :, 64:65], 1.0, wait=[("act", C["act"])])
        vop("memset", vtok_ones[:, :, 129:130], 1.0, inc="dve")
        emit_vtok(19, C["act"])
        qkv_barrier = allcnt()
        if pair == 0:
            for tap, bufr in (("qt", qt_b), ("kt", kt_b), ("vt", vt_b)):
                if tap in DEBUG_TAPS:
                    dma("sync", dbg[tap], bufr[:], wait=qkv_barrier)
        attn_inner(pair, qkv_barrier)
        if pair == 7:
            dma("gpsimd", woT, wo_in, wait=[("pe", C["pe"])])
            woT_ld = C["sg"]
    attn_end = allcnt()
    if "attnout" in DEBUG_TAPS:
        dma("sync", dbg["attnout"], attnout[:], wait=attn_end)

    wo_state = {"i": 0, "evac": {0: 0, 1: 0}}

    def next_wo(n, limit):
        for _ in range(n):
            if wo_state["i"] >= min(32, limit):
                return
            tt, oh = divmod(wo_state["i"], 2)
            wo_state["i"] += 1
            i2 = (tt * 2 + oh) % 2
            ps = PS_AV[i2][:]
            for cc in range(8):
                w = []
                if cc == 0:
                    w = [("dve", wo_state["evac"][i2]), ("sg", woT_ld)]
                    if tt == 0 and oh == 0:
                        w = w + attn_end
                mm(ps, attnout[:, cc * 2048 + tt * 128: cc * 2048 + tt * 128 + 128],
                   woT[:, cc * 1024 + oh * 512: cc * 1024 + oh * 512 + 512],
                   cc == 0, cc == 7, wait=w, inc="pe" if cc == 7 else None)
            wo_state["evac"][i2] = vop(
                "tensor_copy",
                acc[:, tt * 1024 + oh * 512: tt * 1024 + oh * 512 + 512],
                ps, wait=[("pe", C["pe"])], inc="dve")

    ld = dma("sync", raw64[:], cc_out.ap(), wait=[("scc", 1)] + attn_end)
    for cl in range(4):
        for hh in range(4):
            mm(PS_PQ[0:8, :], sel64[:, cl * 8:(cl + 1) * 8],
               raw64[:, hh * 512:(hh + 1) * 512], True, True,
               wait=[("sq", ld), ("dve", C["dve"])], inc="pe")
            vop("tensor_copy", la[:, cl * 2048 + hh * 512: cl * 2048 + (hh + 1) * 512],
                PS_PQ[0:8, :], wait=[("pe", C["pe"])], inc="dve")
    next_wo(2, 18)

    def sink_iter(feed=False):
        for ch in range(8):
            if feed:
                next_wo(1, 18)
            sl = la[:, ch * 1024:(ch + 1) * 1024]
            e1 = act(exs[:], sl, AF.Exp,
                     wait=[("dve", C["dve"]), ("pe", C["pe"])], inc="act")
            mm(PS_PQ[0:1, 0:512], ones8f, exs[:, 0:512], True, True,
               wait=[("act", e1)])
            mm(PS_SA[1][0:1, 512:1024], ones8f, exs[:, 512:1024], True, True,
               inc="pe")
            act(lsee[:, 0:512], PS_PQ[0:1, 0:512], AF.Ln, wait=[("pe", C["pe"])])
            l1 = act(lsee[:, 512:1024], PS_SA[1][0:1, 512:1024], AF.Ln, inc="act")
            for hh in range(2):
                mm(PS_SA[0][0:8, hh * 512:(hh + 1) * 512], ones18f,
                   lsee[:, hh * 512:(hh + 1) * 512], True, True,
                   wait=[("act", l1), ("dve", C["dve"])] if hh == 0 else [],
                   inc="pe" if hh == 1 else None)
            vop("tensor_tensor", sl, sl, PS_SA[0][0:8, 0:1024], A.subtract,
                wait=[("pe", C["pe"])], inc="dve")
        zz = vop("memset", ts8, 0.0, inc="dve")
        for ch in range(8):
            sl = la[:, ch * 1024:(ch + 1) * 1024]
            e2 = act(exs[:], sl, AF.Exp,
                     wait=[("dve", C["dve"]), ("pe", C["pe"])], inc="act")
            rr_ = vop("reduce_sum", ts8b, exs[:], AX.X, wait=[("act", e2)],
                      inc="dve")
            zz = vop("tensor_tensor", ts8, ts8, ts8b, A.add,
                     wait=[("dve", rr_)], inc="dve")
        fz = vop("tensor_copy", fl1[0:8, 0:1], ts8, wait=[("dve", zz)], inc="dve")
        act(lsef, ts8, AF.Ln, wait=[("dve", fz)])
        lt = act(fl1[0:8, 1:2], lsef, AF.Copy, inc="act")
        for ch in range(8):
            vop("tensor_scalar", la[:, ch * 1024:(ch + 1) * 1024],
                la[:, ch * 1024:(ch + 1) * 1024], lsef, None, A.subtract,
                wait=[("act", lt)] if ch == 0 else [], inc="dve")

    for it_ in range(3):
        sink_iter(feed=it_ < 2)
    if "la" in DEBUG_TAPS:
        dma("sync", dbg["la"], la[:], wait=[("dve", C["dve"])])
    skw = dma("sync", sk_scr.ap(), la[:], wait=[("dve", C["dve"])])
    for cl in range(4):
        skr = dma("sync", raw64[cl * 8:(cl + 1) * 8, :],
                  sk_scr.ap()[:, cl * T:(cl + 1) * T], wait=[("sq", skw)])
    ex = act(ex32[:], raw64[0:32, :], AF.Exp, wait=[("sq", skr)], inc="act")
    for g2 in range(2):
        for hh in range(2):
            mm(PS_SA[0][0:8, hh * 512:(hh + 1) * 512], m32,
               ex32[:, g2 * 1024 + hh * 512: g2 * 1024 + (hh + 1) * 512],
               True, True,
               wait=[("act", ex), ("dve", C["dve"])] if hh == 0 else [],
               inc="pe" if hh == 1 else None)
        vop("tensor_copy", gn_own[0:8, g2 * 1024:(g2 + 1) * 1024],
            PS_SA[0][0:8, 0:1024], wait=[("pe", C["pe"])], inc="dve")
    for hh in range(4):
        mm(PS_PQ[0:1, 0:512], g32, ex32[:, hh * 512:(hh + 1) * 512], True, True,
           wait=[("dve", C["dve"])], inc="pe")
        vop("tensor_scalar", dn_own[:, hh * 512:(hh + 1) * 512], PS_PQ[0:1, 0:512],
            1e-8, None, A.add, wait=[("pe", C["pe"])], inc="dve")
    rcp = vop("reciprocal", dn_own[:], dn_own[:], inc="dve")
    for g2 in range(2):
        for hh in range(2):
            mm(PS_SA[1][0:8, hh * 512:(hh + 1) * 512], ones18f,
               dn_own[:, g2 * 1024 + hh * 512: g2 * 1024 + (hh + 1) * 512],
               True, True,
               wait=[("dve", rcp if g2 == 0 else C["dve"])] if hh == 0 else [],
               inc="pe" if hh == 1 else None)
        vop("tensor_tensor", gn_own[0:8, g2 * 1024:(g2 + 1) * 1024],
            gn_own[0:8, g2 * 1024:(g2 + 1) * 1024], PS_SA[1][0:8, 0:1024],
            A.mult, wait=[("pe", C["pe"])], inc="dve")
    if "gates" in DEBUG_TAPS:
        dma("sync", dbg["gates"], gn_own[:], wait=[("dve", C["dve"])])
    vop("memset", baserep, 0.0, inc="dve")
    BCREG = {}
    P.op("gpsimd", lambda h: BCREG.__setitem__("r", h.to_reg(SLOTS - 1)))
    scat = {-2: 0, -1: 0}
    xn_til = xn_dram.ap().rearrange("(t p) d -> t p d", p=128)
    gath_ap = gath.ap()
    for a in range(16):
        tp = tpose(PS_PQ[:, 0:128], gn_own[:, a * 128:(a + 1) * 128], identf,
                   wait=[("dve", C["dve"])], inc="pe")
        q_ = vop("tensor_copy", wtile, PS_PQ[:, 0:8], wait=[("pe", tp)], inc="dve")
        q_ = vop("reduce_max", wm1, wtile, AX.X, wait=[("dve", q_)], inc="dve")
        q_ = vop("tensor_scalar", wmsk1, wtile, wm1, None, A.is_equal,
                 wait=[("dve", q_)], inc="dve")
        q_ = vop("scalar_tensor_tensor", wg2v, wmsk1, NEG, wtile, A.mult, A.add,
                 wait=[("dve", q_)], inc="dve")
        q_ = vop("reduce_max", wm2, wg2v, AX.X, wait=[("dve", q_)], inc="dve")
        q_ = vop("tensor_scalar", wmsk2, wtile, wm2, None, A.is_ge,
                 wait=[("dve", q_)], inc="dve")
        q_ = vop("tensor_tensor", wden, wm1, wm2, A.add, wait=[("dve", q_)],
                 inc="dve")
        q_ = vop("tensor_scalar", wden, wden, 1e-8, None, A.add,
                 wait=[("dve", q_)], inc="dve")
        q_ = vop("reciprocal", wden, wden, wait=[("dve", q_)], inc="dve")
        q_ = vop("tensor_scalar", wtile, wtile, wden, None, A.mult,
                 wait=[("dve", q_)], inc="dve")
        q_ = vop("tensor_tensor", wgt[:, a * 8:(a + 1) * 8], wtile, wmsk2, A.mult,
                 wait=[("dve", q_)], inc="dve")
        q_ = vop("tensor_copy", fbf, wmsk2, wait=[("dve", q_)], inc="dve")
        pm_ = mm(PS_PQ[:, 128:136], ltri, fbf, True, True,
                 wait=[("dve", q_)], inc="pe")
        cm_ = mm(PS_PQ[0:1, 136:144], ones128, fbf, True, True, inc="pe")
        q_ = vop("tensor_copy", cntsb, PS_PQ[0:1, 136:144],
                 wait=[("pe", cm_)], inc="dve")
        rm_ = mm(PS_PQ[:, 144:152], ones1_128f, cntsb, True, True,
                 wait=[("dve", q_)], inc="pe")
        q_ = vop("tensor_tensor", sidx, PS_PQ[:, 128:136], baserep, A.add,
                 wait=[("pe", rm_)], inc="dve")
        q_ = vop("tensor_tensor", baserep, baserep, PS_PQ[:, 144:152], A.add,
                 wait=[("dve", q_)], inc="dve")
        q_ = vop("tensor_scalar", sinr, sidx, float(CAP), None, A.is_lt,
                 wait=[("dve", q_)], inc="dve")
        q_ = vop("tensor_tensor", svld, sinr, wmsk2, A.mult,
                 wait=[("dve", q_)], inc="dve")
        q_ = vop("tensor_tensor", stmp, sidx, ecap, A.add,
                 wait=[("dve", q_)], inc="dve")
        q_ = vop("tensor_scalar", sbig, svld, -BIGIDX, BIGIDX, A.mult, A.add,
                 wait=[("dve", q_)], inc="dve")
        q_ = vop("tensor_tensor", stmp, stmp, svld, A.mult,
                 wait=[("dve", q_)], inc="dve")
        q_ = vop("tensor_tensor", stmp, stmp, sbig, A.add,
                 wait=[("dve", q_)], inc="dve")
        q_ = vop("tensor_tensor", msk21, wmsk2, wmsk1, A.subtract,
                 wait=[("dve", q_)], inc="dve")
        q_ = vop("tensor_tensor", sbig, stmp, wmsk1, A.mult,
                 wait=[("dve", q_)], inc="dve")
        q_ = vop("reduce_sum", s1f, sbig, AX.X, wait=[("dve", q_)], inc="dve")
        q_ = vop("tensor_copy", s1i[:, a:a + 1], s1f, wait=[("dve", q_)], inc="dve")
        q_ = vop("tensor_tensor", sbig, stmp, msk21, A.mult,
                 wait=[("dve", q_)], inc="dve")
        q_ = vop("reduce_sum", s2f, sbig, AX.X, wait=[("dve", q_)], inc="dve")
        q_ = vop("tensor_copy", s2i[:, a:a + 1], s2f, wait=[("dve", q_)], inc="dve")
        sl_rdy = q_
        ld = dma("sync", xsc[a % 2], xn_til[a],
                 wait=[("sg", scat[a - 2]), ("sq", xnd_wr[0])])
        P.op("gpsimd", lambda h, a=a: h.indirect_dma_start(
            out=gath_ap, out_offset=bass.IndirectOffsetOnAxis(
                ap=s1i[:, a:a + 1], axis=0),
            in_=xsc[a % 2][:], in_offset=None, bounds_check=BCREG["r"],
            oob_is_err=False),
            wait=[("sq", ld), ("dve", sl_rdy)], inc="sg", k=16)
        scat[a] = P.op("gpsimd", lambda h, a=a: h.indirect_dma_start(
            out=gath_ap, out_offset=bass.IndirectOffsetOnAxis(
                ap=s2i[:, a:a + 1], axis=0),
            in_=xsc[a % 2][:], in_offset=None, bounds_check=BCREG["r"],
            oob_is_err=False), inc="sg", k=16)
        q_ = vop("tensor_tensor", sbig, wgt[:, a * 8:(a + 1) * 8], wmsk1, A.mult,
                 wait=[("dve", q_)], inc="dve")
        q_ = vop("reduce_sum", tg1[:, a:a + 1], sbig, AX.X,
                 wait=[("dve", q_)], inc="dve")
        q_ = vop("tensor_tensor", sbig, wgt[:, a * 8:(a + 1) * 8], msk21, A.mult,
                 wait=[("dve", q_)], inc="dve")
        q_ = vop("reduce_sum", tg2[:, a:a + 1], sbig, AX.X,
                 wait=[("dve", q_)], inc="dve")
    gates_done = C["dve"]
    next_wo(32, 32)
    scat_end = C["sg"]
    if "sl1" in DEBUG_TAPS:
        dma("gpsimd", dbg["sl1"], s1i[:], wait=[("dve", gates_done)])
        dma("gpsimd", dbg["sl2"], s2i[:], wait=[("dve", gates_done)])
        dma("sync", dbg["tgs"][:, 0:16], tg1[:], wait=[("dve", gates_done)])
        dma("sync", dbg["tgs"][:, 16:32], tg2[:], wait=[("dve", gates_done)])
    if "wgt" in DEBUG_TAPS:
        dma("sync", dbg["wgt"], wgt[:], wait=[("dve", gates_done)])

    accwo_tap = 0
    if "accwo" in DEBUG_TAPS:
        accwo_tap = dma("sync", dbg["accwo"], acc[:], wait=[("dve", C["dve"])])

    ffn_barrier = allcnt()
    oslots_til = oslots.ap().rearrange("(t p) d -> t p d", p=128)
    gath_til = gath.ap().rearrange("(t p) d -> t p d", p=128)
    dma("gpsimd", wgb[0], wg_s[0, 0], wait=[("pe", C["pe"])])
    dma("gpsimd", wpb[0], wp_s[0, 0])
    wg_ld = {0: C["sg"]}
    dma("gpsimd", wob[:, 0:8192], wo_s[0, 0], wait=[("pe", C["pe"])])
    dma("gpsimd", wob[:, 8192:16384], wo_s[0, 1])
    wo_ld2 = {0: C["sg"]}
    wo_pe = {}
    for st in range(6):
        dma("sync", xg[:, st * 1024:(st + 1) * 1024], gath_til[st],
            wait=[("sg", scat_end)] if st == 0 else [])
    xg_ld = {0: C["sq"]}
    if "xg0" in DEBUG_TAPS:
        dma("sync", dbg["xg0"], xg[:], wait=[("sg", C["sg"])])
    gp_evac = {}
    hmul_done = {}
    wo_evac2 = {0: 0, 1: 0}
    osb_wr = {-2: 0, -1: 0}
    TCH = [(0, 512), (512, 256)]

    tp_evac = {"act": 0}

    def emit_tp(e, idx):
        st, dc = idx // 8, idx % 8
        fme = xg_fm[e % 2]
        tpv = tpose(PS_VT[:], xg[:, st * 1024 + dc * 128:
                                 st * 1024 + dc * 128 + 128], ident,
                    wait=[("sq", xg_ld[e]), ("act", tp_evac["act"])], inc="pe")
        tp_evac["act"] = act(
            fme[:, dc * CAP + st * 128: dc * CAP + st * 128 + 128],
            PS_VT[:], AF.Copy, wait=[("pe", tpv)], inc="act")

    for idx in range(48):
        emit_tp(0, idx)
    tp_done = {0: C["act"]}
    for e in range(E):
        tp_i = [0]
        if e + 1 < E:
            for st in range(6):
                dma("sync", xg[:, st * 1024:(st + 1) * 1024],
                    gath_til[(e + 1) * 6 + st],
                    wait=[("pe", C["pe"])] if st == 0 else [])
            xg_ld[e + 1] = C["sq"]
        fme = xg_fm[e % 2]
        for fc in range(16):
            cur = e * 16 + fc
            nxt = cur + 1
            if nxt < 128 and nxt not in wg_ld:
                dma("gpsimd", wgb[nxt % 2], wg_s[nxt // 16, nxt % 16],
                    wait=[("pe", C["pe"])])
                dma("gpsimd", wpb[nxt % 2], wp_s[nxt // 16, nxt % 16])
                wg_ld[nxt] = C["sg"]
            if e >= 1 and fc == 2:
                dma("gpsimd", wob[:, 0:8192], wo_s[e, 0],
                    wait=[("pe", wo_pe[e - 1])])
            if e >= 1 and fc == 6:
                dma("gpsimd", wob[:, 8192:16384], wo_s[e, 1])
                wo_ld2[e] = C["sg"]
            for ti, (toff, tn) in enumerate(TCH):
                hslice = hbuf[:, fc * CAP + toff: fc * CAP + toff + tn]
                qi = (fc * 2 + ti) % 4
                g_ps = (PS_SA[0][:, 0:512], PS_SA[0][:, 512:1024],
                        PS_AV[0][:], PS_AV[1][:])[qi]
                p_ps = (PS_SA[1][:, 0:512], PS_SA[1][:, 512:1024],
                        PS_PQ[:], PS_SA[1][:, 0:512])[qi % 3]
                for which, wbuf, ps_, ebuf, fn, slot in (
                    ("g", wgb[cur % 2], g_ps, gbuf, AF.Relu, qi),
                    ("p", wpb[cur % 2], p_ps, pbuf, AF.Copy, qi % 3),
                ):
                    ps = ps_[:, 0:tn] if ps_.shape[1] > tn else ps_
                    for dc in range(8):
                        w = []
                        if dc == 0:
                            w = [("sg", wg_ld[cur]),
                                 ("act", gp_evac.get((which, slot), 0)),
                                 ("act", tp_done[e])]
                            if which == "g" and qi >= 2:
                                w.append(("dve", wo_evac2[qi - 2]))
                            if cur == 0 and ti == 0 and which == "g":
                                w = w + ffn_barrier
                        mm(ps, wbuf[:, dc * 128:(dc + 1) * 128],
                           fme[:, dc * CAP + toff: dc * CAP + toff + tn],
                           dc == 0, dc == 7, wait=w,
                           inc="pe" if dc == 7 else None)
                    gp_evac[(which, slot)] = act(
                        ebuf[ti][:, 0:tn], ps, fn,
                        wait=[("pe", C["pe"]), ("dve", hmul_done.get(ti, 0))],
                        inc="act")
                hmul_done[ti] = vop(
                    "tensor_tensor", hslice, gbuf[ti][:, 0:tn], pbuf[ti][:, 0:tn],
                    A.mult, wait=[("act", gp_evac[("g", qi)]),
                                  ("act", gp_evac[("p", qi % 3)])], inc="dve")

        hm = C["dve"]
        for st in range(6):
            for oh in range(2):
                i2 = (st * 2 + oh) % 2
                ps = PS_AV[i2][:]
                for fc in range(16):
                    w = []
                    if fc == 0:
                        w = [("sg", wo_ld2[e]), ("dve", wo_evac2[i2])]
                        if st == 0 and oh == 0:
                            w.append(("dve", hm))
                    mm(ps, hbuf[:, fc * CAP + st * 128: fc * CAP + st * 128 + 128],
                       wob[:, fc * 1024 + oh * 512: fc * 1024 + oh * 512 + 512],
                       fc == 0, fc == 15, wait=w, inc="pe" if fc == 15 else None)
                wo_evac2[i2] = vop(
                    "tensor_copy", osb[st % 2][:, oh * 512: oh * 512 + 512], ps,
                    wait=[("pe", C["pe"]), ("sq", osb_wr[st - 2])], inc="dve")
                if e + 1 < E:
                    for _ in range(4):
                        if tp_i[0] < 48:
                            emit_tp(e + 1, tp_i[0])
                            tp_i[0] += 1
            osb_wr[st] = dma("sync", oslots_til[e * 6 + st], osb[st % 2][:],
                             wait=[("dve", wo_evac2[(st * 2 + 1) % 2])])
        wo_pe[e] = C["pe"]
        if e + 1 < E:
            while tp_i[0] < 48:
                emit_tp(e + 1, tp_i[0])
                tp_i[0] += 1
            tp_done[e + 1] = C["act"]
        osb_wr[-2], osb_wr[-1] = osb_wr[4], osb_wr[5]

    ffn_done = C["dve"]
    ffn_sq = C["sq"]

    if "osl" in DEBUG_TAPS:
        dma("sync", dbg["osl"], oslots.ap(), wait=[("sq", ffn_sq), ("dve", ffn_done)])
        dma("sync", dbg["gat"], gath.ap(), wait=[("sq", ffn_sq)])
    out_til = out.rearrange("(a p) d -> a p d", p=128)
    bo_ld = dma("sync", bo_s, bo_in, wait=[("dve", ffn_done)])
    xf_user = {0: 0, 1: 0}
    gsl_done = {-1: 0}
    for tt in range(16):
        a = 5 + tt
        wl = [("dve", xf_user[tt % 2])]
        if tt < 2:
            wl.append(("dve", ffn_done))
        ld = dma("sync", xf[tt % 2], x_til[a], wait=wl)
        z1 = vop("memset", gsl[0][:], 0.0, wait=[("dve", gsl_done[tt - 1])],
                 inc="dve")
        g1 = P.op("gpsimd", lambda h, tt=tt: h.indirect_dma_start(
            out=gsl[0][:], out_offset=None, in_=oslots.ap(),
            in_offset=bass.IndirectOffsetOnAxis(ap=s1i[:, tt:tt + 1], axis=0),
            bounds_check=BCREG["r"], oob_is_err=False),
            wait=[("dve", z1), ("sq", ffn_sq)], inc="sg", k=16)
        z2 = vop("memset", gsl[1][:], 0.0, inc="dve")
        g2 = P.op("gpsimd", lambda h, tt=tt: h.indirect_dma_start(
            out=gsl[1][:], out_offset=None, in_=oslots.ap(),
            in_offset=bass.IndirectOffsetOnAxis(ap=s2i[:, tt:tt + 1], axis=0),
            bounds_check=BCREG["r"], oob_is_err=False),
            wait=[("dve", z2)], inc="sg", k=16)
        g2 = dma("gpsimd", flq, oslots.ap()[0:1, 0:32])
        g1 = g2
        if tt == 0 and "gs0" in DEBUG_TAPS:
            dma("sync", dbg["gs0"][:, 0:1024], gsl[0][:], wait=[("sg", g1)])
            gtap = dma("sync", dbg["gs0"][:, 1024:2048], gsl[1][:],
                       wait=[("sg", g2)])
            P.op("vector", lambda h: h.memset(wden, 0.0), wait=[("sq", gtap)])
        w = [("sq", ld)]
        if tt == 0:
            w.append(("dve", ffn_done))
        sl = acc[:, tt * 1024:(tt + 1) * 1024]
        vop("tensor_tensor", sl, sl, xf[tt % 2], A.add, wait=w)
        vop("tensor_tensor", sl, sl, bo_s, A.add,
            wait=[("sq", bo_ld)] if tt == 0 else ())
        vop("tensor_scalar", t1f, xf[tt % 2], m_all[:, a:a + 1],
            rs_all[:, a:a + 1], A.subtract, A.mult)
        vop("tensor_tensor", sl, sl, t1f, A.add)
        vop("tensor_scalar", t1f, gsl[0][:], tg1[:, tt:tt + 1], None, A.mult,
            wait=[("sg", g1)])
        vop("tensor_tensor", sl, sl, t1f, A.add)
        vop("tensor_scalar", t1f, gsl[1][:], tg2[:, tt:tt + 1], None, A.mult,
            wait=[("sg", g2)])
        gsl_done[tt] = xf_user[tt % 2] = vop(
            "tensor_tensor", sl, sl, t1f, A.add, inc="dve")
        dma("sync", out_til[tt], sl, wait=[("dve", xf_user[tt % 2])])
    final_sq = C["sq"]
    P.op("sync", lambda h: h.nop(), wait=[("sq", final_sq)])

    ctx.enter_context(nc.allow_low_precision(reason="bf16 matmul operands by design"))
    with nc.Block() as block:
        @block.sync
        def _(eng):
            for f in P.q["sync"]:
                f(eng)

        @block.gpsimd
        def _(eng):
            for f in P.q["gpsimd"]:
                f(eng)

        @block.vector
        def _(eng):
            for f in P.q["vector"]:
                f(eng)

        @block.scalar
        def _(eng):
            for f in P.q["scalar"]:
                f(eng)

        @block.tensor
        def _(eng):
            for f in P.q["tensor"]:
                f(eng)

    ctx.close()
    return nc



def _rotary_tables(t0):
    d = np.arange(64)
    inv = 1.0 / (10000.0 ** ((d % 32) * 2.0 / 64.0))
    pos = t0 - 511 + np.arange(KV)
    pos = np.where(pos < 0, 0, pos).astype(np.float64)
    ang = pos[None, :] * inv[:, None]
    cos = np.cos(ang).astype(np.float32)
    sin = np.sin(ang).astype(np.float32)
    return (np.ascontiguousarray(np.concatenate([cos, cos], axis=0)),
            np.ascontiguousarray(np.concatenate([sin, sin], axis=0)))


def _consts():
    c = {}
    c["identity"] = np.eye(128, dtype=np.float32).astype(ml_dtypes.bfloat16)
    c["identityf"] = np.eye(128, dtype=np.float32)
    R = np.zeros((128, 128), dtype=np.float32)
    for do in range(128):
        h64, dd = (do // 64) * 64, do % 64
        if dd < 32:
            R[do, h64 + dd + 32] = -1.0
        else:
            R[do, h64 + dd - 32] = 1.0
    c["rot_t"] = np.ascontiguousarray(R.T).astype(ml_dtypes.bfloat16)
    kp = np.arange(128)[:, None]
    qf = np.arange(128)[None, :]
    c["tri0"] = (kp >= qf).astype(ml_dtypes.bfloat16)
    c["tri4"] = (kp < qf).astype(ml_dtypes.bfloat16)
    sel2 = np.zeros((128, 2), dtype=np.float32)
    sel2[:64, 0] = 1.0
    sel2[64:, 1] = 1.0
    c["sel2"] = sel2.astype(ml_dtypes.bfloat16)
    c["selT2"] = np.ascontiguousarray(sel2.T).astype(ml_dtypes.bfloat16)
    csA = np.zeros((128, 128), dtype=np.float32)
    csA[64, :64] = 1.0
    c["colselA"] = csA.astype(ml_dtypes.bfloat16)
    csB = np.zeros((128, 128), dtype=np.float32)
    csB[64, 64:] = 1.0
    c["colselB"] = csB.astype(ml_dtypes.bfloat16)
    c["ones8f"] = np.ones((8, 1), dtype=np.float32)
    c["ones1_8f"] = np.ones((1, 8), dtype=np.float32)
    kk = np.arange(128)[:, None]
    mm_ = np.arange(128)[None, :]
    c["ltri"] = (kk < mm_).astype(ml_dtypes.bfloat16)
    c["ones128"] = np.ones((128, 1), dtype=ml_dtypes.bfloat16)
    c["ones1_128f"] = np.ones((1, 128), dtype=np.float32)
    c["ecap"] = np.ascontiguousarray(
        np.tile((np.arange(8) * CAP).astype(np.float32)[None, :], (128, 1)))
    ab = np.zeros((128, 4), dtype=np.float32)
    ab[:, 0] = 1e-5
    ab[:, 1] = 1e-12
    ab[:, 2] = np.log(0.125)
    c["actbias"] = ab
    return c


def _prep_weights(inp):
    w = {}
    for nm, key in (("wq", "L0_attn_Wq"), ("wk", "L0_attn_Wk"), ("wv", "L0_attn_Wv")):
        W = np.asarray(inp[key], dtype=np.float32)
        WT = np.ascontiguousarray(W.T)
        w[nm] = np.ascontiguousarray(
            WT.reshape(8, 128, 8, 128).transpose(2, 1, 0, 3)).reshape(8, 128, 1024)
    WoT = np.ascontiguousarray(np.asarray(inp["L0_attn_Wo"], dtype=np.float32).T)
    w["wo"] = np.ascontiguousarray(
        WoT.reshape(8, 128, 8, 128).transpose(1, 0, 2, 3)).reshape(128, 8192)
    for nm, key in (("bq", "L0_attn_bq"), ("bk", "L0_attn_bk"), ("bv", "L0_attn_bv")):
        w[nm] = np.ascontiguousarray(
            np.asarray(inp[key], dtype=np.float32).reshape(8, 128).T)
    w["bo_rep"] = np.ascontiguousarray(np.tile(
        np.asarray(inp["L0_attn_bo"], dtype=np.float32)[None, :], (128, 1)))
    cwl, cbl = [], []
    for kw_, kb_ in (("L0_attn_cqw", "L0_attn_cqb"), ("L0_attn_ckw", "L0_attn_ckb"),
                     ("L0_attn_cvw", "L0_attn_cvb")):
        cwl.append(np.asarray(inp[kw_], dtype=np.float32)
                   .reshape(8, 128, 4).transpose(1, 0, 2).reshape(128, 32))
        cbl.append(np.asarray(inp[kb_], dtype=np.float32).reshape(8, 128).T)
    w["cw"] = np.ascontiguousarray(np.concatenate(cwl, axis=1))
    w["cb"] = np.ascontiguousarray(np.concatenate(cbl, axis=1))
    cd = np.zeros((8, 128, 1536), dtype=np.float32)
    p_ = np.arange(128)
    for ip, kw_ in enumerate(("L0_attn_cqw", "L0_attn_ckw", "L0_attn_cvw")):
        wfull = np.asarray(inp[kw_], dtype=np.float32)
        for pair in range(8):
            for tap in range(4):
                cd[pair, p_, (ip * 4 + tap) * 128 + p_] = wfull[pair * 128 + p_, tap]
    w["cdiag"] = cd
    rw = np.asarray(inp["L0_ffn_router_w"], dtype=np.float32)
    w["router_wt"] = np.ascontiguousarray(
        rw.T.reshape(8, 128, 8).transpose(1, 0, 2)).reshape(128, 64)
    w["router_b"] = np.ascontiguousarray(
        np.asarray(inp["L0_ffn_router_b"], dtype=np.float32).reshape(8, 1))
    w["temp"] = np.ascontiguousarray(
        np.asarray(inp["L0_ffn_temperature"], dtype=np.float32).reshape(1, 1))
    wg = np.asarray(inp["L0_ffn_wg"], dtype=np.float32)
    wp = np.asarray(inp["L0_ffn_wp"], dtype=np.float32)
    wo = np.asarray(inp["L0_ffn_wo"], dtype=np.float32)
    w["wg_s"] = np.ascontiguousarray(
        wg.reshape(8, 8, 128, 16, 128).transpose(0, 3, 2, 1, 4)).reshape(8, 16, 128, 1024)
    w["wp_s"] = np.ascontiguousarray(
        wp.reshape(8, 8, 128, 16, 128).transpose(0, 3, 2, 1, 4)).reshape(8, 16, 128, 1024)
    w["wo_s"] = np.ascontiguousarray(
        wo.reshape(8, 2, 8, 128, 1024).transpose(0, 1, 3, 2, 4)).reshape(8, 2, 128, 8192)
    return w


_BUILT = {}


def kernel(**inputs):
    x = np.asarray(inputs["L0_attn_x"], dtype=np.float32)
    w = _prep_weights(inputs)
    cst = _consts()

    if "nc" not in _BUILT:
        _BUILT["nc"] = build_program()
    nc = _BUILT["nc"]

    shared = dict(cst)
    for k in ("wq", "wk", "wv", "wo", "bq", "bk", "bv", "bo_rep", "cw", "cb",
              "cdiag", "router_wt", "router_b", "temp", "wg_s", "wp_s", "wo_s"):
        shared[k] = w[k]

    rot_cache = {}
    in_maps = []
    for core in range(NCORES):
        b, s = core // 4, core % 4
        t0 = s * T
        m = dict(shared)
        xe = np.zeros((EXT, D), dtype=np.float32)
        lo = t0 - HALO
        src_lo = max(lo, 0)
        xe[src_lo - lo:, :] = x[b, src_lo: t0 + T, :]
        m["x_ext"] = xe
        if t0 not in rot_cache:
            rot_cache[t0] = _rotary_tables(t0)
        m["cost"], m["sint"] = rot_cache[t0]
        km = np.zeros((128, 20), dtype=np.float32)
        if s == 0:
            kj = np.arange(2560).reshape(20, 128).T
            km = np.where(kj < 511, NEG, 0.0).astype(np.float32)
        km[127, 19] = NEG
        m["kmask"] = np.ascontiguousarray(km)
        sel = np.zeros((64, 32), dtype=np.float32)
        for cl in range(4):
            for e in range(8):
                sel[(4 * b + cl) * 8 + e, cl * 8 + e] = 1.0
        m["sel64"] = sel
        m32_ = np.zeros((32, 8), dtype=np.float32)
        g32_ = np.zeros((32, 1), dtype=np.float32)
        for e in range(8):
            m32_[s * 8 + e, e] = 1.0
            g32_[s * 8 + e, 0] = 1.0
        m["m32"] = m32_
        m["g32"] = g32_
        in_maps.append(m)

    res = run_bass_kernel_spmd(nc, in_maps, core_ids=list(range(NCORES)))
    _BUILT["res"] = res
    outp = np.empty((B, S, D), dtype=np.float32)
    for core in range(NCORES):
        b, s = core // 4, core % 4
        outp[b, s * T:(s + 1) * T, :] = res.results[core]["out"]
    return outp
